# version 23
# baseline (speedup 1.0000x reference)
import numpy as np
import ml_dtypes
from contextlib import ExitStack
import concourse.bass as bass
import concourse.mybir as mybir
from concourse.bass_utils import run_bass_kernel_spmd
from concourse.bass_core_rust import IndirectOffsetOnAxis

F32 = mybir.dt.float32
BF16 = mybir.dt.bfloat16
I32 = mybir.dt.int32
AF = mybir.ActivationFunctionType
OP = mybir.AluOpType
AXL = mybir.AxisListType

T = 2048
D = 1024
NT = 16
CC = 8
PAIRS = 8
NITER = 1024
EXP_SHIFT = -20.0
EPS = 1e-5
NCORES = 8
FF = 4096
NE = 8
CAP = 2048
VP = 129 * 8
RG = [list(range(NCORES))]


class SelfChain:

    def __init__(self, eng, sem, n0=0):
        self.e = eng
        self.sem = sem
        self.n = n0

    def op(self, name, *a, **k):
        if self.n:
            self.e.wait_ge(self.sem, self.n)
        inst = getattr(self.e, name)(*a, **k)
        inst.then_inc(self.sem, 1)
        self.n += 1
        return inst


def build_nc(upto="full", detect_races=True, serial=False):
    nc = bass.Bass("TRN2", target_bir_lowering=False,
                   detect_race_conditions=detect_races)

    x_in = nc.dram_tensor("x_in", [T, D], F32, kind="ExternalInput")
    wq_in = nc.dram_tensor("wq", [D, D], F32, kind="ExternalInput")
    wk_in = nc.dram_tensor("wk", [D, D], F32, kind="ExternalInput")
    wv_in = nc.dram_tensor("wv", [D, D], F32, kind="ExternalInput")
    wkp_in = nc.dram_tensor("wkp", [D, D], F32, kind="ExternalInput")
    wo_in = nc.dram_tensor("wo", [D, D], F32, kind="ExternalInput")
    alpha_in = nc.dram_tensor("alpha", [1, 1], F32, kind="ExternalInput")
    win_in = nc.dram_tensor("win", [3 * D, D], F32, kind="ExternalInput")
    binq_in = nc.dram_tensor("binq", [24, 128], F32, kind="ExternalInput")
    wout_in = nc.dram_tensor("wout", [D, D], F32, kind="ExternalInput")
    bout_in = nc.dram_tensor("bout", [1, D], F32, kind="ExternalInput")
    ln1g_in = nc.dram_tensor("ln1g", [1, D], F32, kind="ExternalInput")
    ln1b_in = nc.dram_tensor("ln1b", [1, D], F32, kind="ExternalInput")
    ln2g_in = nc.dram_tensor("ln2g", [1, D], F32, kind="ExternalInput")
    ln2b_in = nc.dram_tensor("ln2b", [1, D], F32, kind="ExternalInput")
    wg_in = nc.dram_tensor("wg", [D, NE], F32, kind="ExternalInput")
    w1_in = nc.dram_tensor("w1", [D, FF], F32, kind="ExternalInput")
    b1_in = nc.dram_tensor("b1", [32, 128], F32, kind="ExternalInput")
    w2_in = nc.dram_tensor("w2", [FF, D], F32, kind="ExternalInput")
    b2_in = nc.dram_tensor("b2", [1, D], F32, kind="ExternalInput")
    prevmask_in = nc.dram_tensor("prevmask", [8, 1], F32, kind="ExternalInput")
    sidxf_in = nc.dram_tensor("sidxf", [128, NT], F32, kind="ExternalInput")
    myslot_in = nc.dram_tensor("myslot", [128, 1], I32, kind="ExternalInput")

    out = nc.dram_tensor("out", [T, D], F32, kind="ExternalOutput")

    sp_x2 = nc.dram_tensor("sp_x2", [T, D], F32, kind="Internal")
    sp_x2n = nc.dram_tensor("sp_x2n", [T, D], F32, kind="Internal")
    sp_x2nh = nc.dram_tensor("sp_x2nh", [128, NT * D], BF16, kind="Internal")
    sp_y1 = nc.dram_tensor("sp_y1", [T, D], F32, kind="Internal")
    sp_y1h = nc.dram_tensor("sp_y1h", [128, NT * D], BF16, kind="Internal")
    cc_kin = nc.dram_tensor("cc_kin", [D, T], BF16, kind="Internal")
    cc_kout = nc.dram_tensor("cc_kout", [8 * D, T], BF16, kind="Internal")
    cc_vin = nc.dram_tensor("cc_vin", [T, D], BF16, kind="Internal")
    cc_vout = nc.dram_tensor("cc_vout", [8 * T, D], BF16, kind="Internal")
    cc_y1in = nc.dram_tensor("cc_y1in", [T, D], BF16, kind="Internal")
    cc_y1out = nc.dram_tensor("cc_y1out", [8 * T, D], BF16, kind="Internal")
    cc_cntin = nc.dram_tensor("cc_cntin", [1, NE], F32, kind="Internal")
    cc_cntout = nc.dram_tensor("cc_cntout", [8, NE], F32, kind="Internal")
    cc_invin = nc.dram_tensor("cc_invin", [16512, 1], F32, kind="Internal")
    cc_invout = nc.dram_tensor("cc_invout", [16512, 1], F32, kind="Internal")
    cc_iall = nc.dram_tensor("cc_iall", [8 * 16512, 1], F32, kind="Internal")
    cc_invw_t = [nc.dram_tensor(f"cc_invwt{tt}", [16512, 1], F32, kind="Internal")
                 for tt in range(NT)]
    w1bf = nc.dram_tensor("w1bf", [D, FF], BF16, kind="Internal")
    w2bf = nc.dram_tensor("w2bf", [FF, D], BF16, kind="Internal")
    cc_yein = nc.dram_tensor("cc_yein", [CAP, D], BF16, kind="Internal")
    cc_yeout = nc.dram_tensor("cc_yeout", [8 * CAP, D], BF16, kind="Internal")

    ident_bf = nc.inline_tensor(np.eye(128, dtype=ml_dtypes.bfloat16), "ident_bf")
    ones_bf = nc.inline_tensor(np.ones((128, 64), dtype=ml_dtypes.bfloat16), "ones_bf")
    stair_np = (np.arange(4096)[None, :] >= (2048 + np.arange(128))[:, None])
    stair = nc.inline_tensor(stair_np.astype(ml_dtypes.bfloat16), "stair")
    erow = nc.inline_tensor(np.arange(8, dtype=np.float32).reshape(1, 8), "erow")
    ident_f32 = nc.inline_tensor(np.eye(128, dtype=np.float32), "ident_f32")
    eps_col = nc.inline_tensor(np.full((128, 1), EPS, dtype=np.float32), "eps_col")
    selc_np = np.zeros((128, 16), dtype=ml_dtypes.bfloat16)
    for v_ in range(4):
        selc_np[:, v_ * 4 + v_] = 1.0
    selc = nc.inline_tensor(selc_np, "selc")
    basesc = nc.dram_tensor("basesc", [1, 8], F32, kind="Internal")

    identb = nc.alloc_sbuf_tensor("identb", [128, 128], BF16)
    identf = nc.alloc_sbuf_tensor("identf", [128, 128], F32)
    onescol = nc.alloc_sbuf_tensor("onescol", [128, 64], BF16)
    alpha_bc = nc.alloc_sbuf_tensor("alpha_bc", [128, 1], F32)
    oma_bc = nc.alloc_sbuf_tensor("oma_bc", [128, 1], F32)
    shiftc = nc.alloc_sbuf_tensor("shiftc", [128, 1], F32)
    epsc = nc.alloc_sbuf_tensor("epsc", [128, 1], F32)

    _es_x = ExitStack()
    xhat_ocomb = _es_x.enter_context(nc.sbuf_tensor("xhat_ocomb", [128, PAIRS * T], BF16))
    _es_a = ExitStack()
    q_feat = _es_a.enter_context(nc.sbuf_tensor("q_feat", [128, PAIRS * T], BF16))
    k_feat = _es_a.enter_context(nc.sbuf_tensor("k_feat", [128, PAIRS * T], BF16))
    kp_feat = _es_a.enter_context(nc.sbuf_tensor("kp_feat", [128, PAIRS * T], BF16))
    vpack = _es_a.enter_context(nc.sbuf_tensor("vpack", [128, NT * VP], BF16))

    xhat = xhat_ocomb

    with ExitStack() as _es:
        xload = _es.enter_context(nc.sbuf_tensor("xload", [128, 2 * D], F32))
        sqt = _es.enter_context(nc.sbuf_tensor("sqt", [128, D], F32))
        xhat_t = _es.enter_context(nc.sbuf_tensor("xhat_t", [128, D], BF16))
        st0 = _es.enter_context(nc.sbuf_tensor("st0", [128, 8], F32))
        trps = _es.enter_context(nc.psum_tensor("trps", [128, 2048], BF16))
        s_c0 = nc.alloc_semaphore("ph0_c")
        sxt0 = nc.alloc_semaphore("ph0_xt0")
        sxt1 = nc.alloc_semaphore("ph0_xt1")
        s_d0 = nc.alloc_semaphore("ph0_d")
        s_a0 = nc.alloc_semaphore("ph0_a")
        s_t0 = nc.alloc_semaphore("ph0_t")
        blk = _es.enter_context(nc.Block())

        def a0_sqrt(i):
            return 9 * i + 1

        def a0_copy(k):
            return 9 * (k // 8) + 2 + (k % 8)

        @blk.tensor
        def _(t):
            for i in range(NT):
                t.wait_ge(s_d0, 8 * i + 10)
                for cc in range(CC):
                    k = i * CC + cc
                    if k >= 2:
                        t.wait_ge(s_a0, a0_copy(k - 2))
                    t.transpose(trps[:, (k % 2) * 1024:(k % 2) * 1024 + 128],
                                xhat_t[:, cc * 128:(cc + 1) * 128],
                                identb[:, :]).then_inc(s_t0, 1)

        @blk.scalar
        def _(sc):
            for i in range(NT):
                sc.wait_ge(s_d0, 8 * i + 8)
                sc.activation(st0[:, 4:5], st0[:, 3:4], AF.Sqrt).then_inc(s_a0, 1)
                for cc in range(CC):
                    k = i * CC + cc
                    sc.wait_ge(s_t0, k + 1)
                    sc.copy(xhat[:, cc * T + i * 128: cc * T + (i + 1) * 128],
                            trps[:, (k % 2) * 1024:(k % 2) * 1024 + 128]).then_inc(s_a0, 1)

        @blk.vector
        def _(v):
            sxt = [sxt0, sxt1]
            ec = SelfChain(v, s_d0)
            v.wait_ge(s_c0, 80)
            ec.op("memset", shiftc[:, :], EXP_SHIFT)
            ec.op("tensor_scalar", oma_bc[:, :], alpha_bc[:, :], -1.0, 1.0, OP.mult, OP.add)
            for i in range(NT):
                xt = xload[:, (i % 2) * D:(i % 2 + 1) * D]
                v.wait_ge(sxt[i % 2], 16 * (i // 2 + 1))
                ec.op("tensor_reduce", st0[:, 0:1], xt, AXL.X, OP.add)
                ec.op("tensor_scalar", st0[:, 1:2], st0[:, 0:1], 1.0 / D, None, OP.mult)
                ec.op("tensor_scalar", xt, xt, st0[:, 1:2], None, OP.subtract)
                ec.op("tensor_tensor", sqt[:, :], xt, xt, OP.mult)
                ec.op("tensor_reduce", st0[:, 2:3], sqt[:, :], AXL.X, OP.add)
                ec.op("tensor_scalar", st0[:, 3:4], st0[:, 2:3], 1.0 / D, EPS,
                      OP.mult, OP.add)
                v.wait_ge(s_a0, a0_sqrt(i))
                ec.op("reciprocal", st0[:, 5:6], st0[:, 4:5])
                if i >= 1:
                    v.wait_ge(s_t0, 8 * i)
                ec.op("tensor_scalar", xhat_t[:, :], xt, st0[:, 5:6], None,
                      OP.mult)
            assert ec.n == 2 + 8 * NT

        @blk.sync
        def _(s):
            s.dma_start(out=identb[:, :], in_=ident_bf[:, :]).then_inc(s_c0, 16)
            s.dma_start(out=onescol[:, :], in_=ones_bf[:, :]).then_inc(s_c0, 16)
            s.dma_start(out=identf[:, :], in_=ident_f32[:, :]).then_inc(s_c0, 16)
            s.dma_start(out=alpha_bc[:, :],
                        in_=alpha_in.ap().broadcast_to([128, 1])).then_inc(s_c0, 16)
            s.dma_start(out=epsc[:, :], in_=eps_col[:, :]).then_inc(s_c0, 16)
            sxt = [sxt0, sxt1]
            for i in range(NT):
                if i >= 2:
                    s.wait_ge(s_d0, 8 * (i - 2) + 10)
                s.dma_start(out=xload[:, (i % 2) * D:(i % 2 + 1) * D],
                            in_=x_in[i * 128:(i + 1) * 128, :]).then_inc(sxt[i % 2], 16)

    def dump_sbuf_bf16(src_t, ncols):
        ov = out.ap().rearrange("(a b) d -> a (b d)", a=128)
        with ExitStack() as es2:
            sl = nc.alloc_semaphore("dump_l")
            blk2 = es2.enter_context(nc.Block())

            @blk2.gpsimd
            def _(g):
                for j in range(4):
                    w = ncols // 4
                    g.dma_start(out=ov[:, j * w:(j + 1) * w],
                                in_=src_t[:, j * w:(j + 1) * w]).then_inc(sl, 16)
                g.wait_ge(sl, 64)

    if upto == "ph0":
        dump_sbuf_bf16(xhat, CC * T)
        return nc

    with ExitStack() as _es:
        wrow = _es.enter_context(nc.sbuf_tensor("wrow", [128, 2 * D], BF16))
        wts = _es.enter_context(nc.sbuf_tensor("wts", [128, 2 * D], BF16))
        wvt = _es.enter_context(nc.sbuf_tensor("wvt", [128, CC * D], BF16))
        trp1 = _es.enter_context(nc.psum_tensor("trp1", [128, 4096], BF16))
        pp1 = _es.enter_context(nc.psum_tensor("pp1", [128, 512], F32))
        pp2 = _es.enter_context(nc.psum_tensor("pp2", [128, 512], F32))
        s_wcast = nc.alloc_semaphore("ph1_wcast")
        s_wl0 = nc.alloc_semaphore("ph1_wl0")
        s_wl1 = nc.alloc_semaphore("ph1_wl1")
        s_t1 = nc.alloc_semaphore("ph1_t")
        s_a1 = nc.alloc_semaphore("ph1_a")
        s_d1 = nc.alloc_semaphore("ph1_d")
        blk = _es.enter_context(nc.Block())

        loads = [("v", rc) for rc in range(CC)] + \
                [(w, pair) for pair in range(PAIRS) for w in ("q", "k", "kp")]
        wd_map = {"v": wv_in, "q": wq_in, "k": wk_in, "kp": wkp_in}
        dest_map = {"q": q_feat, "k": k_feat, "kp": kp_feat}

        def t1_tr(li, cc):
            if li < 8:
                return li * 8 + cc + 1
            return 64 + (li - 8) * 12 + cc + 1

        def t1_proj(npm):
            if npm < 96:
                li = 8 + npm // 4
                win = npm % 4
                return 64 + (li - 8) * 12 + 8 + win + 1
            return 64 + 288 + (npm - 96) + 1

        @blk.tensor
        def _(t):
            ntr = 0
            npm = 0
            for li, (w, rc) in enumerate(loads):
                t.wait_ge([s_wl0, s_wl1][li % 2], 16 * (li // 2 + 1))
                for cc in range(CC):
                    if ntr >= 4:
                        t.wait_ge(s_a1, ntr - 3)
                    t.transpose(trp1[:, (ntr % 4) * 1024:(ntr % 4) * 1024 + 128],
                                wrow[:, (li % 2) * D + cc * 128:(li % 2) * D + (cc + 1) * 128],
                                identb[:, :]).then_inc(s_t1, 1)
                    ntr += 1
                if w == "v":
                    continue
                t.wait_ge(s_a1, ntr)
                for win in range(4):
                    pb = [pp1, pp2][npm % 2]
                    if npm >= 2:
                        t.wait_ge(s_d1, npm - 1)
                    for cc in range(CC):
                        mm = t.matmul(pb[:, :],
                                      wts[:, (li % 2) * D + cc * 128:(li % 2) * D + (cc + 1) * 128],
                                      xhat[:, cc * T + win * 512:cc * T + (win + 1) * 512],
                                      start=(cc == 0), stop=(cc == CC - 1))
                    mm.then_inc(s_t1, 1)
                    npm += 1
            t.wait_ge(s_a1, 64)
            for tt in range(NT):
                for win in range(2):
                    pb = [pp1, pp2][npm % 2]
                    if npm >= 2:
                        t.wait_ge(s_d1, npm - 1)
                    for cc in range(CC):
                        mm = t.matmul(pb[:, :],
                                      xhat[:, cc * T + tt * 128:cc * T + (tt + 1) * 128],
                                      wvt[:, cc * D + win * 512:cc * D + (win + 1) * 512],
                                      start=(cc == 0), stop=(cc == CC - 1))
                    mm.then_inc(s_t1, 1)
                    npm += 1

        @blk.scalar
        def _(sc):
            ntr = 0
            for li, (w, rc) in enumerate(loads):
                for cc in range(CC):
                    sc.wait_ge(s_t1, t1_tr(li, cc))
                    if w == "v":
                        dst = wvt[:, cc * D + rc * 128: cc * D + rc * 128 + 128]
                    else:
                        dst = wts[:, (li % 2) * D + cc * 128:(li % 2) * D + (cc + 1) * 128]
                    sc.copy(dst, trp1[:, (ntr % 4) * 1024:(ntr % 4) * 1024 + 128]).then_inc(s_a1, 1)
                    ntr += 1

        @blk.vector
        def _(v):
            vp4 = vpack.ap().rearrange("p (t pr c) -> p t pr c", t=NT, pr=8, c=129)
            v.memset(vp4[:, :, :, 64:65], 1.0)
            npm = 0
            for li, (w, rc) in enumerate(loads):
                if w == "v":
                    continue
                pair = rc
                dest = dest_map[w]
                for win in range(4):
                    pb = [pp1, pp2][npm % 2]
                    v.wait_ge(s_t1, t1_proj(npm))
                    if w == "q":
                        v.tensor_scalar(dest[:, pair * T + win * 512: pair * T + (win + 1) * 512],
                                        pb[:, :], 0.125, None, OP.mult).then_inc(s_d1, 1)
                    else:
                        v.tensor_copy(dest[:, pair * T + win * 512: pair * T + (win + 1) * 512],
                                      pb[:, :]).then_inc(s_d1, 1)
                    npm += 1
            for tt in range(NT):
                for win in range(2):
                    pb = [pp1, pp2][npm % 2]
                    v.wait_ge(s_t1, t1_proj(npm))
                    for pw in range(4):
                        pair = win * 4 + pw
                        base = tt * VP + pair * 129
                        v.tensor_copy(vpack[:, base: base + 64], pb[:, pw * 128: pw * 128 + 64])
                        inst = v.tensor_copy(vpack[:, base + 65: base + 129],
                                             pb[:, pw * 128 + 64: pw * 128 + 128])
                    inst.then_inc(s_d1, 1)
                    npm += 1

        @blk.gpsimd
        def _(g):
            for li, (w, rc) in enumerate(loads):
                if li >= 2:
                    g.wait_ge(s_t1, t1_tr(li - 2, 7))
                g.dma_start(out=wrow[:, (li % 2) * D:(li % 2 + 1) * D],
                            in_=wd_map[w][rc * 128:(rc + 1) * 128, :]).then_inc(
                                [s_wl0, s_wl1][li % 2], 16)
            g.dma_start(out=w1bf[:, :], in_=w1_in[:, :]).then_inc(s_wcast, 16)
            g.dma_start(out=w2bf[:, :], in_=w2_in[:, :]).then_inc(s_wcast, 16)

    if upto == "ph1":
        dump_sbuf_bf16(q_feat, PAIRS * T)
        return nc

    o_comb = xhat_ocomb

    with ExitStack() as _es:
        pbuf = _es.enter_context(nc.sbuf_tensor("pbuf", [128, 3 * 1024], BF16))
        osb = _es.enter_context(nc.sbuf_tensor("osb", [128, 1024], F32))
        recrow = _es.enter_context(nc.sbuf_tensor("recrow", [128, 1024], F32))
        recrow_bf = _es.enter_context(nc.sbuf_tensor("recrow_bf", [128, 1024], BF16))
        sstd = _es.enter_context(nc.sbuf_tensor("sstd", [128, 4 * 512], F32))
        S0 = _es.enter_context(nc.psum_tensor("S0", [128, 1024], F32))
        S1 = _es.enter_context(nc.psum_tensor("S1", [128, 1024], F32))
        oeA = _es.enter_context(nc.psum_tensor("oeA", [65, 512], F32))
        ooA = _es.enter_context(nc.psum_tensor("ooA", [128, 512], F32))
        oeB = _es.enter_context(nc.psum_tensor("oeB", [65, 512], F32))
        ooB = _es.enter_context(nc.psum_tensor("ooB", [128, 512], F32))
        s_t2 = nc.alloc_semaphore("p2_t")
        s_a2 = nc.alloc_semaphore("p2_a")
        s_d2 = nc.alloc_semaphore("p2_d")
        blk = _es.enter_context(nc.Block())

        Sbufs = [S0, S1]
        oe = [oeA, oeB]
        oo = [ooA, ooB]

        def iter_idx(g):
            return g >> 7, (g >> 6) & 1, (g >> 4) & 3, g & 15

        pe_ev = {"s": {}, "o": {}, "rep": {}}

        def d2_start(gr):
            n = 0
            for g2 in range(gr):
                n += 8 if ((g2 >> 2) & 1) == 0 else 12
            return n

        def d2_od(gr, head):
            return d2_start(gr) + 3 * (head + 1)

        def d2_on(gr, head):
            br = (gr >> 2) & 1
            if br == 0:
                return d2_start(gr) + 7 + head
            return d2_start(gr) + 9 + 3 * head

        def d2_end(gr):
            return d2_start(gr + 1)

        @blk.tensor
        def _(t):
            nev = 0

            def emit_oside(go):
                nonlocal nev
                pair2, br2, qt2, kt2 = iter_idx(go)
                gr2 = go >> 4
                st = gr2 % 2
                t.wait_ge(s_a2, go + 1)
                if kt2 == 0 and gr2 >= 2:
                    t.wait_ge(s_d2, d2_end(gr2 - 2))
                vb = vpack[:, kt2 * VP + pair2 * 129: kt2 * VP + pair2 * 129 + 65]
                t.matmul(oe[st][:, :], vb, pbuf[:, (go % 3) * 1024:(go % 3) * 1024 + 512],
                         start=(kt2 == 0), stop=(kt2 == 15))
                vo = vpack[:, kt2 * VP + pair2 * 129 + 65: kt2 * VP + pair2 * 129 + 129]
                t.matmul(oo[st][64:128, :], vo,
                         pbuf[:, (go % 3) * 1024 + 512:(go % 3) * 1024 + 1024],
                         start=(kt2 == 0), stop=(kt2 == 15), tile_position=(0, 64))
                nev += 1
                t.matmul(oo[st][0:1, :], onescol[:, 0:1],
                         pbuf[:, (go % 3) * 1024 + 512:(go % 3) * 1024 + 1024],
                         start=(kt2 == 0), stop=(kt2 == 15)).then_inc(s_t2, 1)
                pe_ev["o"][go] = nev
                if kt2 == 2 and gr2 >= 1:
                    gp = gr2 - 1
                    t.wait_ge(s_d2, d2_od(gp, 0))
                    nev += 1
                    t.matmul(oe[gp % 2][0:64, :], onescol[64:65, :],
                             recrow_bf[64:65, 0:512], start=True, stop=True,
                             tile_position=(64, 0)).then_inc(s_t2, 1)
                    pe_ev["rep"][2 * gp] = nev
                if kt2 == 4 and gr2 >= 1:
                    gp = gr2 - 1
                    t.wait_ge(s_d2, d2_od(gp, 1))
                    nev += 1
                    t.matmul(oo[gp % 2][64:128, :], onescol[0:1, :],
                             recrow_bf[0:1, 512:1024], start=True, stop=True,
                             tile_position=(0, 64)).then_inc(s_t2, 1)
                    pe_ev["rep"][2 * gp + 1] = nev

            for g in range(NITER):
                pair, br, qt, kt = iter_idx(g)
                kc = k_feat if br == 0 else kp_feat
                if g >= 2:
                    t.wait_ge(s_a2, g - 1)
                sb = Sbufs[g % 2]
                t.matmul(sb[:, 0:512],
                         kc[0:64, pair * T + kt * 128: pair * T + (kt + 1) * 128],
                         q_feat[0:64, pair * T + qt * 512: pair * T + (qt + 1) * 512],
                         start=True, stop=True, tile_position=(0, 0))
                nev += 1
                t.matmul(sb[:, 512:1024],
                         kc[64:128, pair * T + kt * 128: pair * T + (kt + 1) * 128],
                         q_feat[64:128, pair * T + qt * 512: pair * T + (qt + 1) * 512],
                         start=True, stop=True, tile_position=(64, 0)).then_inc(s_t2, 1)
                pe_ev["s"][g] = nev
                if g >= 1:
                    emit_oside(g - 1)
            emit_oside(NITER - 1)
            for gp in (63,):
                t.wait_ge(s_d2, d2_od(gp, 0))
                nev += 1
                t.matmul(oe[gp % 2][0:64, :], onescol[64:65, :], recrow_bf[64:65, 0:512],
                         start=True, stop=True, tile_position=(64, 0)).then_inc(s_t2, 1)
                pe_ev["rep"][2 * gp] = nev
                t.wait_ge(s_d2, d2_od(gp, 1))
                nev += 1
                t.matmul(oo[gp % 2][64:128, :], onescol[0:1, :], recrow_bf[0:1, 512:1024],
                         start=True, stop=True, tile_position=(0, 64)).then_inc(s_t2, 1)
                pe_ev["rep"][2 * gp + 1] = nev

        @blk.scalar
        def _(sc):
            for g in range(NITER):
                sc.wait_ge(s_t2, pe_ev["s"][g])
                if g >= 3:
                    sc.wait_ge(s_t2, pe_ev["o"][g - 3])
                sc.activation(pbuf[:, (g % 3) * 1024:(g % 3) * 1024 + 1024],
                              Sbufs[g % 2][:, :], AF.Exp,
                              bias=shiftc[:, :]).then_inc(s_a2, 1)

        @blk.vector
        def _(v):
            ec = SelfChain(v, s_d2)
            for gr in range(64):
                pair, br, qt = gr >> 3, (gr >> 2) & 1, gr & 3
                st = gr % 2
                v.wait_ge(s_t2, pe_ev["o"][16 * gr + 15])
                ec.op("reciprocal", recrow[64:65, 0:512], oe[st][64:65, :])
                ec.op("tensor_copy", recrow_bf[64:65, 0:512], recrow[64:65, 0:512])
                ec.op("tensor_copy", osb[0:64, 0:512], oe[st][0:64, :])
                assert ec.n == d2_od(gr, 0)
                ec.op("reciprocal", recrow[0:1, 512:1024], oo[st][0:1, :])
                ec.op("tensor_copy", recrow_bf[0:1, 512:1024], recrow[0:1, 512:1024])
                ec.op("tensor_copy", osb[64:128, 512:1024], oo[st][64:128, :])
                assert ec.n == d2_od(gr, 1)
                v.wait_ge(s_t2, pe_ev["rep"][2 * gr])
                if br == 0:
                    ec.op("tensor_tensor", sstd[0:64, qt * 512:(qt + 1) * 512],
                          osb[0:64, 0:512], oe[st][0:64, :], OP.mult)
                else:
                    ec.op("tensor_tensor", osb[0:64, 0:512], osb[0:64, 0:512], oe[st][0:64, :], OP.mult)
                    ec.op("tensor_scalar", osb[0:64, 0:512], osb[0:64, 0:512], oma_bc[0:64, :], None, OP.mult)
                    ec.op("scalar_tensor_tensor",
                          o_comb[0:64, pair * T + qt * 512: pair * T + (qt + 1) * 512],
                          sstd[0:64, qt * 512:(qt + 1) * 512], alpha_bc[0:64, :],
                          osb[0:64, 0:512], OP.mult, OP.add)
                assert ec.n == d2_on(gr, 0)
                v.wait_ge(s_t2, pe_ev["rep"][2 * gr + 1])
                if br == 0:
                    ec.op("tensor_tensor", sstd[64:128, qt * 512:(qt + 1) * 512],
                          osb[64:128, 512:1024], oo[st][64:128, :], OP.mult)
                else:
                    ec.op("tensor_tensor", osb[64:128, 512:1024], osb[64:128, 512:1024],
                          oo[st][64:128, :], OP.mult)
                    ec.op("tensor_scalar", osb[64:128, 512:1024], osb[64:128, 512:1024],
                          oma_bc[64:128, :], None, OP.mult)
                    ec.op("scalar_tensor_tensor",
                          o_comb[64:128, pair * T + qt * 512: pair * T + (qt + 1) * 512],
                          sstd[64:128, qt * 512:(qt + 1) * 512], alpha_bc[64:128, :],
                          osb[64:128, 512:1024], OP.mult, OP.add)
                assert ec.n == d2_on(gr, 1) == d2_end(gr)

    if upto == "ph2":
        dump_sbuf_bf16(o_comb, PAIRS * T)
        _es_a.close()
        return nc

    _es_a.close()

    with ExitStack() as _es:
        wrow3 = _es.enter_context(nc.sbuf_tensor("wrow3", [128, 2 * D], BF16))
        wot = _es.enter_context(nc.sbuf_tensor("wot", [128, CC * D], BF16))
        xld3 = _es.enter_context(nc.sbuf_tensor("xld3", [128, 2 * D], F32))
        att3 = _es.enter_context(nc.sbuf_tensor("att3", [128, 2 * D], F32))
        scr3 = _es.enter_context(nc.sbuf_tensor("scr3", [128, D], F32))
        x1t = _es.enter_context(nc.sbuf_tensor("x1t", [128, 2 * D], F32))
        x2nt = _es.enter_context(nc.sbuf_tensor("x2nt", [128, 2 * D], F32))
        x2nb = _es.enter_context(nc.sbuf_tensor("x2nb", [128, D], BF16))
        x2nhs = _es.enter_context(nc.sbuf_tensor("x2nhs", [128, 2 * D], BF16))
        st3 = _es.enter_context(nc.sbuf_tensor("st3", [128, 8 * NT], F32))
        trp3 = _es.enter_context(nc.psum_tensor("trp3", [128, 4096], BF16))
        wop0 = _es.enter_context(nc.psum_tensor("wop0", [128, 512], F32))
        wop1 = _es.enter_context(nc.psum_tensor("wop1", [128, 512], F32))
        wop2 = _es.enter_context(nc.psum_tensor("wop2", [128, 512], F32))
        wop3 = _es.enter_context(nc.psum_tensor("wop3", [128, 512], F32))
        s_wl3a = nc.alloc_semaphore("p3_wla")
        s_wl3b = nc.alloc_semaphore("p3_wlb")
        s_sp3a = nc.alloc_semaphore("p3_spa")
        s_sp3b = nc.alloc_semaphore("p3_spb")
        s_xl3a = nc.alloc_semaphore("p3_xla")
        s_xl3b = nc.alloc_semaphore("p3_xlb")
        s_st3a = nc.alloc_semaphore("p3_sta")
        s_st3b = nc.alloc_semaphore("p3_stb")
        s_t3 = nc.alloc_semaphore("p3_t")
        s_a3 = nc.alloc_semaphore("p3_a")
        s_d3 = nc.alloc_semaphore("p3_d")
        blk = _es.enter_context(nc.Block())
        RSD = 1.0 / 32.0

        wops = [wop0, wop1, wop2, wop3]

        act_order = []
        for tt in range(NT):
            if tt >= 2:
                for cc in range(CC):
                    act_order.append(("ev", tt - 2, cc))
            for nm in ("acc1", "sqacc1", "sqrt1", "acc2", "sqacc2", "sqrt2"):
                act_order.append((nm, tt))
        for j in (NT - 2, NT - 1):
            for cc in range(CC):
                act_order.append(("ev", j, cc))
        a3ev = {k: 64 + i + 1 for i, k in enumerate(act_order)}

        def t3_proj(tt, win):
            n = 64
            for t2 in range(NT):
                n += 1
                if t2 == tt and win == 0:
                    return n
                n += 1
                if t2 == tt and win == 1:
                    return n
                if t2 >= 2:
                    n += 8
            raise AssertionError

        def t3_trx(j, cc):
            n = 64
            for t2 in range(NT):
                n += 2
                if t2 >= 2:
                    if t2 - 2 == j:
                        return n + cc + 1
                    n += 8
            for jj in (NT - 2, NT - 1):
                if jj == j:
                    return n + cc + 1
                n += 8
            raise AssertionError

        @blk.tensor
        def _(t):
            ntr = 0
            for rc in range(CC):
                t.wait_ge([s_wl3a, s_wl3b][rc % 2], 16 * (rc // 2 + 1))
                for cc in range(CC):
                    if ntr >= 4:
                        t.wait_ge(s_a3, ntr - 3)
                    t.transpose(trp3[:, (ntr % 4) * 1024:(ntr % 4) * 1024 + 128],
                                wrow3[:, (rc % 2) * D + cc * 128:(rc % 2) * D + (cc + 1) * 128],
                                identb[:, :]).then_inc(s_t3, 1)
                    ntr += 1

            def emit_tb(j):
                t.wait_ge(s_d3, 9 * j + 9)
                for cc in range(CC):
                    k = j * CC + cc
                    if k < 4:
                        t.wait_ge(s_a3, 64)
                    else:
                        kp = k - 4
                        t.wait_ge(s_a3, a3ev[("ev", kp // 8, kp % 8)])
                    t.transpose(trp3[:, (k % 4) * 1024:(k % 4) * 1024 + 128],
                                x2nb[:, cc * 128:(cc + 1) * 128],
                                identb[:, :]).then_inc(s_t3, 1)

            t.wait_ge(s_a3, 64)
            for tt in range(NT):
                for win in range(2):
                    pb = wops[(tt * 2 + win) % 4]
                    if tt >= 2:
                        t.wait_ge(s_d3, 9 * (tt - 2) + 2)
                    for cc in range(CC):
                        mm = t.matmul(pb[:, :],
                                      o_comb[:, cc * T + tt * 128:cc * T + (tt + 1) * 128],
                                      wot[:, cc * D + win * 512:cc * D + (win + 1) * 512],
                                      start=(cc == 0), stop=(cc == CC - 1))
                    mm.then_inc(s_t3, 1)
                if tt >= 2:
                    emit_tb(tt - 2)
            emit_tb(NT - 2)
            emit_tb(NT - 1)

        @blk.scalar
        def _(sc):
            na = 0

            def aop(wait_t=None, wait_d=None, wait_sem=None, copy=None, act=None):
                nonlocal na
                if wait_t is not None:
                    sc.wait_ge(s_t3, wait_t)
                if wait_d is not None:
                    sc.wait_ge(s_d3, wait_d)
                if wait_sem is not None:
                    sc.wait_ge(*wait_sem)
                if na:
                    sc.wait_ge(s_a3, na)
                inst = sc.copy(*copy) if copy else sc.activation(*act[0], **act[1])
                inst.then_inc(s_a3, 1)
                na += 1
            ntr = 0
            for rc in range(CC):
                for cc in range(CC):
                    aop(wait_t=ntr + 1,
                        copy=(wot[:, cc * D + rc * 128: cc * D + rc * 128 + 128],
                              trp3[:, (ntr % 4) * 1024:(ntr % 4) * 1024 + 128]))
                    ntr += 1
            for key in act_order:
                if key[0] == "ev":
                    _, j, cc = key
                    k = j * CC + cc
                    aop(wait_t=t3_trx(j, cc),
                        wait_sem=(([s_sp3a, s_sp3b][j % 2], 16 * (j // 2))
                                  if j >= 2 else None),
                        copy=(x2nhs[:, (j % 2) * D + cc * 128:(j % 2) * D + (cc + 1) * 128],
                              trp3[:, (k % 4) * 1024:(k % 4) * 1024 + 128]))
                    continue
                nm, tt = key
                sb = st3[:, 8 * tt:8 * tt + 8]
                at = att3[:, (tt % 2) * D:(tt % 2 + 1) * D]
                x1 = x1t[:, (tt % 2) * D:(tt % 2 + 1) * D]
                x2n = x2nt[:, (tt % 2) * D:(tt % 2 + 1) * D]
                if nm == "acc1":
                    aop(wait_d=9 * tt + 2,
                        act=((scr3[:, :], at, AF.Identity),
                             dict(bias=0.0, scale=1.0 / D, accum_out=sb[:, 0:1])))
                elif nm == "sqacc1":
                    aop(wait_d=9 * tt + 3,
                        act=((scr3[:, :], at, AF.Square),
                             dict(scale=RSD, accum_out=sb[:, 1:2])))
                elif nm == "sqrt1":
                    aop(act=((sb[:, 3:4], sb[:, 1:2], AF.Sqrt),
                             dict(bias=epsc[:, :])))
                elif nm == "acc2":
                    aop(wait_d=9 * tt + 5,
                        act=((scr3[:, :], x1, AF.Identity),
                             dict(bias=0.0, scale=1.0 / D, accum_out=sb[:, 5:6])))
                elif nm == "sqacc2":
                    aop(wait_d=9 * tt + 6,
                        act=((scr3[:, :], x2n, AF.Square),
                             dict(scale=RSD, accum_out=sb[:, 6:7])))
                else:
                    aop(act=((sb[:, 2:3], sb[:, 6:7], AF.Sqrt),
                             dict(bias=epsc[:, :])))
            assert na == 64 + len(act_order)

        @blk.vector
        def _(v):
            ec = SelfChain(v, s_d3)
            for tt in range(NT):
                sb = st3[:, 8 * tt:8 * tt + 8]
                at = att3[:, (tt % 2) * D:(tt % 2 + 1) * D]
                xt = xld3[:, (tt % 2) * D:(tt % 2 + 1) * D]
                x1 = x1t[:, (tt % 2) * D:(tt % 2 + 1) * D]
                x2n = x2nt[:, (tt % 2) * D:(tt % 2 + 1) * D]
                v.wait_ge(s_t3, t3_proj(tt, 0))
                if tt >= 2:
                    v.wait_ge([s_st3a, s_st3b][tt % 2], 32 * (tt // 2))
                    v.wait_ge(s_a3, a3ev[("sqacc1", tt - 2)])
                ec.op("tensor_copy", at[:, 0:512], wops[(2 * tt) % 4][:, :])
                v.wait_ge(s_t3, t3_proj(tt, 1))
                ec.op("tensor_copy", at[:, 512:1024], wops[(2 * tt + 1) % 4][:, :])
                v.wait_ge(s_a3, a3ev[("acc1", tt)])
                ec.op("tensor_scalar", at, at, sb[:, 0:1], None, OP.subtract)
                v.wait_ge(s_a3, a3ev[("sqrt1", tt)])
                ec.op("reciprocal", sb[:, 4:5], sb[:, 3:4])
                v.wait_ge([s_xl3a, s_xl3b][tt % 2], 16 * (tt // 2 + 1))
                ec.op("scalar_tensor_tensor", x1, at, sb[:, 4:5], xt, OP.mult, OP.add)
                v.wait_ge(s_a3, a3ev[("acc2", tt)])
                ec.op("tensor_scalar", x2n, x1, sb[:, 5:6], None, OP.subtract)
                v.wait_ge(s_a3, a3ev[("sqrt2", tt)])
                ec.op("reciprocal", sb[:, 7:8], sb[:, 2:3])
                ec.op("tensor_scalar", x2n, x2n, sb[:, 7:8], None, OP.mult)
                if tt >= 1:
                    v.wait_ge(s_t3, t3_trx(tt - 1, 7))
                ec.op("tensor_copy", x2nb[:, :], x2n)
                assert ec.n == 9 * tt + 9

        @blk.sync
        def _(s):
            for tt in range(NT):
                if tt >= 2:
                    s.wait_ge(s_d3, 9 * (tt - 2) + 5)
                s.dma_start(out=xld3[:, (tt % 2) * D:(tt % 2 + 1) * D],
                            in_=x_in[tt * 128:(tt + 1) * 128, :]).then_inc(
                                [s_xl3a, s_xl3b][tt % 2], 16)
                s.wait_ge(s_d3, 9 * tt + 5)
                s.dma_start(out=sp_x2[tt * 128:(tt + 1) * 128, :],
                            in_=x1t[:, (tt % 2) * D:(tt % 2 + 1) * D]).then_inc(
                                [s_st3a, s_st3b][tt % 2], 16)
                s.wait_ge(s_d3, 9 * tt + 9)
                s.dma_start(out=sp_x2n[tt * 128:(tt + 1) * 128, :],
                            in_=x2nt[:, (tt % 2) * D:(tt % 2 + 1) * D]).then_inc(
                                [s_st3a, s_st3b][tt % 2], 16)
            s.wait_ge(s_st3a, 32 * 8)
            s.wait_ge(s_st3b, 32 * 8)

        @blk.gpsimd
        def _(g):
            for rc in range(CC):
                if rc >= 2:
                    g.wait_ge(s_t3, 8 * (rc - 1))
                g.dma_start(out=wrow3[:, (rc % 2) * D:(rc % 2 + 1) * D],
                            in_=wo_in[rc * 128:(rc + 1) * 128, :]).then_inc(
                                [s_wl3a, s_wl3b][rc % 2], 16)
            for tt in range(NT):
                g.wait_ge(s_a3, a3ev[("ev", tt, 7)])
                g.dma_start(out=sp_x2nh[:, tt * D:(tt + 1) * D],
                            in_=x2nhs[:, (tt % 2) * D:(tt % 2 + 1) * D]).then_inc(
                                [s_sp3a, s_sp3b][tt % 2], 16)
            g.wait_ge(s_sp3a, 16 * 8)
            g.wait_ge(s_sp3b, 16 * 8)

    _es_x.close()

    if upto == "p3":
        s_l = nc.alloc_semaphore("fx_l")
        s_st = nc.alloc_semaphore("fx_s")
        with (
            nc.sbuf_tensor("ot", [128, D], F32) as ot,
            nc.Block() as blk,
        ):
            @blk.gpsimd
            def _(g):
                for tt in range(NT):
                    g.dma_start(out=ot[:, :], in_=sp_x2[tt * 128:(tt + 1) * 128, :]).then_inc(s_l, 16)
                    g.wait_ge(s_l, 16 * (tt + 1))
                    g.dma_start(out=out[tt * 128:(tt + 1) * 128, :], in_=ot[:, :]).then_inc(s_st, 16)
                    g.wait_ge(s_st, 16 * (tt + 1))
        return nc


    _es_r = ExitStack()
    dst_i32 = _es_r.enter_context(nc.sbuf_tensor("dst_i32", [128, NT], I32))
    gval = _es_r.enter_context(nc.sbuf_tensor("gval", [128, NT], F32))
    dstc_i32 = _es_r.enter_context(nc.sbuf_tensor("dstc_i32", [128, NT], I32))
    _es_g = ExitStack()
    G_sb = _es_g.enter_context(nc.sbuf_tensor("G_sb", [128, NT * 8], F32))
    _es_o = ExitStack()
    o_tok = _es_o.enter_context(nc.sbuf_tensor("o_tok", [128, NT * D], F32))
    _es_q = ExitStack()
    qf2 = _es_q.enter_context(nc.sbuf_tensor("qf2", [128, CC * T], BF16))
    s_ag4 = nc.alloc_semaphore("p4_ag")

    with ExitStack() as _es:
        wrow4 = _es.enter_context(nc.sbuf_tensor("wrow4", [128, 2 * D], BF16))
        wint = _es.enter_context(nc.sbuf_tensor("wint", [128, CC * 3072], BF16))
        x2h = _es.enter_context(nc.sbuf_tensor("x2h", [128, CC * T], BF16))
        kstg = _es.enter_context(nc.sbuf_tensor("kstg", [128, T], BF16))
        vstg = _es.enter_context(nc.sbuf_tensor("vstg", [128, D], BF16))
        binq_sb = _es.enter_context(nc.sbuf_tensor("binq_sb", [24, 128], F32))
        binqT = _es.enter_context(nc.sbuf_tensor("binqT", [128, 24], F32))
        bvbc = _es.enter_context(nc.sbuf_tensor("bvbc", [128, D], F32))
        trp4 = _es.enter_context(nc.psum_tensor("trp4", [128, 4096], BF16))
        pq1 = _es.enter_context(nc.psum_tensor("pq1", [128, 512], F32))
        pq2 = _es.enter_context(nc.psum_tensor("pq2", [128, 512], F32))
        s_wl4 = [nc.alloc_semaphore("p4_wla"), nc.alloc_semaphore("p4_wlb")]
        s_x4 = nc.alloc_semaphore("p4_x")
        s_t4 = nc.alloc_semaphore("p4_t")
        s_a4 = nc.alloc_semaphore("p4_a")
        s_d4 = nc.alloc_semaphore("p4_d")
        s_ks = nc.alloc_semaphore("p4_ks")
        s_vs = nc.alloc_semaphore("p4_vs")
        blk = _es.enter_context(nc.Block())

        @blk.tensor
        def _(t):
            ntr = 0
            for rc in range(24):
                t.wait_ge(s_wl4[rc % 2], 16 * (rc // 2 + 1))
                for cc in range(CC):
                    if ntr >= 4:
                        t.wait_ge(s_a4, ntr - 3)
                    t.transpose(trp4[:, (ntr % 4) * 1024:(ntr % 4) * 1024 + 128],
                                wrow4[:, (rc % 2) * D + cc * 128:(rc % 2) * D + (cc + 1) * 128],
                                identb[:, :]).then_inc(s_t4, 1)
                    ntr += 1
            t.wait_ge(s_x4, 160)
            t.wait_ge(s_a4, 192)
            t.transpose(trp4.bitcast(F32)[:, 1536:1560], binq_sb[0:24, :],
                        identf[0:24, 0:24]).then_inc(s_t4, 1)
            npm = 0
            for kind in ("k", "q"):
                for dj in range(8):
                    for win in range(4):
                        pb = [pq1, pq2][npm % 2]
                        if npm >= 2:
                            t.wait_ge(s_d4, npm - 1)
                        joff = (0 if kind == "q" else 1024) + dj * 128
                        for cc in range(CC):
                            mm = t.matmul(pb[:, :],
                                          wint[:, cc * 3072 + joff: cc * 3072 + joff + 128],
                                          x2h[:, cc * T + win * 512: cc * T + (win + 1) * 512],
                                          start=(cc == 0), stop=(cc == CC - 1))
                        mm.then_inc(s_t4, 1)
                        npm += 1
            for tt in range(NT):
                for win in range(2):
                    pb = [pq1, pq2][npm % 2]
                    if npm >= 2:
                        t.wait_ge(s_d4, npm - 1)
                    for cc in range(CC):
                        mm = t.matmul(pb[:, :],
                                      x2h[:, cc * T + tt * 128: cc * T + (tt + 1) * 128],
                                      wint[:, cc * 3072 + 2048 + win * 512:
                                           cc * 3072 + 2048 + (win + 1) * 512],
                                      start=(cc == 0), stop=(cc == CC - 1))
                    mm.then_inc(s_t4, 1)
                    npm += 1

        @blk.scalar
        def _(sc):
            ntr = 0
            for rc in range(24):
                for cc in range(CC):
                    sc.wait_ge(s_t4, ntr + 1)
                    sc.copy(wint[:, cc * 3072 + rc * 128: cc * 3072 + rc * 128 + 128],
                            trp4[:, (ntr % 4) * 1024:(ntr % 4) * 1024 + 128]).then_inc(s_a4, 1)
                    ntr += 1
            sc.wait_ge(s_t4, 193)
            sc.copy(binqT[:, :], trp4.bitcast(F32)[:, 1536:1560]).then_inc(s_a4, 1)

        @blk.vector
        def _(v):
            ec = SelfChain(v, s_d4)
            v.wait_ge(s_a4, 193)
            npm = 0
            for kind in ("k", "q"):
                for dj in range(8):
                    for win in range(4):
                        pb = [pq1, pq2][npm % 2]
                        v.wait_ge(s_t4, 193 + npm + 1)
                        if kind == "q":
                            ec.op("tensor_scalar",
                                  qf2[:, dj * T + win * 512: dj * T + (win + 1) * 512],
                                  pb[:, :], binqT[:, dj:dj + 1], 1.0 / 16.0,
                                  OP.add, OP.mult)
                        else:
                            if win == 0 and dj >= 1:
                                v.wait_ge(s_ks, 16 * dj)
                            ec.op("tensor_scalar",
                                  kstg[:, win * 512:(win + 1) * 512],
                                  pb[:, :], binqT[:, 8 + dj:9 + dj], None, OP.add)
                        npm += 1
            for tt in range(NT):
                for win in range(2):
                    pb = [pq1, pq2][npm % 2]
                    v.wait_ge(s_t4, 193 + npm + 1)
                    if win == 0 and tt >= 1:
                        v.wait_ge(s_vs, 16 * tt)
                    ec.op("tensor_tensor",
                          vstg[:, win * 512:(win + 1) * 512],
                          pb[:, :], bvbc[:, win * 512:(win + 1) * 512], OP.add)
                    npm += 1
            assert ec.n == 96

        @blk.sync
        def _(s):
            for cc in range(CC):
                ap_in = bass.AP(sp_x2nh, cc * 128, [[NT * D, 128], [D, NT], [1, 128]])
                s.dma_start(out=x2h.ap().rearrange("p (c t j) -> p c t j", c=CC, t=NT, j=128)[:, cc],
                            in_=ap_in).then_inc(s_x4, 16)
            s.dma_start(out=binq_sb[:, :], in_=binq_in[:, :]).then_inc(s_x4, 16)
            s.dma_start(out=bvbc[:, :],
                        in_=bass.AP(binq_in, 2048, [[0, 128], [1, D]])).then_inc(s_x4, 16)
            for dj in range(8):
                s.wait_ge(s_d4, 4 * dj + 4)
                s.dma_start(out=cc_kin[dj * 128:(dj + 1) * 128, :],
                            in_=kstg[:, :]).then_inc(s_ks, 16)
            for tt in range(NT):
                s.wait_ge(s_d4, 66 + 2 * tt)
                s.dma_start(out=cc_vin[tt * 128:(tt + 1) * 128, :],
                            in_=vstg[:, :]).then_inc(s_vs, 16)

        @blk.gpsimd
        def _(g):
            for rc in range(24):
                if rc >= 2:
                    g.wait_ge(s_t4, 8 * (rc - 1))
                g.dma_start(out=wrow4[:, (rc % 2) * D:(rc % 2 + 1) * D],
                            in_=win_in[rc * 128:(rc + 1) * 128, :]).then_inc(s_wl4[rc % 2], 16)
            g.wait_ge(s_ks, 128)
            g.collective_compute("AllGather", OP.bypass, replica_groups=RG,
                                 ins=[cc_kin.ap().opt()],
                                 outs=[cc_kout.ap().opt()]).then_inc(s_ag4, 1)
            g.wait_ge(s_vs, 256)
            g.collective_compute("AllGather", OP.bypass, replica_groups=RG,
                                 ins=[cc_vin.ap().opt()],
                                 outs=[cc_vout.ap().opt()]).then_inc(s_ag4, 1)

    _es_s = ExitStack()
    selcb = _es_s.enter_context(nc.sbuf_tensor("selcb", [128, 16], BF16))
    stok = _es_s.enter_context(nc.sbuf_tensor("stok", [128, NT * 32], F32))
    etok = _es_s.enter_context(nc.sbuf_tensor("etok", [128, NT * 32], F32))
    vmb = _es_s.enter_context(nc.sbuf_tensor("vmb", [128, 3 * D], F32))
    s_t5 = nc.alloc_semaphore("p5_t")
    s_a5 = nc.alloc_semaphore("p5_a")
    s_d5 = nc.alloc_semaphore("p5_d")
    s_vm = [nc.alloc_semaphore(f"p5_vm{i}") for i in range(3)]

    with ExitStack() as _es:
        kmb = _es.enter_context(nc.sbuf_tensor("kmb", [128, 2 * T], BF16))
        prodb = _es.enter_context(nc.sbuf_tensor("prodb", [128, 2 * T], BF16))
        ssb = _es.enter_context(nc.sbuf_tensor("ssb", [4, 2 * T], F32))
        smax = _es.enter_context(nc.sbuf_tensor("smax", [128, 12], F32))
        Sps = _es.enter_context(nc.psum_tensor("Sps", [4, 2048], F32))
        trp5 = _es.enter_context(nc.psum_tensor("trp5", [128, 1024], F32))
        s_c5 = nc.alloc_semaphore("p5_c")
        s_km = [nc.alloc_semaphore("p5_km0"), nc.alloc_semaphore("p5_km1")]
        blk = _es.enter_context(nc.Block())

        def t5_sc(m, cc):
            return 24 * m + cc + 1

        def t5_tr(m, tt):
            return 24 * m + 8 + tt + 1

        def a5_ssb(m):
            return 17 * m + 1

        def a5_st(m, tt):
            return 17 * m + 2 + tt

        def a5_exp(tt):
            return 137 + tt

        def d5_prod(m, cc):
            return 8 * m + cc + 1

        def d5_sub(tt):
            return 64 + 16 * tt + 16

        def stok_sl(tt, m):
            return bass.AP(stok, tt * 32 + m, [[NT * 32, 128], [8, 4]])

        def etok_sl(tt, m):
            return bass.AP(etok, tt * 32 + m, [[NT * 32, 128], [8, 4]])

        @blk.tensor
        def _(t):
            t.wait_ge(s_c5, 16)
            for m in range(8):
                for cc in range(CC):
                    t.wait_ge(s_d5, d5_prod(m, cc))
                    if cc == 0 and m >= 1:
                        t.wait_ge(s_a5, a5_ssb(m - 1))
                    for win in range(4):
                        mm = t.matmul(Sps[0:4, win * 512:(win + 1) * 512],
                                      selcb[:, (cc // 2) * 4:(cc // 2) * 4 + 4],
                                      prodb[:, (cc % 2) * T + win * 512:
                                            (cc % 2) * T + (win + 1) * 512],
                                      start=(cc == 0), stop=(cc == CC - 1))
                    mm.then_inc(s_t5, 1)
                t.wait_ge(s_a5, a5_ssb(m))
                for tt in range(NT):
                    k = 16 * m + tt
                    if k >= 2:
                        t.wait_ge(s_a5, a5_st((k - 2) // 16, (k - 2) % 16))
                    t.transpose(trp5[:, (k % 2) * 512:(k % 2) * 512 + 4],
                                ssb[0:4, (m % 2) * T + tt * 128:(m % 2) * T + (tt + 1) * 128],
                                identf[0:4, 0:4]).then_inc(s_t5, 1)

        @blk.scalar
        def _(sc):
            for m in range(8):
                sc.wait_ge(s_t5, t5_sc(m, 7))
                if m >= 2:
                    sc.wait_ge(s_t5, t5_tr(m - 2, 15))
                sc.copy(ssb[0:4, (m % 2) * T:(m % 2 + 1) * T], Sps[0:4, :]).then_inc(s_a5, 1)
                for tt in range(NT):
                    sc.wait_ge(s_t5, t5_tr(m, tt))
                    k = 16 * m + tt
                    sc.copy(stok_sl(tt, m),
                            trp5[:, (k % 2) * 512:(k % 2) * 512 + 4]).then_inc(s_a5, 1)
            for tt in range(NT):
                sc.wait_ge(s_d5, d5_sub(tt))
                sc.activation(etok[:, tt * 32:(tt + 1) * 32],
                              stok[:, tt * 32:(tt + 1) * 32], AF.Exp).then_inc(s_a5, 1)

        @blk.vector
        def _(v):
            ec = SelfChain(v, s_d5)
            for m in range(8):
                for cc in range(CC):
                    j = 8 * m + cc
                    v.wait_ge(s_km[j % 2], 16 * (j // 2 + 1))
                    if j >= 2:
                        v.wait_ge(s_t5, t5_sc((j - 2) // 8, (j - 2) % 8))
                    ec.op("tensor_tensor",
                          prodb[:, (cc % 2) * T:(cc % 2 + 1) * T],
                          qf2[:, cc * T:(cc + 1) * T],
                          kmb[:, (j % 2) * T:(j % 2 + 1) * T], OP.mult)
            assert ec.n == 64
            v.wait_ge(s_a5, 136)
            for tt in range(NT):
                ec.op("tensor_copy", smax[:, 0:4], stok_sl(tt, 0))
                for m in range(1, 8):
                    ec.op("tensor_tensor", smax[:, 0:4], smax[:, 0:4],
                          stok_sl(tt, m), OP.max)
                for m in range(8):
                    ec.op("tensor_tensor", stok_sl(tt, m), stok_sl(tt, m),
                          smax[:, 0:4], OP.subtract)
            assert ec.n == 320
            for tt in range(NT):
                v.wait_ge(s_a5, a5_exp(tt))
                ec.op("tensor_copy", smax[:, 4:8], etok_sl(tt, 0))
                for m in range(1, 8):
                    ec.op("tensor_tensor", smax[:, 4:8], smax[:, 4:8],
                          etok_sl(tt, m), OP.add)
                ec.op("reciprocal", smax[:, 8:12], smax[:, 4:8])
                for m in range(8):
                    ec.op("tensor_tensor", etok_sl(tt, m), etok_sl(tt, m),
                          smax[:, 8:12], OP.mult)
            assert ec.n == 592

        @blk.sync
        def _(s):
            s.dma_start(out=selcb[:, :], in_=selc[:, :]).then_inc(s_c5, 16)
            s.wait_ge(s_ag4, 1)
            for m in range(8):
                for cc in range(CC):
                    j = 8 * m + cc
                    if j >= 2:
                        s.wait_ge(s_d5, j - 1)
                    s.dma_start(out=kmb[:, (j % 2) * T:(j % 2 + 1) * T],
                                in_=cc_kout[m * D + cc * 128: m * D + (cc + 1) * 128, :]
                                ).then_inc(s_km[j % 2], 16)

    s_ag6 = nc.alloc_semaphore("p6_ag")

    with ExitStack() as _es:
        wrow6 = _es.enter_context(nc.sbuf_tensor("wrow6", [128, 2 * D], BF16))
        wouT = _es.enter_context(nc.sbuf_tensor("wouT", [128, CC * D], BF16))
        ob16 = _es.enter_context(nc.sbuf_tensor("ob16", [128, 2 * D], BF16))
        ofs = _es.enter_context(nc.sbuf_tensor("ofs", [128, 2 * D], BF16))
        x2nld = _es.enter_context(nc.sbuf_tensor("x2nld", [128, 2 * D], F32))
        att6 = _es.enter_context(nc.sbuf_tensor("att6", [128, D], F32))
        scr6 = _es.enter_context(nc.sbuf_tensor("scr6", [128, D], F32))
        y1t = _es.enter_context(nc.sbuf_tensor("y1t", [128, 2 * D], F32))
        y1b16 = _es.enter_context(nc.sbuf_tensor("y1b16", [128, 2 * D], BF16))
        y1fs = _es.enter_context(nc.sbuf_tensor("y1fs", [128, 2 * D], F32))
        st6 = _es.enter_context(nc.sbuf_tensor("st6", [128, 8 * NT], F32))
        boutbc = _es.enter_context(nc.sbuf_tensor("boutbc", [128, D], F32))
        g1v = _es.enter_context(nc.sbuf_tensor("g1v", [128, D], F32))
        b1v = _es.enter_context(nc.sbuf_tensor("b1v", [128, D], F32))
        wgb = _es.enter_context(nc.sbuf_tensor("wgb", [128, 64], F32))
        trp6 = _es.enter_context(nc.psum_tensor("trp6", [128, 4096], BF16))
        wopA = _es.enter_context(nc.psum_tensor("wopA", [128, 512], F32))
        wopB = _es.enter_context(nc.psum_tensor("wopB", [128, 512], F32))
        Gps = _es.enter_context(nc.psum_tensor("Gps", [128, 1024], F32))
        s_wl6 = [nc.alloc_semaphore("p6_wla"), nc.alloc_semaphore("p6_wlb")]
        s_xl6 = [nc.alloc_semaphore("p6_xla"), nc.alloc_semaphore("p6_xlb")]
        s_y1s = [nc.alloc_semaphore("p6_y1sa"), nc.alloc_semaphore("p6_y1sb")]
        s_y1c = [nc.alloc_semaphore("p6_y1ca"), nc.alloc_semaphore("p6_y1cb")]
        s_cg = nc.alloc_semaphore("p6_cg")
        s_t6 = nc.alloc_semaphore("p6_t")
        s_a6 = nc.alloc_semaphore("p6_a")
        blk = _es.enter_context(nc.Block())
        RSD = 1.0 / 32.0

        dve_order = []
        for tt in range(NT):
            for m in range(8):
                for h in range(4):
                    dve_order.append(("cmb", tt, 8 * m + h * 2))
        dve_order = []
        for tt in range(NT):
            for m in range(8):
                for h in range(4):
                    dve_order.append(("cmb", tt, m, h))
            if tt >= 2:
                for j in range(1, 9):
                    dve_order.append(("ln", tt - 2, j))
        for jt in (NT - 2, NT - 1):
            for j in range(1, 9):
                dve_order.append(("ln", jt, j))
        d5ev = {k: 592 + i + 1 for i, k in enumerate(dve_order)}

        def cmb_end(tt):
            return d5ev[("cmb", tt, 7, 3)]

        def t6_tr(tt, cc):
            return 64 + 19 * tt + cc + 1

        def t6_pr(tt, win):
            return 64 + 19 * tt + 9 + win

        def t6_ftr(tt, cc):
            return 64 + 19 * tt + 11 + cc

        def t6_g(tt):
            return 64 + 19 * tt + 19

        def a6(tt, k):
            return 64 + 21 * tt + k

        @blk.vector
        def _(v):
            ec = SelfChain(v, s_d5, n0=592)
            v.wait_ge(s_cg, 64)
            for key in dve_order:
                if key[0] == "cmb":
                    _, tt, m, h = key
                    q = tt * 8 + m
                    if h == 0:
                        v.wait_ge(s_vm[q % 3], 16 * (q // 3 + 1))
                    col = etok[:, tt * 32 + h * 8 + m: tt * 32 + h * 8 + m + 1]
                    osl = o_tok[:, tt * D + h * 256: tt * D + (h + 1) * 256]
                    vsl = vmb[:, (q % 3) * D + h * 256:(q % 3) * D + (h + 1) * 256]
                    if m == 0:
                        ec.op("tensor_scalar", osl, vsl, col, None, OP.mult)
                    else:
                        ec.op("scalar_tensor_tensor", osl, vsl, col, osl,
                              OP.mult, OP.add)
                    continue
                _, tt, j = key
                sb = st6[:, 8 * tt:8 * tt + 8]
                at = att6[:, :]
                xl = x2nld[:, (tt % 2) * D:(tt % 2 + 1) * D]
                y1 = y1t[:, (tt % 2) * D:(tt % 2 + 1) * D]
                if j == 1:
                    v.wait_ge(s_t6, t6_pr(tt, 0))
                    v.wait_ge(s_xl6[tt % 2], 16 * (tt // 2 + 1))
                    if tt >= 1:
                        v.wait_ge(s_a6, a6(tt - 1, 11))
                    ec.op("tensor_tensor", at[:, 0:512], wopA[:, :], xl[:, 0:512], OP.add)
                elif j == 2:
                    v.wait_ge(s_t6, t6_pr(tt, 1))
                    ec.op("tensor_tensor", at[:, 512:1024], wopB[:, :], xl[:, 512:1024], OP.add)
                elif j == 3:
                    ec.op("tensor_tensor", at, at, boutbc[:, :], OP.add)
                elif j == 4:
                    v.wait_ge(s_a6, a6(tt, 10))
                    ec.op("tensor_scalar", at, at, sb[:, 0:1], None, OP.subtract)
                elif j == 5:
                    v.wait_ge(s_a6, a6(tt, 12))
                    ec.op("reciprocal", sb[:, 4:5], sb[:, 3:4])
                elif j == 6:
                    if tt >= 2:
                        v.wait_ge(s_t6, t6_ftr(tt - 2, 7))
                        v.wait_ge(s_y1s[tt % 2], 16 * (tt // 2))
                    ec.op("scalar_tensor_tensor", y1, at, sb[:, 4:5], g1v[:, :],
                          OP.mult, OP.mult)
                elif j == 7:
                    ec.op("tensor_tensor", y1, y1, b1v[:, :], OP.add)
                else:
                    if tt >= 2:
                        v.wait_ge(s_y1c[tt % 2], 16 * (tt // 2))
                    ec.op("tensor_copy", y1b16[:, (tt % 2) * D:(tt % 2 + 1) * D], y1)
            assert ec.n == 592 + len(dve_order)

        @blk.tensor
        def _(t):
            ntr = 0
            for rc in range(CC):
                t.wait_ge(s_wl6[rc % 2], 16 * (rc // 2 + 1))
                for cc in range(CC):
                    if ntr >= 4:
                        t.wait_ge(s_a6, ntr - 3)
                    t.transpose(trp6[:, (ntr % 4) * 1024:(ntr % 4) * 1024 + 128],
                                wrow6[:, (rc % 2) * D + cc * 128:(rc % 2) * D + (cc + 1) * 128],
                                identb[:, :]).then_inc(s_t6, 1)
                    ntr += 1
            t.wait_ge(s_a6, 64)
            t.wait_ge(s_cg, 64)
            for tt in range(NT):
                t.wait_ge(s_a6, a6(tt, 1))
                for cc in range(CC):
                    k = 8 * tt + cc
                    if k >= 2:
                        t.wait_ge(s_a6, a6((k - 2) // 8, 2 + (k - 2) % 8))
                    t.transpose(trp6[:, (k % 2) * 1024:(k % 2) * 1024 + 128],
                                ob16[:, (tt % 2) * D + cc * 128:(tt % 2) * D + (cc + 1) * 128],
                                identb[:, :]).then_inc(s_t6, 1)
                t.wait_ge(s_a6, a6(tt, 9))
                for win in range(2):
                    if tt >= 1:
                        t.wait_ge(s_d5, d5ev[("ln", tt - 1, win + 1)])
                    pb = [wopA, wopB][win]
                    for cc in range(CC):
                        mm = t.matmul(pb[:, :],
                                      ofs[:, (tt % 2) * D + cc * 128:(tt % 2) * D + (cc + 1) * 128],
                                      wouT[:, cc * D + win * 512: cc * D + (win + 1) * 512],
                                      start=(cc == 0), stop=(cc == CC - 1))
                    mm.then_inc(s_t6, 1)
                t.wait_ge(s_d5, d5ev[("ln", tt, 7)])
                for cc in range(CC):
                    k = 8 * tt + cc
                    if k >= 2:
                        t.wait_ge(s_a6, a6((k - 2) // 8, 13 + (k - 2) % 8))
                    t.transpose(trp6.bitcast(F32)[:, 1024 + (k % 2) * 512:
                                                  1024 + (k % 2) * 512 + 128],
                                y1t[:, (tt % 2) * D + cc * 128:(tt % 2) * D + (cc + 1) * 128],
                                identf[:, :]).then_inc(s_t6, 1)
                t.wait_ge(s_a6, a6(tt, 20))
                if tt >= 2:
                    t.wait_ge(s_a6, a6(tt - 2, 21))
                for cc in range(CC):
                    mm = t.matmul(Gps[:, (tt % 2) * 512:(tt % 2) * 512 + 8],
                                  y1fs[:, (tt % 2) * D + cc * 128:(tt % 2) * D + (cc + 1) * 128],
                                  wgb[:, cc * 8:(cc + 1) * 8],
                                  start=(cc == 0), stop=(cc == CC - 1))
                mm.then_inc(s_t6, 1)

        @blk.scalar
        def _(sc):
            na = 0

            def aop(wait_t=None, wait_d=None, copy=None, act=None):
                nonlocal na
                if wait_t is not None:
                    sc.wait_ge(s_t6, wait_t)
                if wait_d is not None:
                    sc.wait_ge(s_d5, wait_d)
                if na:
                    sc.wait_ge(s_a6, na)
                inst = sc.copy(*copy) if copy else sc.activation(*act[0], **act[1])
                inst.then_inc(s_a6, 1)
                na += 1
            ntr = 0
            for rc in range(CC):
                for cc in range(CC):
                    aop(wait_t=ntr + 1,
                        copy=(wouT[:, cc * D + rc * 128: cc * D + rc * 128 + 128],
                              trp6[:, (ntr % 4) * 1024:(ntr % 4) * 1024 + 128]))
                    ntr += 1
            for tt in range(NT):
                sb = st6[:, 8 * tt:8 * tt + 8]
                aop(wait_t=(t6_tr(tt - 2, 7) if tt >= 2 else None),
                    wait_d=cmb_end(tt),
                    copy=(ob16[:, (tt % 2) * D:(tt % 2 + 1) * D],
                          o_tok[:, tt * D:(tt + 1) * D]))
                for cc in range(CC):
                    aop(wait_t=max(t6_tr(tt, cc),
                                   t6_pr(tt - 2, 1) if (cc == 0 and tt >= 2) else 0),
                        copy=(ofs[:, (tt % 2) * D + cc * 128:(tt % 2) * D + (cc + 1) * 128],
                              trp6[:, ((8 * tt + cc) % 2) * 1024:((8 * tt + cc) % 2) * 1024 + 128]))
                aop(wait_d=d5ev[("ln", tt, 3)],
                    act=((scr6[:, :], att6[:, :], AF.Identity),
                         dict(bias=0.0, scale=1.0 / D, accum_out=sb[:, 0:1])))
                aop(wait_d=d5ev[("ln", tt, 4)],
                    act=((scr6[:, :], att6[:, :], AF.Square),
                         dict(scale=RSD, accum_out=sb[:, 1:2])))
                aop(act=((sb[:, 3:4], sb[:, 1:2], AF.Sqrt),
                         dict(bias=epsc[:, :])))
                for cc in range(CC):
                    aop(wait_t=max(t6_ftr(tt, cc),
                                   t6_g(tt - 2) if (cc == 0 and tt >= 2) else 0),
                        copy=(y1fs[:, (tt % 2) * D + cc * 128:(tt % 2) * D + (cc + 1) * 128],
                              trp6.bitcast(F32)[:, 1024 + ((8 * tt + cc) % 2) * 512:
                                                1024 + ((8 * tt + cc) % 2) * 512 + 128]))
                aop(wait_t=t6_g(tt),
                    copy=(G_sb[:, 8 * tt:8 * tt + 8],
                          Gps[:, (tt % 2) * 512:(tt % 2) * 512 + 8]))
            assert na == 64 + 21 * NT

        @blk.sync
        def _(s):
            for tt in range(NT):
                if tt >= 2:
                    s.wait_ge(s_d5, d5ev[("ln", tt - 2, 2)])
                s.dma_start(out=x2nld[:, (tt % 2) * D:(tt % 2 + 1) * D],
                            in_=sp_x2n[tt * 128:(tt + 1) * 128, :]).then_inc(s_xl6[tt % 2], 16)
                s.wait_ge(s_d5, d5ev[("ln", tt, 7)])
                s.dma_start(out=sp_y1[tt * 128:(tt + 1) * 128, :],
                            in_=y1t[:, (tt % 2) * D:(tt % 2 + 1) * D]).then_inc(s_y1s[tt % 2], 16)
                s.wait_ge(s_d5, d5ev[("ln", tt, 8)])
                s.dma_start(out=cc_y1in[tt * 128:(tt + 1) * 128, :],
                            in_=y1b16[:, (tt % 2) * D:(tt % 2 + 1) * D]).then_inc(s_y1c[tt % 2], 16)
            s.wait_ge(s_y1s[0], 128)
            s.wait_ge(s_y1s[1], 128)

        @blk.gpsimd
        def _(g):
            for rc in range(CC):
                if rc >= 2:
                    g.wait_ge(s_t6, 8 * (rc - 1))
                g.dma_start(out=wrow6[:, (rc % 2) * D:(rc % 2 + 1) * D],
                            in_=wout_in[rc * 128:(rc + 1) * 128, :]).then_inc(s_wl6[rc % 2], 16)
            g.dma_start(out=wgb.ap().rearrange("p (c e) -> p c e", c=8),
                        in_=bass.AP(wg_in, 0, [[8, 128], [1024, 8], [1, 8]])).then_inc(s_cg, 16)
            g.dma_start(out=boutbc[:, :],
                        in_=bout_in.ap().broadcast_to([128, D])).then_inc(s_cg, 16)
            g.dma_start(out=g1v[:, :],
                        in_=ln1g_in.ap().broadcast_to([128, D])).then_inc(s_cg, 16)
            g.dma_start(out=b1v[:, :],
                        in_=ln1b_in.ap().broadcast_to([128, D])).then_inc(s_cg, 16)
            g.wait_ge(s_ag4, 2)
            for q in range(128):
                tt, m = q // 8, q % 8
                if q >= 3:
                    qp = q - 3
                    g.wait_ge(s_d5, d5ev[("cmb", qp // 8, qp % 8, 3)])
                g.dma_start(out=vmb[:, (q % 3) * D:(q % 3 + 1) * D],
                            in_=cc_vout[m * T + tt * 128: m * T + (tt + 1) * 128, :]
                            ).then_inc(s_vm[q % 3], 16)

    _es_s.close()
    _es_q.close()
    _es_o.close()

    if upto == "p4":
        s_l = nc.alloc_semaphore("fx4_l")
        s_st = nc.alloc_semaphore("fx4_s")
        with (
            nc.sbuf_tensor("ot4", [128, D], F32) as ot4,
            nc.Block() as blk,
        ):
            @blk.gpsimd
            def _(g):
                for tt in range(NT):
                    g.dma_start(out=ot4[:, :], in_=sp_y1[tt * 128:(tt + 1) * 128, :]).then_inc(s_l, 16)
                    g.wait_ge(s_l, 16 * (tt + 1))
                    g.dma_start(out=out[tt * 128:(tt + 1) * 128, :], in_=ot4[:, :]).then_inc(s_st, 16)
                    g.wait_ge(s_st, 16 * (tt + 1))
        return nc

    _es_w = ExitStack()
    w1sb = _es_w.enter_context(nc.sbuf_tensor("w1sb", [128, CC * FF], BF16))
    w2sb = _es_w.enter_context(nc.sbuf_tensor("w2sb", [128, 32 * D], BF16))
    gidx = _es_w.enter_context(nc.sbuf_tensor("gidx", [128, NT], I32))
    s_wld = nc.alloc_semaphore("p7_wld")
    s_d7 = nc.alloc_semaphore("p7_d")

    with ExitStack() as _es:
        stairb = _es.enter_context(nc.sbuf_tensor("stairb", [128, 4096], BF16))
        erowb = _es.enter_context(nc.sbuf_tensor("erowb", [128, 8], F32))
        crev = _es.enter_context(nc.sbuf_tensor("crev", [128, 8], F32))
        mb = _es.enter_context(nc.sbuf_tensor("mb", [128, NT * 8], BF16))
        mfull = _es.enter_context(nc.sbuf_tensor("mfull", [128, NT * 8], F32))
        m8 = _es.enter_context(nc.sbuf_tensor("m8", [128, 8], F32))
        l8 = _es.enter_context(nc.sbuf_tensor("l8", [128, 8], F32))
        gsubb = _es.enter_context(nc.sbuf_tensor("gsubb", [128, NT * 8], F32))
        gexpb = _es.enter_context(nc.sbuf_tensor("gexpb", [128, NT * 8], F32))
        smx5 = _es.enter_context(nc.sbuf_tensor("smx5", [128, 8], F32))
        idxf = _es.enter_context(nc.sbuf_tensor("idxf", [128, NT], F32))
        posb = _es.enter_context(nc.sbuf_tensor("posb", [128, NT], F32))
        kpb = _es.enter_context(nc.sbuf_tensor("kpb", [128, NT], F32))
        dstf = _es.enter_context(nc.sbuf_tensor("dstf", [128, NT], F32))
        dstcf = _es.enter_context(nc.sbuf_tensor("dstcf", [128, NT], F32))
        cntsb = _es.enter_context(nc.sbuf_tensor("cntsb", [8, 1], F32))
        cntb = _es.enter_context(nc.sbuf_tensor("cntb", [8, 8], F32))
        pmb = _es.enter_context(nc.sbuf_tensor("pmb", [8, 1], F32))
        basesb = _es.enter_context(nc.sbuf_tensor("basesb", [8, 1], F32))
        baseb = _es.enter_context(nc.sbuf_tensor("baseb", [128, 8], F32))
        locsb = _es.enter_context(nc.sbuf_tensor("locsb", [8, 2048], F32))
        ltok = _es.enter_context(nc.sbuf_tensor("ltok", [128, NT * 8], F32))
        zrow = _es.enter_context(nc.sbuf_tensor("zrow", [128, 129], F32))
        invsb = _es.enter_context(nc.sbuf_tensor("invsb", [128, NT], F32))
        invw_sb = _es.enter_context(nc.sbuf_tensor("invw_sb", [128, 129 * NT], F32))
        invcol = _es.enter_context(nc.sbuf_tensor("invcol", [128, 129], F32))
        myinvb = _es.enter_context(nc.sbuf_tensor("myinvb", [128, 1], I32))
        c128b = _es.enter_context(nc.sbuf_tensor("c128b", [128, 1], F32))
        myinvf = _es.enter_context(nc.sbuf_tensor("myinvf", [128, 1], F32))
        off8f = _es.enter_context(nc.sbuf_tensor("off8f", [128, 8], F32))
        off8i = _es.enter_context(nc.sbuf_tensor("off8i", [128, 8], I32))
        inv8 = _es.enter_context(nc.sbuf_tensor("inv8", [128, 8 * NT], F32))
        izb = _es.enter_context(nc.sbuf_tensor("izb", [128, NT], F32))
        gidxf = _es.enter_context(nc.sbuf_tensor("gidxf", [128, NT], F32))
        sidxsb = _es.enter_context(nc.sbuf_tensor("sidxsb", [128, NT], F32))
        loc_ps = _es.enter_context(nc.psum_tensor("loc_ps", [8, 2048], F32))
        cnt_ps = _es.enter_context(nc.psum_tensor("cnt_ps", [8, 512], F32))
        base_ps = _es.enter_context(nc.psum_tensor("base_ps", [8, 512], F32))
        ltps = _es.enter_context(nc.psum_tensor("ltps", [128, 1024], F32))
        s_c7 = nc.alloc_semaphore("p7_c")
        s_t7 = nc.alloc_semaphore("p7_t")
        s_a7 = nc.alloc_semaphore("p7_a")
        s_cc0 = nc.alloc_semaphore("p7_cc0")
        s_cag = nc.alloc_semaphore("p7_cag")
        s_cnb = nc.alloc_semaphore("p7_cnb")
        s_bb = nc.alloc_semaphore("p7_bb")
        s_cb = nc.alloc_semaphore("p7_cb")
        s_zi = nc.alloc_semaphore("p7_zi")
        s_sc7 = nc.alloc_semaphore("p7_sc")
        s_iw = nc.alloc_semaphore("p7_iw")
        s_is = nc.alloc_semaphore("p7_is")
        s_ar = nc.alloc_semaphore("p7_ar")
        s_ms = nc.alloc_semaphore("p7_ms")
        blk = _es.enter_context(nc.Block())

        @blk.vector
        def _(v):
            ec = SelfChain(v, s_d7)
            v.wait_ge(s_c7, 80)
            ec.op("tensor_scalar", crev[:, :], erowb[:, :], -1.0, 8.0, OP.mult, OP.add)
            ec.op("memset", zrow[:, :], 0.0)
            ec.op("memset", c128b[:, :], 128.0)
            for tt in range(NT):
                Gt = G_sb[:, 8 * tt:8 * tt + 8]
                ec.op("tensor_reduce", smx5[:, 0:1], Gt, AXL.X, OP.max)
                ec.op("tensor_scalar", mfull[:, 8 * tt:8 * tt + 8], Gt,
                      smx5[:, 0:1], None, OP.is_equal)
                ec.op("tensor_copy", mb[:, 8 * tt:8 * tt + 8], mfull[:, 8 * tt:8 * tt + 8])
                ec.op("tensor_tensor", m8[:, :], mfull[:, 8 * tt:8 * tt + 8],
                      crev[:, :], OP.mult)
                ec.op("tensor_reduce", smx5[:, 1:2], m8[:, :], AXL.X, OP.max)
                ec.op("tensor_scalar", idxf[:, tt:tt + 1], smx5[:, 1:2], -1.0, 8.0,
                      OP.mult, OP.add)
                ec.op("tensor_scalar", gsubb[:, 8 * tt:8 * tt + 8], Gt,
                      smx5[:, 0:1], None, OP.subtract)
            assert ec.n == 115
            for tt in range(NT):
                v.wait_ge(s_a7, tt + 1)
                ec.op("tensor_reduce", smx5[:, 2:3], gexpb[:, 8 * tt:8 * tt + 8],
                      AXL.X, OP.add)
                ec.op("reciprocal", gval[:, tt:tt + 1], smx5[:, 2:3])
            assert ec.n == 147
            v.wait_ge(s_cb, 16)
            for tt in range(NT):
                v.wait_ge(s_a7, 20 + tt)
                ec.op("tensor_tensor", l8[:, :], ltok[:, 8 * tt:8 * tt + 8],
                      baseb[:, :], OP.add)
                ec.op("tensor_tensor", l8[:, :], l8[:, :],
                      mfull[:, 8 * tt:8 * tt + 8], OP.mult)
                ec.op("tensor_reduce", posb[:, tt:tt + 1], l8[:, :], AXL.X, OP.add)
                ec.op("tensor_scalar", smx5[:, 3:4], posb[:, tt:tt + 1], 2047.0,
                      None, OP.min)
                ec.op("tensor_tensor", kpb[:, tt:tt + 1], smx5[:, 3:4],
                      posb[:, tt:tt + 1], OP.is_equal)
                ec.op("tensor_scalar", dstcf[:, tt:tt + 1], idxf[:, tt:tt + 1],
                      2048.0, None, OP.mult)
                ec.op("tensor_tensor", dstcf[:, tt:tt + 1], dstcf[:, tt:tt + 1],
                      posb[:, tt:tt + 1], OP.add)
                ec.op("tensor_scalar", smx5[:, 4:5], posb[:, tt:tt + 1],
                      1.0 / 128.0, 8388608.0, OP.mult, OP.add)
                ec.op("tensor_scalar", smx5[:, 4:5], smx5[:, 4:5],
                      8388608.0, None, OP.subtract)
                ec.op("tensor_scalar", smx5[:, 5:6], smx5[:, 4:5],
                      128.0, None, OP.mult)
                ec.op("tensor_tensor", smx5[:, 5:6], posb[:, tt:tt + 1],
                      smx5[:, 5:6], OP.subtract)
                ec.op("tensor_scalar", smx5[:, 6:7], smx5[:, 5:6],
                      -1.0, None, OP.min)
                ec.op("tensor_tensor", smx5[:, 6:7], smx5[:, 6:7],
                      smx5[:, 5:6], OP.is_equal)
                ec.op("tensor_tensor", smx5[:, 4:5], smx5[:, 4:5],
                      smx5[:, 6:7], OP.subtract)
                ec.op("tensor_scalar", smx5[:, 6:7], smx5[:, 6:7],
                      128.0, None, OP.mult)
                ec.op("tensor_tensor", smx5[:, 5:6], smx5[:, 5:6],
                      smx5[:, 6:7], OP.add)
                ec.op("tensor_scalar", smx5[:, 5:6], smx5[:, 5:6],
                      16.0, None, OP.mult)
                ec.op("tensor_tensor", smx5[:, 5:6], smx5[:, 5:6],
                      smx5[:, 4:5], OP.add)
                ec.op("tensor_scalar", dstf[:, tt:tt + 1], idxf[:, tt:tt + 1],
                      2064.0, None, OP.mult)
                ec.op("tensor_tensor", dstf[:, tt:tt + 1], dstf[:, tt:tt + 1],
                      smx5[:, 5:6], OP.add)
                ec.op("tensor_scalar", dstf[:, tt:tt + 1], dstf[:, tt:tt + 1],
                      2048.0, None, OP.subtract)
                ec.op("tensor_tensor", dstf[:, tt:tt + 1], dstf[:, tt:tt + 1],
                      kpb[:, tt:tt + 1], OP.mult)
                ec.op("tensor_scalar", dstf[:, tt:tt + 1], dstf[:, tt:tt + 1],
                      2048.0, None, OP.add)
            assert ec.n == 147 + 23 * NT
            ec.op("tensor_tensor", gval[:, :], gval[:, :], kpb[:, :], OP.mult)
            ec.op("tensor_copy", dst_i32[:, :], dstf[:, :])
            ec.op("tensor_scalar", dstcf[:, :], dstcf[:, :], 16383.0, None, OP.min)
            ec.op("tensor_copy", dstc_i32[:, :], dstcf[:, :])
            ec.op("tensor_copy", myinvf[:, :], myinvb[:, :])
            for r in range(8):
                ec.op("tensor_scalar", off8f[:, r:r + 1], myinvf[:, :],
                      1032.0 * r, None, OP.add)
            ec.op("tensor_copy", off8i[:, :], off8f[:, :])
            v.wait_ge(s_iw, 256)
            ec.op("tensor_copy", invcol[:, :], invw_sb[:, 0:129])
            for cidx in range(1, NT):
                ec.op("tensor_tensor", invcol[:, :], invcol[:, :],
                      invw_sb[:, cidx * 129:(cidx + 1) * 129],
                      OP.add)
            v.wait_ge(s_ms, 128)
            ec.op("tensor_tensor", invsb[:, :], inv8[:, 0:NT],
                  inv8[:, NT:2 * NT], OP.add)
            for r in range(2, 8):
                ec.op("tensor_tensor", invsb[:, :], invsb[:, :],
                      inv8[:, r * NT:(r + 1) * NT], OP.add)
            ec.op("tensor_scalar", izb[:, :], invsb[:, :], 0.0, None, OP.is_equal)
            ec.op("tensor_scalar", gidxf[:, :], izb[:, :], 65536.0, None, OP.mult)
            ec.op("tensor_tensor", gidxf[:, :], gidxf[:, :], invsb[:, :], OP.add)
            ec.op("tensor_scalar", gidxf[:, :], gidxf[:, :], 1.0, None, OP.subtract)
            ec.op("tensor_scalar", gidxf[:, :], gidxf[:, :], 16383.0, None, OP.min)
            ec.op("tensor_copy", gidx[:, :], gidxf[:, :])
            assert ec.n == 558

        @blk.scalar
        def _(sc):
            for tt in range(NT):
                sc.wait_ge(s_d7, 7 * tt + 10)
                sc.activation(gexpb[:, 8 * tt:8 * tt + 8],
                              gsubb[:, 8 * tt:8 * tt + 8], AF.Exp).then_inc(s_a7, 1)
            sc.wait_ge(s_t7, 16)
            sc.copy(cntsb[0:8, 0:1], cnt_ps[0:8, 0:1]).then_inc(s_a7, 1)
            sc.wait_ge(s_t7, 80)
            sc.copy(locsb[0:8, :], loc_ps[0:8, :]).then_inc(s_a7, 1)
            sc.wait_ge(s_t7, 81)
            sc.copy(basesb[0:8, 0:1], base_ps[0:8, 0:1]).then_inc(s_a7, 1)
            for tt in range(NT):
                sc.wait_ge(s_t7, 82 + tt)
                sc.copy(ltok[:, 8 * tt:8 * tt + 8],
                        ltps[:, (tt % 2) * 512:(tt % 2) * 512 + 8]).then_inc(s_a7, 1)

        @blk.tensor
        def _(t):
            for tt in range(NT):
                t.wait_ge(s_d7, 7 * tt + 6)
                t.matmul(cnt_ps[0:8, 0:1], mb[:, 8 * tt:8 * tt + 8], onescol[:, 0:1],
                         start=(tt == 0), stop=(tt == NT - 1)).then_inc(s_t7, 1)
            t.wait_ge(s_c7, 80)
            for win in range(4):
                for tt in range(NT):
                    off = 2048 - 128 * tt - 1 + win * 512
                    t.matmul(loc_ps[0:8, win * 512:(win + 1) * 512],
                             mb[:, 8 * tt:8 * tt + 8], stairb[:, off:off + 512],
                             start=(tt == 0), stop=(tt == NT - 1)).then_inc(s_t7, 1)
            t.wait_ge(s_cnb, 16)
            t.matmul(base_ps[0:8, 0:1], cntb[0:8, :], pmb[0:8, 0:1],
                     start=True, stop=True).then_inc(s_t7, 1)
            t.wait_ge(s_a7, 18)
            for tt in range(NT):
                if tt >= 2:
                    t.wait_ge(s_a7, 20 + tt - 2)
                t.transpose(ltps[:, (tt % 2) * 512:(tt % 2) * 512 + 8],
                            locsb[0:8, tt * 128:(tt + 1) * 128],
                            identf[0:8, 0:8]).then_inc(s_t7, 1)

        @blk.sync
        def _(s):
            s.dma_start(out=erowb[:, :],
                        in_=erow.ap().broadcast_to([128, 8])).then_inc(s_c7, 16)
            s.dma_start(out=stairb[:, :], in_=stair[:, :]).then_inc(s_c7, 16)
            s.dma_start(out=sidxsb[:, :], in_=sidxf_in[:, :]).then_inc(s_c7, 16)
            s.dma_start(out=pmb[0:8, :], in_=prevmask_in[:, :]).then_inc(s_c7, 16)
            s.dma_start(out=myinvb[:, :], in_=myslot_in[:, :]).then_inc(s_c7, 16)
            s.wait_ge(s_wcast, 32)
            for cc in range(CC):
                s.dma_start(out=w1sb[:, cc * FF:(cc + 1) * FF],
                            in_=w1bf[cc * 128:(cc + 1) * 128, :]).then_inc(s_wld, 16)
            for j in range(8):
                s.dma_start(out=w2sb.ap().rearrange("p (f q o) -> p f q o", f=8, q=4)[:, j],
                            in_=bass.AP(w2bf, j * 512 * D,
                                        [[D, 128], [128 * D, 4], [1, D]])).then_inc(s_wld, 16)

        @blk.gpsimd
        def _(g):
            g.wait_ge(s_a7, 17)
            g.dma_start(out=cc_cntin.ap().rearrange("a b -> b a"),
                        in_=cntsb[0:8, 0:1]).then_inc(s_cc0, 16)
            g.wait_ge(s_cc0, 16)
            g.collective_compute("AllGather", OP.bypass, replica_groups=RG,
                                 ins=[cc_cntin.ap().opt()],
                                 outs=[cc_cntout.ap().opt()]).then_inc(s_cag, 1)
            g.wait_ge(s_y1c[0], 128)
            g.wait_ge(s_y1c[1], 128)
            g.collective_compute("AllGather", OP.bypass, replica_groups=RG,
                                 ins=[cc_y1in.ap().opt()],
                                 outs=[cc_y1out.ap().opt()]).then_inc(s_ag6, 1)
            g.wait_ge(s_cag, 1)
            g.dma_start(out=cntb[0:8, :], in_=cc_cntout[:, :]).then_inc(s_cnb, 16)
            g.wait_ge(s_a7, 19)
            g.dma_start(out=basesc.ap().rearrange("a b -> b a"),
                        in_=basesb[0:8, 0:1]).then_inc(s_bb, 16)
            g.wait_ge(s_bb, 16)
            g.dma_start(out=baseb[:, :],
                        in_=basesc.ap().broadcast_to([128, 8])).then_inc(s_cb, 16)
            g.wait_ge(s_d7, 2)
            for tt in range(NT):
                g.dma_start(out=cc_invw_t[tt].ap().rearrange("(p a) b -> p (a b)", p=128),
                            in_=zrow[:, :]).then_inc(s_zi, 16)
            g.wait_ge(s_zi, 256)
            g.wait_ge(s_d7, 517)
            for tt in range(NT):
                g.indirect_dma_start(
                    out=cc_invw_t[tt][:, :],
                    out_offset=IndirectOffsetOnAxis(ap=dst_i32[:, tt:tt + 1], axis=0),
                    in_=sidxsb[:, tt:tt + 1], in_offset=None).then_inc(s_sc7, 16)
            g.wait_ge(s_sc7, 256)
            for tt in range(NT):
                g.dma_start(out=invw_sb[:, tt * 129:(tt + 1) * 129],
                            in_=cc_invw_t[tt].ap().rearrange("(p a) b -> p (a b)", p=128)
                            ).then_inc(s_iw, 16)
            g.wait_ge(s_d7, 545)
            g.dma_start(out=cc_invin.ap().rearrange("(p a) b -> p (a b)", p=128),
                        in_=invcol[:, :]).then_inc(s_is, 16)
            g.wait_ge(s_is, 16)
            g.collective_compute("AllGather", OP.bypass, replica_groups=RG,
                                 ins=[cc_invin.ap().rearrange(
                                     "(a b) c -> a (b c)", b=128).opt()],
                                 outs=[cc_iall.ap().rearrange(
                                     "(a b) c -> a (b c)", b=128).opt()]).then_inc(s_ar, 1)
            g.wait_ge(s_ar, 1)
            g.wait_ge(s_d7, 529)
            iview = cc_iall.ap().rearrange("(g w) b -> g (w b)", w=16)
            for r in range(8):
                g.indirect_dma_start(
                    out=inv8[:, r * NT:(r + 1) * NT], out_offset=None,
                    in_=iview,
                    in_offset=IndirectOffsetOnAxis(ap=off8i[:, r:r + 1], axis=0)
                    ).then_inc(s_ms, 16)
            if upto == "p5":
                s_dg = nc.alloc_semaphore("p5_dump")
                g.wait_ge(s_d7, 558)
                dumps = [dstf, idxf, gval, kpb, posb, invsb, gidxf]
                for i, tsr in enumerate(dumps):
                    g.dma_start(out=out[i * 128:(i + 1) * 128, 0:NT],
                                in_=tsr[:, :]).then_inc(s_dg, 16)
                g.dma_start(out=out[7 * 128:8 * 128, 0:128],
                            in_=G_sb[:, :]).then_inc(s_dg, 16)
                g.dma_start(out=out[8 * 128:9 * 128, 0:128],
                            in_=ltok[:, :]).then_inc(s_dg, 16)
                g.dma_start(out=out[9 * 128:10 * 128, 0:8],
                            in_=baseb[:, :]).then_inc(s_dg, 16)
                g.wait_ge(s_dg, 160)


    if upto == "p5":
        return nc

    with ExitStack() as _es:
        dsb = _es.enter_context(nc.sbuf_tensor("dsb", [128, 2 * D], BF16))
        dispf = _es.enter_context(nc.sbuf_tensor("dispf", [128, 2 * CC * 512], BF16))
        h_sb = _es.enter_context(nc.sbuf_tensor("h_sb", [128, 32 * 512], BF16))
        yesb = _es.enter_context(nc.sbuf_tensor("yesb", [128, 2 * D], BF16))
        b1sb = _es.enter_context(nc.sbuf_tensor("b1sb", [32, 128], F32))
        b1T = _es.enter_context(nc.sbuf_tensor("b1T", [128, 32], F32))
        b2bc = _es.enter_context(nc.sbuf_tensor("b2bc", [128, D], F32))
        h_ps = _es.enter_context(nc.psum_tensor("h_ps", [128, 1024], F32))
        ye_ps = _es.enter_context(nc.psum_tensor("ye_ps", [128, 1024], F32))
        trdp = _es.enter_context(nc.psum_tensor("trdp", [128, 2048], BF16))
        s_c8 = nc.alloc_semaphore("p8_c")
        s_ds = [nc.alloc_semaphore("p8_ds0"), nc.alloc_semaphore("p8_ds1")]
        s_ye = [nc.alloc_semaphore("p8_ye0"), nc.alloc_semaphore("p8_ye1")]
        s_t8 = nc.alloc_semaphore("p8_t")
        s_a8 = nc.alloc_semaphore("p8_a")
        s_d8 = nc.alloc_semaphore("p8_d")
        s_ag8 = nc.alloc_semaphore("p8_ag")
        blk = _es.enter_context(nc.Block())

        pe_order = [("b1tr",)]
        for jg in range(4):
            for cc in range(CC):
                pe_order.append(("tr", jg, cc))
        for jw in range(4):
            for fs in range(32):
                pe_order.append(("w1", jw, fs))
            if jw + 1 < 4:
                for jg in range(4 * (jw + 1), 4 * (jw + 2)):
                    for cc in range(CC):
                        pe_order.append(("tr", jg, cc))
            for g8 in range(8):
                pe_order.append(("w2", jw, g8))
        t8ev = {k: i + 1 for i, k in enumerate(pe_order)}

        act_order = [("b1ev",)]
        for jw in range(4):
            if jw == 0:
                for jg in range(4):
                    for cc in range(CC):
                        act_order.append(("dp", jg, cc))
            for fs in range(32):
                act_order.append(("h", jw, fs))
            if jw + 1 < 4:
                for jg in range(4 * (jw + 1), 4 * (jw + 2)):
                    for cc in range(CC):
                        act_order.append(("dp", jg, cc))
        a8ev = {k: i + 1 for i, k in enumerate(act_order)}

        dve_order = []
        for jg in range(4):
            dve_order.append(("ms", jg))
        for jw in range(4):
            for jg in range(4 * (jw + 1), min(4 * (jw + 2), 16)):
                dve_order.append(("ms", jg))
            for g8 in range(8):
                dve_order.append(("ye", jw, g8))
        d8ev = {k: i + 1 for i, k in enumerate(dve_order)}

        @blk.tensor
        def _(t):
            for key in pe_order:
                if key[0] == "b1tr":
                    t.wait_ge(s_c8, 32)
                    t.transpose(trdp.bitcast(F32)[:, 0:32], b1sb[0:32, :],
                                identf[0:32, 0:32]).then_inc(s_t8, 1)
                elif key[0] == "tr":
                    _, jg, cc = key
                    k = jg * 8 + cc
                    if cc == 0:
                        t.wait_ge(s_ds[jg % 2], 16 * (jg // 2 + 1))
                    if k == 0:
                        t.wait_ge(s_a8, 1)
                    if k >= 2:
                        t.wait_ge(s_a8, a8ev[("dp", (k - 2) // 8, (k - 2) % 8)])
                    t.transpose(trdp[:, (k % 2) * 1024:(k % 2) * 1024 + 128],
                                dsb[:, (jg % 2) * D + cc * 128:(jg % 2) * D + (cc + 1) * 128],
                                identb[:, :]).then_inc(s_t8, 1)
                elif key[0] == "w1":
                    _, jw, fs = key
                    if fs == 0:
                        t.wait_ge(s_a8, a8ev[("dp", 4 * jw + 3, 7)])
                        if jw == 0:
                            t.wait_ge(s_wld, 256)
                    if fs >= 2:
                        t.wait_ge(s_a8, a8ev[("h", jw, fs - 2)])
                    elif jw >= 1:
                        t.wait_ge(s_a8, a8ev[("h", jw - 1, 30 + fs)])
                    for cc in range(CC):
                        mm = t.matmul(h_ps[:, (fs % 2) * 512:(fs % 2 + 1) * 512],
                                      w1sb[:, cc * FF + fs * 128: cc * FF + (fs + 1) * 128],
                                      dispf[:, (jw % 2) * 4096 + cc * 512:
                                            (jw % 2) * 4096 + (cc + 1) * 512],
                                      start=(cc == 0), stop=(cc == CC - 1))
                    mm.then_inc(s_t8, 1)
                else:
                    _, jw, g8 = key
                    if g8 == 0:
                        t.wait_ge(s_a8, a8ev[("h", jw, 31)])
                    if g8 >= 2:
                        t.wait_ge(s_d8, d8ev[("ye", jw, g8 - 2)])
                    elif jw >= 1:
                        t.wait_ge(s_d8, d8ev[("ye", jw - 1, 6 + g8)])
                    js, ow = g8 // 2, g8 % 2
                    for fs in range(32):
                        mm = t.matmul(ye_ps[:, (g8 % 2) * 512:(g8 % 2 + 1) * 512],
                                      h_sb[:, fs * 512 + js * 128: fs * 512 + (js + 1) * 128],
                                      w2sb[:, fs * D + ow * 512: fs * D + (ow + 1) * 512],
                                      start=(fs == 0), stop=(fs == 31))
                    mm.then_inc(s_t8, 1)

        @blk.scalar
        def _(sc):
            for key in act_order:
                if key[0] == "b1ev":
                    sc.wait_ge(s_t8, 1)
                    sc.copy(b1T[:, :], trdp.bitcast(F32)[:, 0:32]).then_inc(s_a8, 1)
                elif key[0] == "dp":
                    _, jg, cc = key
                    jw = jg // 4
                    g4 = jg % 4
                    k = jg * 8 + cc
                    sc.wait_ge(s_t8, t8ev[("tr", jg, cc)])
                    sc.copy(dispf[:, (jw % 2) * 4096 + cc * 512 + g4 * 128:
                                  (jw % 2) * 4096 + cc * 512 + (g4 + 1) * 128],
                            trdp[:, (k % 2) * 1024:(k % 2) * 1024 + 128]).then_inc(s_a8, 1)
                else:
                    _, jw, fs = key
                    sc.wait_ge(s_t8, t8ev[("w1", jw, fs)])
                    if fs == 0 and jw >= 1:
                        sc.wait_ge(s_t8, t8ev[("w2", jw - 1, 7)])
                    sc.activation(h_sb[:, fs * 512:(fs + 1) * 512],
                                  h_ps[:, (fs % 2) * 512:(fs % 2 + 1) * 512],
                                  AF.Identity, bias=b1T[:, fs:fs + 1]).then_inc(s_a8, 1)

        @blk.vector
        def _(v):
            ec = SelfChain(v, s_d8)
            v.wait_ge(s_c8, 32)
            for key in dve_order:
                if key[0] == "ms":
                    jg = key[1]
                    if jg >= 2:
                        v.wait_ge(s_t8, t8ev[("tr", jg - 2, 7)])
                    ec.op("memset", dsb[:, (jg % 2) * D:(jg % 2 + 1) * D], 0.0)
                else:
                    _, jw, g8 = key
                    js, ow = g8 // 2, g8 % 2
                    js16 = jw * 4 + js
                    v.wait_ge(s_t8, t8ev[("w2", jw, g8)])
                    if js16 >= 2 and ow == 0:
                        v.wait_ge(s_ye[js16 % 2], 16 * (js16 // 2))
                    ec.op("tensor_tensor",
                          yesb[:, (js % 2) * D + ow * 512:(js % 2) * D + (ow + 1) * 512],
                          ye_ps[:, (g8 % 2) * 512:(g8 % 2 + 1) * 512],
                          b2bc[:, ow * 512:(ow + 1) * 512], OP.add)

        @blk.sync
        def _(s):
            for jw in range(4):
                for js in range(4):
                    s.wait_ge(s_d8, d8ev[("ye", jw, 2 * js + 1)])
                    js16 = jw * 4 + js
                    s.dma_start(out=cc_yein[js16 * 128:(js16 + 1) * 128, :],
                                in_=yesb[:, (js % 2) * D:(js % 2 + 1) * D]
                                ).then_inc(s_ye[js16 % 2], 16)

        @blk.gpsimd
        def _(g):
            g.dma_start(out=b1sb[0:32, :], in_=b1_in[:, :]).then_inc(s_c8, 16)
            g.dma_start(out=b2bc[:, :],
                        in_=b2_in.ap().broadcast_to([128, D])).then_inc(s_c8, 16)
            g.wait_ge(s_d7, 558)
            g.wait_ge(s_ag6, 1)
            for jg in range(16):
                g.wait_ge(s_d8, d8ev[("ms", jg)])
                g.indirect_dma_start(
                    out=dsb[:, (jg % 2) * D:(jg % 2 + 1) * D], out_offset=None,
                    in_=cc_y1out[:, :],
                    in_offset=IndirectOffsetOnAxis(ap=gidx[:, jg:jg + 1], axis=0)
                    ).then_inc(s_ds[jg % 2], 16)
            g.wait_ge(s_ye[0], 128)
            g.wait_ge(s_ye[1], 128)
            g.collective_compute("AllGather", OP.bypass, replica_groups=RG,
                                 ins=[cc_yein.ap().opt()],
                                 outs=[cc_yeout.ap().opt()]).then_inc(s_ag8, 1)

    _es_w.close()
    _es_g.close()

    with ExitStack() as _es:
        ymg = _es.enter_context(nc.sbuf_tensor("ymg", [128, NT * D], BF16))
        ymf = _es.enter_context(nc.sbuf_tensor("ymf", [128, 2 * D], F32))
        scr9 = _es.enter_context(nc.sbuf_tensor("scr9", [128, D], F32))
        y1ld = _es.enter_context(nc.sbuf_tensor("y1ld", [128, 2 * D], F32))
        x2ld = _es.enter_context(nc.sbuf_tensor("x2ld", [128, 2 * D], F32))
        t2b = _es.enter_context(nc.sbuf_tensor("t2b", [128, 2 * D], F32))
        y2b = _es.enter_context(nc.sbuf_tensor("y2b", [128, 2 * D], F32))
        outb = _es.enter_context(nc.sbuf_tensor("outb", [128, 2 * D], F32))
        st9 = _es.enter_context(nc.sbuf_tensor("st9", [128, 12 * NT], F32))
        g2v = _es.enter_context(nc.sbuf_tensor("g2v", [128, D], F32))
        l2bv = _es.enter_context(nc.sbuf_tensor("l2bv", [128, D], F32))
        s_c9 = nc.alloc_semaphore("p9_c")
        s_ymg = nc.alloc_semaphore("p9_ym")
        s_yl = [nc.alloc_semaphore("p9_yl0"), nc.alloc_semaphore("p9_yl1")]
        s_xl9 = [nc.alloc_semaphore("p9_xl0"), nc.alloc_semaphore("p9_xl1")]
        s_os = [nc.alloc_semaphore("p9_os0"), nc.alloc_semaphore("p9_os1")]
        s_a9 = nc.alloc_semaphore("p9_a")
        s_d9 = nc.alloc_semaphore("p9_d")
        blk = _es.enter_context(nc.Block())
        RSD = 1.0 / 32.0

        def d9(tt, j):
            return 1 + 8 * tt + j

        def a9(tt, k):
            return 7 * tt + k

        @blk.vector
        def _(v):
            ec = SelfChain(v, s_d9)
            ec.op("memset", ymg[:, :], 0.0)
            v.wait_ge(s_c9, 32)
            for tt in range(NT):
                sb = st9[:, 12 * tt:12 * tt + 12]
                sl = (tt % 2) * D
                t2 = t2b[:, sl:sl + D]
                y2 = y2b[:, sl:sl + D]
                ot = outb[:, sl:sl + D]
                v.wait_ge(s_a9, a9(tt, 1))
                v.wait_ge(s_yl[tt % 2], 16 * (tt // 2 + 1))
                ec.op("scalar_tensor_tensor", t2, ymf[:, sl:sl + D],
                      gval[:, tt:tt + 1], y1ld[:, sl:sl + D], OP.mult, OP.add)
                v.wait_ge(s_a9, a9(tt, 2))
                ec.op("tensor_scalar", t2, t2, sb[:, 0:1], None, OP.subtract)
                v.wait_ge(s_a9, a9(tt, 4))
                ec.op("reciprocal", sb[:, 4:5], sb[:, 3:4])
                ec.op("scalar_tensor_tensor", y2, t2, sb[:, 4:5], g2v[:, :],
                      OP.mult, OP.mult)
                ec.op("tensor_tensor", y2, y2, l2bv[:, :], OP.add)
                v.wait_ge(s_a9, a9(tt, 5))
                ec.op("tensor_scalar", y2, y2, sb[:, 5:6], None, OP.subtract)
                v.wait_ge(s_a9, a9(tt, 7))
                ec.op("reciprocal", sb[:, 9:10], sb[:, 8:9])
                v.wait_ge(s_xl9[tt % 2], 16 * (tt // 2 + 1))
                if tt >= 2:
                    v.wait_ge(s_os[tt % 2], 16 * (tt // 2))
                ec.op("scalar_tensor_tensor", ot, y2, sb[:, 9:10],
                      x2ld[:, sl:sl + D], OP.mult, OP.add)
            assert ec.n == 1 + 8 * NT

        @blk.scalar
        def _(sc):
            sc.wait_ge(s_ymg, 256)
            sc.wait_ge(s_c9, 32)
            na = 0

            def aop(*a, **k):
                nonlocal na
                if na:
                    sc.wait_ge(s_a9, na)
                sc.activation(*a, **k).then_inc(s_a9, 1)
                na += 1
            for tt in range(NT):
                sb = st9[:, 12 * tt:12 * tt + 12]
                sl = (tt % 2) * D
                if tt >= 2:
                    sc.wait_ge(s_d9, d9(tt - 2, 1))
                aop(ymf[:, sl:sl + D], ymg[:, tt * D:(tt + 1) * D], AF.Identity,
                    bias=0.0)
                sc.wait_ge(s_d9, d9(tt, 1))
                aop(scr9[:, :], t2b[:, sl:sl + D], AF.Identity, bias=0.0,
                    scale=1.0 / D, accum_out=sb[:, 0:1])
                sc.wait_ge(s_d9, d9(tt, 2))
                aop(scr9[:, :], t2b[:, sl:sl + D], AF.Square, scale=RSD,
                    accum_out=sb[:, 1:2])
                aop(sb[:, 3:4], sb[:, 1:2], AF.Sqrt, bias=epsc[:, :])
                sc.wait_ge(s_d9, d9(tt, 5))
                aop(scr9[:, :], y2b[:, sl:sl + D], AF.Identity, bias=0.0,
                    scale=1.0 / D, accum_out=sb[:, 5:6])
                sc.wait_ge(s_d9, d9(tt, 6))
                aop(scr9[:, :], y2b[:, sl:sl + D], AF.Square, scale=RSD,
                    accum_out=sb[:, 6:7])
                aop(sb[:, 8:9], sb[:, 6:7], AF.Sqrt, bias=epsc[:, :])
            assert na == 7 * NT

        @blk.gpsimd
        def _(g):
            g.dma_start(out=g2v[:, :],
                        in_=ln2g_in.ap().broadcast_to([128, D])).then_inc(s_c9, 16)
            g.dma_start(out=l2bv[:, :],
                        in_=ln2b_in.ap().broadcast_to([128, D])).then_inc(s_c9, 16)
            g.wait_ge(s_ag8, 1)
            g.wait_ge(s_d9, 1)
            for tt in range(NT):
                g.indirect_dma_start(
                    out=ymg[:, tt * D:(tt + 1) * D], out_offset=None,
                    in_=cc_yeout[:, :],
                    in_offset=IndirectOffsetOnAxis(ap=dstc_i32[:, tt:tt + 1], axis=0)
                    ).then_inc(s_ymg, 16)

        @blk.sync
        def _(s):
            for tt in range(NT):
                if tt >= 2:
                    s.wait_ge(s_d9, d9(tt - 2, 1))
                s.dma_start(out=y1ld[:, (tt % 2) * D:(tt % 2 + 1) * D],
                            in_=sp_y1[tt * 128:(tt + 1) * 128, :]).then_inc(s_yl[tt % 2], 16)
                if tt >= 2:
                    s.wait_ge(s_d9, d9(tt - 2, 8))
                s.dma_start(out=x2ld[:, (tt % 2) * D:(tt % 2 + 1) * D],
                            in_=sp_x2[tt * 128:(tt + 1) * 128, :]).then_inc(s_xl9[tt % 2], 16)
                s.wait_ge(s_d9, d9(tt, 8))
                s.dma_start(out=out[tt * 128:(tt + 1) * 128, :],
                            in_=outb[:, (tt % 2) * D:(tt % 2 + 1) * D]).then_inc(s_os[tt % 2], 16)
            s.wait_ge(s_os[0], 128)
            s.wait_ge(s_os[1], 128)

    _es_r.close()
    return nc


_NC_CACHE = {}
UPTO = "full"


def kernel(**inputs):
    if UPTO not in _NC_CACHE:
        _NC_CACHE[UPTO] = build_nc(UPTO)
    nc = _NC_CACHE[UPTO]

    f32 = np.float32
    x = np.asarray(inputs["L0_attn_x"], f32)
    in_maps = []
    for c in range(NCORES):
        m = {
            "x_in": np.ascontiguousarray(x[c]),
            "wq": np.asarray(inputs["L0_attn_W_Q"], f32),
            "wk": np.asarray(inputs["L0_attn_W_K"], f32),
            "wv": np.asarray(inputs["L0_attn_W_V"], f32),
            "wkp": np.asarray(inputs["L0_attn_W_K_prime"], f32),
            "wo": np.asarray(inputs["L0_attn_W_O"], f32),
            "alpha": np.asarray(inputs["L0_attn_alpha"], f32).reshape(1, 1),
            "win": np.asarray(inputs["L0_ffn_in_proj_w"], f32),
            "binq": np.asarray(inputs["L0_ffn_in_proj_b"], f32).reshape(24, 128),
            "wout": np.asarray(inputs["L0_ffn_out_proj_w"], f32),
            "bout": np.asarray(inputs["L0_ffn_out_proj_b"], f32).reshape(1, D),
            "ln1g": np.asarray(inputs["L0_ffn_ln1_g"], f32).reshape(1, D),
            "ln1b": np.asarray(inputs["L0_ffn_ln1_b"], f32).reshape(1, D),
            "ln2g": np.asarray(inputs["L0_ffn_ln2_g"], f32).reshape(1, D),
            "ln2b": np.asarray(inputs["L0_ffn_ln2_b"], f32).reshape(1, D),
            "wg": np.asarray(inputs["L0_ffn_wg"], f32),
            "w1": np.ascontiguousarray(np.asarray(inputs["L0_ffn_w1"], f32)[c]),
            "b1": np.ascontiguousarray(np.asarray(inputs["L0_ffn_b1"], f32)[c]).reshape(32, 128),
            "w2": np.ascontiguousarray(np.asarray(inputs["L0_ffn_w2"], f32)[c]),
            "b2": np.ascontiguousarray(np.asarray(inputs["L0_ffn_b2"], f32)[c]).reshape(1, D),
            "prevmask": (np.arange(8) < c).astype(f32).reshape(8, 1),
            "sidxf": np.ascontiguousarray(
                np.arange(T, dtype=np.float32).reshape(NT, 128).T + (c * T + 1)),
            "myslot": np.ascontiguousarray(
                (np.arange(128, dtype=np.int32) + 129 * c).reshape(128, 1)),
        }
        in_maps.append(m)

    res = run_bass_kernel_spmd(nc, in_maps, core_ids=list(range(NCORES)))
    return np.stack([r["out"] for r in res.results], axis=0)


# version 24
# speedup vs baseline: 1.1143x; 1.1143x over previous
import numpy as np
import ml_dtypes
from contextlib import ExitStack
import concourse.bass as bass
import concourse.mybir as mybir
from concourse.bass_utils import run_bass_kernel_spmd
from concourse.bass_core_rust import IndirectOffsetOnAxis

F32 = mybir.dt.float32
BF16 = mybir.dt.bfloat16
I32 = mybir.dt.int32
AF = mybir.ActivationFunctionType
OP = mybir.AluOpType
AXL = mybir.AxisListType

T = 2048
D = 1024
NT = 16
CC = 8
PAIRS = 8
NITER = 1024
EXP_SHIFT = -20.0
EPS = 1e-5
NCORES = 8
FF = 4096
NE = 8
CAP = 2048
VP = 129 * 8
RG = [list(range(NCORES))]


class SelfChain:

    def __init__(self, eng, sem, n0=0):
        self.e = eng
        self.sem = sem
        self.n = n0

    def op(self, name, *a, **k):
        if self.n:
            self.e.wait_ge(self.sem, self.n)
        inst = getattr(self.e, name)(*a, **k)
        inst.then_inc(self.sem, 1)
        self.n += 1
        return inst


def build_nc(upto="full", detect_races=True, serial=False):
    nc = bass.Bass("TRN2", target_bir_lowering=False,
                   detect_race_conditions=detect_races)

    x_in = nc.dram_tensor("x_in", [T, D], F32, kind="ExternalInput")
    wq_in = nc.dram_tensor("wq", [D, D], F32, kind="ExternalInput")
    wk_in = nc.dram_tensor("wk", [D, D], F32, kind="ExternalInput")
    wv_in = nc.dram_tensor("wv", [D, D], F32, kind="ExternalInput")
    wkp_in = nc.dram_tensor("wkp", [D, D], F32, kind="ExternalInput")
    wo_in = nc.dram_tensor("wo", [D, D], F32, kind="ExternalInput")
    alpha_in = nc.dram_tensor("alpha", [1, 1], F32, kind="ExternalInput")
    win_in = nc.dram_tensor("win", [3 * D, D], F32, kind="ExternalInput")
    binq_in = nc.dram_tensor("binq", [24, 128], F32, kind="ExternalInput")
    wout_in = nc.dram_tensor("wout", [D, D], F32, kind="ExternalInput")
    bout_in = nc.dram_tensor("bout", [1, D], F32, kind="ExternalInput")
    ln1g_in = nc.dram_tensor("ln1g", [1, D], F32, kind="ExternalInput")
    ln1b_in = nc.dram_tensor("ln1b", [1, D], F32, kind="ExternalInput")
    ln2g_in = nc.dram_tensor("ln2g", [1, D], F32, kind="ExternalInput")
    ln2b_in = nc.dram_tensor("ln2b", [1, D], F32, kind="ExternalInput")
    wg_in = nc.dram_tensor("wg", [D, NE], F32, kind="ExternalInput")
    w1_in = nc.dram_tensor("w1", [D, FF], F32, kind="ExternalInput")
    b1_in = nc.dram_tensor("b1", [32, 128], F32, kind="ExternalInput")
    w2_in = nc.dram_tensor("w2", [FF, D], F32, kind="ExternalInput")
    b2_in = nc.dram_tensor("b2", [1, D], F32, kind="ExternalInput")
    prevmask_in = nc.dram_tensor("prevmask", [8, 1], F32, kind="ExternalInput")
    sidxf_in = nc.dram_tensor("sidxf", [128, NT], F32, kind="ExternalInput")
    myslot_in = nc.dram_tensor("myslot", [128, 1], I32, kind="ExternalInput")

    out = nc.dram_tensor("out", [T, D], F32, kind="ExternalOutput")

    sp_x2 = nc.dram_tensor("sp_x2", [T, D], F32, kind="Internal")
    sp_x2n = nc.dram_tensor("sp_x2n", [T, D], F32, kind="Internal")
    sp_x2nh = nc.dram_tensor("sp_x2nh", [128, NT * D], BF16, kind="Internal")
    sp_y1 = nc.dram_tensor("sp_y1", [T, D], F32, kind="Internal")
    sp_y1h = nc.dram_tensor("sp_y1h", [128, NT * D], BF16, kind="Internal")
    cc_kin = nc.dram_tensor("cc_kin", [D, T], BF16, kind="Internal")
    cc_kout = nc.dram_tensor("cc_kout", [8 * D, T], BF16, kind="Internal")
    cc_vin = nc.dram_tensor("cc_vin", [T, D], BF16, kind="Internal")
    cc_vout = nc.dram_tensor("cc_vout", [8 * T, D], BF16, kind="Internal")
    cc_y1in = nc.dram_tensor("cc_y1in", [T, D], BF16, kind="Internal")
    cc_y1out = nc.dram_tensor("cc_y1out", [8 * T, D], BF16, kind="Internal")
    cc_cntin = nc.dram_tensor("cc_cntin", [1, NE], F32, kind="Internal")
    cc_cntout = nc.dram_tensor("cc_cntout", [8, NE], F32, kind="Internal")
    cc_invin = nc.dram_tensor("cc_invin", [16512, 1], F32, kind="Internal")
    cc_invout = nc.dram_tensor("cc_invout", [16512, 1], F32, kind="Internal")
    cc_iall = nc.dram_tensor("cc_iall", [8 * 16512, 1], F32, kind="Internal")
    cc_invw_t = [nc.dram_tensor(f"cc_invwt{tt}", [16512, 1], F32, kind="Internal")
                 for tt in range(NT)]
    w1bf = nc.dram_tensor("w1bf", [D, FF], BF16, kind="Internal")
    w2bf = nc.dram_tensor("w2bf", [FF, D], BF16, kind="Internal")
    cc_yein = nc.dram_tensor("cc_yein", [CAP, D], BF16, kind="Internal")
    cc_yeout = nc.dram_tensor("cc_yeout", [8 * CAP, D], BF16, kind="Internal")

    ident_bf = nc.inline_tensor(np.eye(128, dtype=ml_dtypes.bfloat16), "ident_bf")
    ones_bf = nc.inline_tensor(np.ones((128, 64), dtype=ml_dtypes.bfloat16), "ones_bf")
    stair_np = (np.arange(4096)[None, :] >= (2048 + np.arange(128))[:, None])
    stair = nc.inline_tensor(stair_np.astype(ml_dtypes.bfloat16), "stair")
    erow = nc.inline_tensor(np.arange(8, dtype=np.float32).reshape(1, 8), "erow")
    ident_f32 = nc.inline_tensor(np.eye(128, dtype=np.float32), "ident_f32")
    eps_col = nc.inline_tensor(np.full((128, 1), EPS, dtype=np.float32), "eps_col")
    selc_np = np.zeros((128, 16), dtype=ml_dtypes.bfloat16)
    for v_ in range(4):
        selc_np[:, v_ * 4 + v_] = 1.0
    selc = nc.inline_tensor(selc_np, "selc")
    basesc = nc.dram_tensor("basesc", [1, 8], F32, kind="Internal")

    identb = nc.alloc_sbuf_tensor("identb", [128, 128], BF16)
    identf = nc.alloc_sbuf_tensor("identf", [128, 128], F32)
    onescol = nc.alloc_sbuf_tensor("onescol", [128, 64], BF16)
    alpha_bc = nc.alloc_sbuf_tensor("alpha_bc", [128, 1], F32)
    oma_bc = nc.alloc_sbuf_tensor("oma_bc", [128, 1], F32)
    shiftc = nc.alloc_sbuf_tensor("shiftc", [128, 1], F32)
    epsc = nc.alloc_sbuf_tensor("epsc", [128, 1], F32)

    _es_x = ExitStack()
    xhat_ocomb = _es_x.enter_context(nc.sbuf_tensor("xhat_ocomb", [128, PAIRS * T], BF16))
    _es_a = ExitStack()
    q_feat = _es_a.enter_context(nc.sbuf_tensor("q_feat", [128, PAIRS * T], BF16))
    k_feat = _es_a.enter_context(nc.sbuf_tensor("k_feat", [128, PAIRS * T], BF16))
    kp_feat = _es_a.enter_context(nc.sbuf_tensor("kp_feat", [128, PAIRS * T], BF16))
    vpack = _es_a.enter_context(nc.sbuf_tensor("vpack", [128, NT * VP], BF16))

    xhat = xhat_ocomb

    with ExitStack() as _es:
        xload = _es.enter_context(nc.sbuf_tensor("xload", [128, 2 * D], F32))
        sqt = _es.enter_context(nc.sbuf_tensor("sqt", [128, D], F32))
        xhat_t = _es.enter_context(nc.sbuf_tensor("xhat_t", [128, D], BF16))
        st0 = _es.enter_context(nc.sbuf_tensor("st0", [128, 8], F32))
        trps = _es.enter_context(nc.psum_tensor("trps", [128, 2048], BF16))
        s_c0 = nc.alloc_semaphore("ph0_c")
        sxt0 = nc.alloc_semaphore("ph0_xt0")
        sxt1 = nc.alloc_semaphore("ph0_xt1")
        s_d0 = nc.alloc_semaphore("ph0_d")
        s_a0 = nc.alloc_semaphore("ph0_a")
        s_t0 = nc.alloc_semaphore("ph0_t")
        blk = _es.enter_context(nc.Block())

        def a0_sqrt(i):
            return 9 * i + 1

        def a0_copy(k):
            return 9 * (k // 8) + 2 + (k % 8)

        @blk.tensor
        def _(t):
            for i in range(NT):
                t.wait_ge(s_d0, 8 * i + 10)
                for cc in range(CC):
                    k = i * CC + cc
                    if k >= 2:
                        t.wait_ge(s_a0, a0_copy(k - 2))
                    t.transpose(trps[:, (k % 2) * 1024:(k % 2) * 1024 + 128],
                                xhat_t[:, cc * 128:(cc + 1) * 128],
                                identb[:, :]).then_inc(s_t0, 1)

        @blk.scalar
        def _(sc):
            for i in range(NT):
                sc.wait_ge(s_d0, 8 * i + 8)
                sc.activation(st0[:, 4:5], st0[:, 3:4], AF.Sqrt).then_inc(s_a0, 1)
                for cc in range(CC):
                    k = i * CC + cc
                    sc.wait_ge(s_t0, k + 1)
                    sc.copy(xhat[:, cc * T + i * 128: cc * T + (i + 1) * 128],
                            trps[:, (k % 2) * 1024:(k % 2) * 1024 + 128]).then_inc(s_a0, 1)

        @blk.vector
        def _(v):
            sxt = [sxt0, sxt1]
            ec = SelfChain(v, s_d0)
            v.wait_ge(s_c0, 80)
            ec.op("memset", shiftc[:, :], EXP_SHIFT)
            ec.op("tensor_scalar", oma_bc[:, :], alpha_bc[:, :], -1.0, 1.0, OP.mult, OP.add)
            for i in range(NT):
                xt = xload[:, (i % 2) * D:(i % 2 + 1) * D]
                v.wait_ge(sxt[i % 2], 16 * (i // 2 + 1))
                ec.op("tensor_reduce", st0[:, 0:1], xt, AXL.X, OP.add)
                ec.op("tensor_scalar", st0[:, 1:2], st0[:, 0:1], 1.0 / D, None, OP.mult)
                ec.op("tensor_scalar", xt, xt, st0[:, 1:2], None, OP.subtract)
                ec.op("tensor_tensor", sqt[:, :], xt, xt, OP.mult)
                ec.op("tensor_reduce", st0[:, 2:3], sqt[:, :], AXL.X, OP.add)
                ec.op("tensor_scalar", st0[:, 3:4], st0[:, 2:3], 1.0 / D, EPS,
                      OP.mult, OP.add)
                v.wait_ge(s_a0, a0_sqrt(i))
                ec.op("reciprocal", st0[:, 5:6], st0[:, 4:5])
                if i >= 1:
                    v.wait_ge(s_t0, 8 * i)
                ec.op("tensor_scalar", xhat_t[:, :], xt, st0[:, 5:6], None,
                      OP.mult)
            assert ec.n == 2 + 8 * NT

        @blk.sync
        def _(s):
            s.dma_start(out=identb[:, :], in_=ident_bf[:, :]).then_inc(s_c0, 16)
            s.dma_start(out=onescol[:, :], in_=ones_bf[:, :]).then_inc(s_c0, 16)
            s.dma_start(out=identf[:, :], in_=ident_f32[:, :]).then_inc(s_c0, 16)
            s.dma_start(out=alpha_bc[:, :],
                        in_=alpha_in.ap().broadcast_to([128, 1])).then_inc(s_c0, 16)
            s.dma_start(out=epsc[:, :], in_=eps_col[:, :]).then_inc(s_c0, 16)
            sxt = [sxt0, sxt1]
            for i in range(NT):
                if i >= 2:
                    s.wait_ge(s_d0, 8 * (i - 2) + 10)
                s.dma_start(out=xload[:, (i % 2) * D:(i % 2 + 1) * D],
                            in_=x_in[i * 128:(i + 1) * 128, :]).then_inc(sxt[i % 2], 16)

    def dump_sbuf_bf16(src_t, ncols):
        ov = out.ap().rearrange("(a b) d -> a (b d)", a=128)
        with ExitStack() as es2:
            sl = nc.alloc_semaphore("dump_l")
            blk2 = es2.enter_context(nc.Block())

            @blk2.gpsimd
            def _(g):
                for j in range(4):
                    w = ncols // 4
                    g.dma_start(out=ov[:, j * w:(j + 1) * w],
                                in_=src_t[:, j * w:(j + 1) * w]).then_inc(sl, 16)
                g.wait_ge(sl, 64)

    if upto == "ph0":
        dump_sbuf_bf16(xhat, CC * T)
        return nc

    with ExitStack() as _es:
        wrow = _es.enter_context(nc.sbuf_tensor("wrow", [128, 2 * D], BF16))
        wts = _es.enter_context(nc.sbuf_tensor("wts", [128, 2 * D], BF16))
        wvt = _es.enter_context(nc.sbuf_tensor("wvt", [128, CC * D], BF16))
        trp1 = _es.enter_context(nc.psum_tensor("trp1", [128, 4096], BF16))
        pp1 = _es.enter_context(nc.psum_tensor("pp1", [128, 512], F32))
        pp2 = _es.enter_context(nc.psum_tensor("pp2", [128, 512], F32))
        s_wcast = nc.alloc_semaphore("ph1_wcast")
        s_wl0 = nc.alloc_semaphore("ph1_wl0")
        s_wl1 = nc.alloc_semaphore("ph1_wl1")
        s_t1 = nc.alloc_semaphore("ph1_t")
        s_a1 = nc.alloc_semaphore("ph1_a")
        s_d1 = nc.alloc_semaphore("ph1_d")
        blk = _es.enter_context(nc.Block())

        loads = [("v", rc) for rc in range(CC)] + \
                [(w, pair) for pair in range(PAIRS) for w in ("q", "k", "kp")]
        wd_map = {"v": wv_in, "q": wq_in, "k": wk_in, "kp": wkp_in}
        dest_map = {"q": q_feat, "k": k_feat, "kp": kp_feat}

        def t1_tr(li, cc):
            if li < 8:
                return li * 8 + cc + 1
            return 64 + (li - 8) * 12 + cc + 1

        def t1_proj(npm):
            if npm < 96:
                li = 8 + npm // 4
                win = npm % 4
                return 64 + (li - 8) * 12 + 8 + win + 1
            return 64 + 288 + (npm - 96) + 1

        @blk.tensor
        def _(t):
            ntr = 0
            npm = 0
            for li, (w, rc) in enumerate(loads):
                t.wait_ge([s_wl0, s_wl1][li % 2], 16 * (li // 2 + 1))
                for cc in range(CC):
                    if ntr >= 4:
                        t.wait_ge(s_a1, ntr - 3)
                    t.transpose(trp1[:, (ntr % 4) * 1024:(ntr % 4) * 1024 + 128],
                                wrow[:, (li % 2) * D + cc * 128:(li % 2) * D + (cc + 1) * 128],
                                identb[:, :]).then_inc(s_t1, 1)
                    ntr += 1
                if w == "v":
                    continue
                t.wait_ge(s_a1, ntr)
                for win in range(4):
                    pb = [pp1, pp2][npm % 2]
                    if npm >= 2:
                        t.wait_ge(s_d1, npm - 1)
                    for cc in range(CC):
                        mm = t.matmul(pb[:, :],
                                      wts[:, (li % 2) * D + cc * 128:(li % 2) * D + (cc + 1) * 128],
                                      xhat[:, cc * T + win * 512:cc * T + (win + 1) * 512],
                                      start=(cc == 0), stop=(cc == CC - 1))
                    mm.then_inc(s_t1, 1)
                    npm += 1
            t.wait_ge(s_a1, 64)
            for tt in range(NT):
                for win in range(2):
                    pb = [pp1, pp2][npm % 2]
                    if npm >= 2:
                        t.wait_ge(s_d1, npm - 1)
                    for cc in range(CC):
                        mm = t.matmul(pb[:, :],
                                      xhat[:, cc * T + tt * 128:cc * T + (tt + 1) * 128],
                                      wvt[:, cc * D + win * 512:cc * D + (win + 1) * 512],
                                      start=(cc == 0), stop=(cc == CC - 1))
                    mm.then_inc(s_t1, 1)
                    npm += 1

        @blk.scalar
        def _(sc):
            ntr = 0
            for li, (w, rc) in enumerate(loads):
                for cc in range(CC):
                    sc.wait_ge(s_t1, t1_tr(li, cc))
                    if w == "v":
                        dst = wvt[:, cc * D + rc * 128: cc * D + rc * 128 + 128]
                    else:
                        dst = wts[:, (li % 2) * D + cc * 128:(li % 2) * D + (cc + 1) * 128]
                    sc.copy(dst, trp1[:, (ntr % 4) * 1024:(ntr % 4) * 1024 + 128]).then_inc(s_a1, 1)
                    ntr += 1

        @blk.vector
        def _(v):
            vp4 = vpack.ap().rearrange("p (t pr c) -> p t pr c", t=NT, pr=8, c=129)
            v.memset(vp4[:, :, :, 64:65], 1.0)
            npm = 0
            for li, (w, rc) in enumerate(loads):
                if w == "v":
                    continue
                pair = rc
                dest = dest_map[w]
                for win in range(4):
                    pb = [pp1, pp2][npm % 2]
                    v.wait_ge(s_t1, t1_proj(npm))
                    if w == "q":
                        v.tensor_scalar(dest[:, pair * T + win * 512: pair * T + (win + 1) * 512],
                                        pb[:, :], 0.125, None, OP.mult).then_inc(s_d1, 1)
                    else:
                        v.tensor_copy(dest[:, pair * T + win * 512: pair * T + (win + 1) * 512],
                                      pb[:, :]).then_inc(s_d1, 1)
                    npm += 1
            for tt in range(NT):
                for win in range(2):
                    pb = [pp1, pp2][npm % 2]
                    v.wait_ge(s_t1, t1_proj(npm))
                    for pw in range(4):
                        pair = win * 4 + pw
                        base = tt * VP + pair * 129
                        v.tensor_copy(vpack[:, base: base + 64], pb[:, pw * 128: pw * 128 + 64])
                        inst = v.tensor_copy(vpack[:, base + 65: base + 129],
                                             pb[:, pw * 128 + 64: pw * 128 + 128])
                    inst.then_inc(s_d1, 1)
                    npm += 1

        @blk.gpsimd
        def _(g):
            for li, (w, rc) in enumerate(loads):
                if li >= 2:
                    g.wait_ge(s_t1, t1_tr(li - 2, 7))
                g.dma_start(out=wrow[:, (li % 2) * D:(li % 2 + 1) * D],
                            in_=wd_map[w][rc * 128:(rc + 1) * 128, :]).then_inc(
                                [s_wl0, s_wl1][li % 2], 16)
            g.dma_start(out=w1bf[:, :], in_=w1_in[:, :]).then_inc(s_wcast, 16)
            g.dma_start(out=w2bf[:, :], in_=w2_in[:, :]).then_inc(s_wcast, 16)

    if upto == "ph1":
        dump_sbuf_bf16(q_feat, PAIRS * T)
        return nc

    o_comb = xhat_ocomb

    with ExitStack() as _es:
        pbuf = _es.enter_context(nc.sbuf_tensor("pbuf", [128, 3 * 1024], BF16))
        osb = _es.enter_context(nc.sbuf_tensor("osb", [128, 1024], F32))
        recrow = _es.enter_context(nc.sbuf_tensor("recrow", [128, 1024], F32))
        recrow_bf = _es.enter_context(nc.sbuf_tensor("recrow_bf", [128, 1024], BF16))
        sstd = _es.enter_context(nc.sbuf_tensor("sstd", [128, 4 * 512], F32))
        S0 = _es.enter_context(nc.psum_tensor("S0", [128, 1024], F32))
        S1 = _es.enter_context(nc.psum_tensor("S1", [128, 1024], F32))
        oeA = _es.enter_context(nc.psum_tensor("oeA", [65, 512], F32))
        ooA = _es.enter_context(nc.psum_tensor("ooA", [128, 512], F32))
        oeB = _es.enter_context(nc.psum_tensor("oeB", [65, 512], F32))
        ooB = _es.enter_context(nc.psum_tensor("ooB", [128, 512], F32))
        s_t2 = nc.alloc_semaphore("p2_t")
        s_a2 = nc.alloc_semaphore("p2_a")
        s_d2 = nc.alloc_semaphore("p2_d")
        blk = _es.enter_context(nc.Block())

        Sbufs = [S0, S1]
        oe = [oeA, oeB]
        oo = [ooA, ooB]

        def iter_idx(g):
            return g >> 7, (g >> 6) & 1, (g >> 4) & 3, g & 15

        pe_ev = {"s": {}, "o": {}, "rep": {}}

        def d2_start(gr):
            n = 0
            for g2 in range(gr):
                n += 8 if ((g2 >> 2) & 1) == 0 else 12
            return n

        def d2_od(gr, head):
            return d2_start(gr) + 3 * (head + 1)

        def d2_on(gr, head):
            br = (gr >> 2) & 1
            if br == 0:
                return d2_start(gr) + 7 + head
            return d2_start(gr) + 9 + 3 * head

        def d2_end(gr):
            return d2_start(gr + 1)

        @blk.tensor
        def _(t):
            nev = 0

            def emit_oside(go):
                nonlocal nev
                pair2, br2, qt2, kt2 = iter_idx(go)
                gr2 = go >> 4
                st = gr2 % 2
                t.wait_ge(s_a2, go + 1)
                if kt2 == 0 and gr2 >= 2:
                    t.wait_ge(s_d2, d2_end(gr2 - 2))
                vb = vpack[:, kt2 * VP + pair2 * 129: kt2 * VP + pair2 * 129 + 65]
                t.matmul(oe[st][:, :], vb, pbuf[:, (go % 3) * 1024:(go % 3) * 1024 + 512],
                         start=(kt2 == 0), stop=(kt2 == 15))
                vo = vpack[:, kt2 * VP + pair2 * 129 + 65: kt2 * VP + pair2 * 129 + 129]
                t.matmul(oo[st][64:128, :], vo,
                         pbuf[:, (go % 3) * 1024 + 512:(go % 3) * 1024 + 1024],
                         start=(kt2 == 0), stop=(kt2 == 15), tile_position=(0, 64))
                nev += 1
                t.matmul(oo[st][0:1, :], onescol[:, 0:1],
                         pbuf[:, (go % 3) * 1024 + 512:(go % 3) * 1024 + 1024],
                         start=(kt2 == 0), stop=(kt2 == 15)).then_inc(s_t2, 1)
                pe_ev["o"][go] = nev
                if kt2 == 2 and gr2 >= 1:
                    gp = gr2 - 1
                    t.wait_ge(s_d2, d2_od(gp, 0))
                    nev += 1
                    t.matmul(oe[gp % 2][0:64, :], onescol[64:65, :],
                             recrow_bf[64:65, 0:512], start=True, stop=True,
                             tile_position=(64, 0)).then_inc(s_t2, 1)
                    pe_ev["rep"][2 * gp] = nev
                if kt2 == 4 and gr2 >= 1:
                    gp = gr2 - 1
                    t.wait_ge(s_d2, d2_od(gp, 1))
                    nev += 1
                    t.matmul(oo[gp % 2][64:128, :], onescol[0:1, :],
                             recrow_bf[0:1, 512:1024], start=True, stop=True,
                             tile_position=(0, 64)).then_inc(s_t2, 1)
                    pe_ev["rep"][2 * gp + 1] = nev

            for g in range(NITER):
                pair, br, qt, kt = iter_idx(g)
                kc = k_feat if br == 0 else kp_feat
                if g >= 2:
                    t.wait_ge(s_a2, g - 1)
                sb = Sbufs[g % 2]
                t.matmul(sb[:, 0:512],
                         kc[0:64, pair * T + kt * 128: pair * T + (kt + 1) * 128],
                         q_feat[0:64, pair * T + qt * 512: pair * T + (qt + 1) * 512],
                         start=True, stop=True, tile_position=(0, 0))
                nev += 1
                t.matmul(sb[:, 512:1024],
                         kc[64:128, pair * T + kt * 128: pair * T + (kt + 1) * 128],
                         q_feat[64:128, pair * T + qt * 512: pair * T + (qt + 1) * 512],
                         start=True, stop=True, tile_position=(64, 0)).then_inc(s_t2, 1)
                pe_ev["s"][g] = nev
                if g >= 1:
                    emit_oside(g - 1)
            emit_oside(NITER - 1)
            for gp in (63,):
                t.wait_ge(s_d2, d2_od(gp, 0))
                nev += 1
                t.matmul(oe[gp % 2][0:64, :], onescol[64:65, :], recrow_bf[64:65, 0:512],
                         start=True, stop=True, tile_position=(64, 0)).then_inc(s_t2, 1)
                pe_ev["rep"][2 * gp] = nev
                t.wait_ge(s_d2, d2_od(gp, 1))
                nev += 1
                t.matmul(oo[gp % 2][64:128, :], onescol[0:1, :], recrow_bf[0:1, 512:1024],
                         start=True, stop=True, tile_position=(0, 64)).then_inc(s_t2, 1)
                pe_ev["rep"][2 * gp + 1] = nev

        @blk.scalar
        def _(sc):
            for g in range(NITER):
                sc.wait_ge(s_t2, pe_ev["s"][g])
                if g >= 3:
                    sc.wait_ge(s_t2, pe_ev["o"][g - 3])
                sc.activation(pbuf[:, (g % 3) * 1024:(g % 3) * 1024 + 1024],
                              Sbufs[g % 2][:, :], AF.Exp,
                              bias=shiftc[:, :]).then_inc(s_a2, 1)

        @blk.vector
        def _(v):
            ec = SelfChain(v, s_d2)
            for gr in range(64):
                pair, br, qt = gr >> 3, (gr >> 2) & 1, gr & 3
                st = gr % 2
                v.wait_ge(s_t2, pe_ev["o"][16 * gr + 15])
                ec.op("reciprocal", recrow[64:65, 0:512], oe[st][64:65, :])
                ec.op("tensor_copy", recrow_bf[64:65, 0:512], recrow[64:65, 0:512])
                ec.op("tensor_copy", osb[0:64, 0:512], oe[st][0:64, :])
                assert ec.n == d2_od(gr, 0)
                ec.op("reciprocal", recrow[0:1, 512:1024], oo[st][0:1, :])
                ec.op("tensor_copy", recrow_bf[0:1, 512:1024], recrow[0:1, 512:1024])
                ec.op("tensor_copy", osb[64:128, 512:1024], oo[st][64:128, :])
                assert ec.n == d2_od(gr, 1)
                v.wait_ge(s_t2, pe_ev["rep"][2 * gr])
                if br == 0:
                    ec.op("tensor_tensor", sstd[0:64, qt * 512:(qt + 1) * 512],
                          osb[0:64, 0:512], oe[st][0:64, :], OP.mult)
                else:
                    ec.op("tensor_tensor", osb[0:64, 0:512], osb[0:64, 0:512], oe[st][0:64, :], OP.mult)
                    ec.op("tensor_scalar", osb[0:64, 0:512], osb[0:64, 0:512], oma_bc[0:64, :], None, OP.mult)
                    ec.op("scalar_tensor_tensor",
                          o_comb[0:64, pair * T + qt * 512: pair * T + (qt + 1) * 512],
                          sstd[0:64, qt * 512:(qt + 1) * 512], alpha_bc[0:64, :],
                          osb[0:64, 0:512], OP.mult, OP.add)
                assert ec.n == d2_on(gr, 0)
                v.wait_ge(s_t2, pe_ev["rep"][2 * gr + 1])
                if br == 0:
                    ec.op("tensor_tensor", sstd[64:128, qt * 512:(qt + 1) * 512],
                          osb[64:128, 512:1024], oo[st][64:128, :], OP.mult)
                else:
                    ec.op("tensor_tensor", osb[64:128, 512:1024], osb[64:128, 512:1024],
                          oo[st][64:128, :], OP.mult)
                    ec.op("tensor_scalar", osb[64:128, 512:1024], osb[64:128, 512:1024],
                          oma_bc[64:128, :], None, OP.mult)
                    ec.op("scalar_tensor_tensor",
                          o_comb[64:128, pair * T + qt * 512: pair * T + (qt + 1) * 512],
                          sstd[64:128, qt * 512:(qt + 1) * 512], alpha_bc[64:128, :],
                          osb[64:128, 512:1024], OP.mult, OP.add)
                assert ec.n == d2_on(gr, 1) == d2_end(gr)

    if upto == "ph2":
        dump_sbuf_bf16(o_comb, PAIRS * T)
        _es_a.close()
        return nc

    _es_a.close()

    with ExitStack() as _es:
        wrow3 = _es.enter_context(nc.sbuf_tensor("wrow3", [128, 2 * D], BF16))
        wot = _es.enter_context(nc.sbuf_tensor("wot", [128, CC * D], BF16))
        xld3 = _es.enter_context(nc.sbuf_tensor("xld3", [128, 2 * D], F32))
        att3 = _es.enter_context(nc.sbuf_tensor("att3", [128, 2 * D], F32))
        scr3 = _es.enter_context(nc.sbuf_tensor("scr3", [128, D], F32))
        x1t = _es.enter_context(nc.sbuf_tensor("x1t", [128, 2 * D], F32))
        x2nt = _es.enter_context(nc.sbuf_tensor("x2nt", [128, 2 * D], F32))
        x2nb = _es.enter_context(nc.sbuf_tensor("x2nb", [128, D], BF16))
        x2nhs = _es.enter_context(nc.sbuf_tensor("x2nhs", [128, 2 * D], BF16))
        st3 = _es.enter_context(nc.sbuf_tensor("st3", [128, 8 * NT], F32))
        trp3 = _es.enter_context(nc.psum_tensor("trp3", [128, 4096], BF16))
        wop0 = _es.enter_context(nc.psum_tensor("wop0", [128, 512], F32))
        wop1 = _es.enter_context(nc.psum_tensor("wop1", [128, 512], F32))
        wop2 = _es.enter_context(nc.psum_tensor("wop2", [128, 512], F32))
        wop3 = _es.enter_context(nc.psum_tensor("wop3", [128, 512], F32))
        s_wl3a = nc.alloc_semaphore("p3_wla")
        s_wl3b = nc.alloc_semaphore("p3_wlb")
        s_sp3a = nc.alloc_semaphore("p3_spa")
        s_sp3b = nc.alloc_semaphore("p3_spb")
        s_xl3a = nc.alloc_semaphore("p3_xla")
        s_xl3b = nc.alloc_semaphore("p3_xlb")
        s_st3a = nc.alloc_semaphore("p3_sta")
        s_st3b = nc.alloc_semaphore("p3_stb")
        s_t3 = nc.alloc_semaphore("p3_t")
        s_a3 = nc.alloc_semaphore("p3_a")
        s_d3 = nc.alloc_semaphore("p3_d")
        blk = _es.enter_context(nc.Block())
        RSD = 1.0 / 32.0

        wops = [wop0, wop1, wop2, wop3]

        act_order = []
        for tt in range(NT):
            if tt >= 2:
                for cc in range(CC):
                    act_order.append(("ev", tt - 2, cc))
            for nm in ("acc1", "sqacc1", "sqrt1", "acc2", "sqacc2", "sqrt2"):
                act_order.append((nm, tt))
        for j in (NT - 2, NT - 1):
            for cc in range(CC):
                act_order.append(("ev", j, cc))
        a3ev = {k: 64 + i + 1 for i, k in enumerate(act_order)}

        def t3_proj(tt, win):
            n = 64
            for t2 in range(NT):
                n += 1
                if t2 == tt and win == 0:
                    return n
                n += 1
                if t2 == tt and win == 1:
                    return n
                if t2 >= 2:
                    n += 8
            raise AssertionError

        def t3_trx(j, cc):
            n = 64
            for t2 in range(NT):
                n += 2
                if t2 >= 2:
                    if t2 - 2 == j:
                        return n + cc + 1
                    n += 8
            for jj in (NT - 2, NT - 1):
                if jj == j:
                    return n + cc + 1
                n += 8
            raise AssertionError

        @blk.tensor
        def _(t):
            ntr = 0
            for rc in range(CC):
                t.wait_ge([s_wl3a, s_wl3b][rc % 2], 16 * (rc // 2 + 1))
                for cc in range(CC):
                    if ntr >= 4:
                        t.wait_ge(s_a3, ntr - 3)
                    t.transpose(trp3[:, (ntr % 4) * 1024:(ntr % 4) * 1024 + 128],
                                wrow3[:, (rc % 2) * D + cc * 128:(rc % 2) * D + (cc + 1) * 128],
                                identb[:, :]).then_inc(s_t3, 1)
                    ntr += 1

            def emit_tb(j):
                t.wait_ge(s_d3, 9 * j + 9)
                for cc in range(CC):
                    k = j * CC + cc
                    if k < 4:
                        t.wait_ge(s_a3, 64)
                    else:
                        kp = k - 4
                        t.wait_ge(s_a3, a3ev[("ev", kp // 8, kp % 8)])
                    t.transpose(trp3[:, (k % 4) * 1024:(k % 4) * 1024 + 128],
                                x2nb[:, cc * 128:(cc + 1) * 128],
                                identb[:, :]).then_inc(s_t3, 1)

            t.wait_ge(s_a3, 64)
            for tt in range(NT):
                for win in range(2):
                    pb = wops[(tt * 2 + win) % 4]
                    if tt >= 2:
                        t.wait_ge(s_d3, 9 * (tt - 2) + 2)
                    for cc in range(CC):
                        mm = t.matmul(pb[:, :],
                                      o_comb[:, cc * T + tt * 128:cc * T + (tt + 1) * 128],
                                      wot[:, cc * D + win * 512:cc * D + (win + 1) * 512],
                                      start=(cc == 0), stop=(cc == CC - 1))
                    mm.then_inc(s_t3, 1)
                if tt >= 2:
                    emit_tb(tt - 2)
            emit_tb(NT - 2)
            emit_tb(NT - 1)

        @blk.scalar
        def _(sc):
            na = 0

            def aop(wait_t=None, wait_d=None, wait_sem=None, copy=None, act=None):
                nonlocal na
                if wait_t is not None:
                    sc.wait_ge(s_t3, wait_t)
                if wait_d is not None:
                    sc.wait_ge(s_d3, wait_d)
                if wait_sem is not None:
                    sc.wait_ge(*wait_sem)
                if na:
                    sc.wait_ge(s_a3, na)
                inst = sc.copy(*copy) if copy else sc.activation(*act[0], **act[1])
                inst.then_inc(s_a3, 1)
                na += 1
            ntr = 0
            for rc in range(CC):
                for cc in range(CC):
                    aop(wait_t=ntr + 1,
                        copy=(wot[:, cc * D + rc * 128: cc * D + rc * 128 + 128],
                              trp3[:, (ntr % 4) * 1024:(ntr % 4) * 1024 + 128]))
                    ntr += 1
            for key in act_order:
                if key[0] == "ev":
                    _, j, cc = key
                    k = j * CC + cc
                    aop(wait_t=t3_trx(j, cc),
                        wait_sem=(([s_sp3a, s_sp3b][j % 2], 16 * (j // 2))
                                  if j >= 2 else None),
                        copy=(x2nhs[:, (j % 2) * D + cc * 128:(j % 2) * D + (cc + 1) * 128],
                              trp3[:, (k % 4) * 1024:(k % 4) * 1024 + 128]))
                    continue
                nm, tt = key
                sb = st3[:, 8 * tt:8 * tt + 8]
                at = att3[:, (tt % 2) * D:(tt % 2 + 1) * D]
                x1 = x1t[:, (tt % 2) * D:(tt % 2 + 1) * D]
                x2n = x2nt[:, (tt % 2) * D:(tt % 2 + 1) * D]
                if nm == "acc1":
                    aop(wait_d=9 * tt + 2,
                        act=((scr3[:, :], at, AF.Identity),
                             dict(bias=0.0, scale=1.0 / D, accum_out=sb[:, 0:1])))
                elif nm == "sqacc1":
                    aop(wait_d=9 * tt + 3,
                        act=((scr3[:, :], at, AF.Square),
                             dict(scale=RSD, accum_out=sb[:, 1:2])))
                elif nm == "sqrt1":
                    aop(act=((sb[:, 3:4], sb[:, 1:2], AF.Sqrt),
                             dict(bias=epsc[:, :])))
                elif nm == "acc2":
                    aop(wait_d=9 * tt + 5,
                        act=((scr3[:, :], x1, AF.Identity),
                             dict(bias=0.0, scale=1.0 / D, accum_out=sb[:, 5:6])))
                elif nm == "sqacc2":
                    aop(wait_d=9 * tt + 6,
                        act=((scr3[:, :], x2n, AF.Square),
                             dict(scale=RSD, accum_out=sb[:, 6:7])))
                else:
                    aop(act=((sb[:, 2:3], sb[:, 6:7], AF.Sqrt),
                             dict(bias=epsc[:, :])))
            assert na == 64 + len(act_order)

        @blk.vector
        def _(v):
            ec = SelfChain(v, s_d3)
            for tt in range(NT):
                sb = st3[:, 8 * tt:8 * tt + 8]
                at = att3[:, (tt % 2) * D:(tt % 2 + 1) * D]
                xt = xld3[:, (tt % 2) * D:(tt % 2 + 1) * D]
                x1 = x1t[:, (tt % 2) * D:(tt % 2 + 1) * D]
                x2n = x2nt[:, (tt % 2) * D:(tt % 2 + 1) * D]
                v.wait_ge(s_t3, t3_proj(tt, 0))
                if tt >= 2:
                    v.wait_ge([s_st3a, s_st3b][tt % 2], 32 * (tt // 2))
                    v.wait_ge(s_a3, a3ev[("sqacc1", tt - 2)])
                ec.op("tensor_copy", at[:, 0:512], wops[(2 * tt) % 4][:, :])
                v.wait_ge(s_t3, t3_proj(tt, 1))
                ec.op("tensor_copy", at[:, 512:1024], wops[(2 * tt + 1) % 4][:, :])
                v.wait_ge(s_a3, a3ev[("acc1", tt)])
                ec.op("tensor_scalar", at, at, sb[:, 0:1], None, OP.subtract)
                v.wait_ge(s_a3, a3ev[("sqrt1", tt)])
                ec.op("reciprocal", sb[:, 4:5], sb[:, 3:4])
                v.wait_ge([s_xl3a, s_xl3b][tt % 2], 16 * (tt // 2 + 1))
                ec.op("scalar_tensor_tensor", x1, at, sb[:, 4:5], xt, OP.mult, OP.add)
                v.wait_ge(s_a3, a3ev[("acc2", tt)])
                ec.op("tensor_scalar", x2n, x1, sb[:, 5:6], None, OP.subtract)
                v.wait_ge(s_a3, a3ev[("sqrt2", tt)])
                ec.op("reciprocal", sb[:, 7:8], sb[:, 2:3])
                ec.op("tensor_scalar", x2n, x2n, sb[:, 7:8], None, OP.mult)
                if tt >= 1:
                    v.wait_ge(s_t3, t3_trx(tt - 1, 7))
                ec.op("tensor_copy", x2nb[:, :], x2n)
                assert ec.n == 9 * tt + 9

        @blk.sync
        def _(s):
            for tt in range(NT):
                if tt >= 2:
                    s.wait_ge(s_d3, 9 * (tt - 2) + 5)
                s.dma_start(out=xld3[:, (tt % 2) * D:(tt % 2 + 1) * D],
                            in_=x_in[tt * 128:(tt + 1) * 128, :]).then_inc(
                                [s_xl3a, s_xl3b][tt % 2], 16)
                s.wait_ge(s_d3, 9 * tt + 5)
                s.dma_start(out=sp_x2[tt * 128:(tt + 1) * 128, :],
                            in_=x1t[:, (tt % 2) * D:(tt % 2 + 1) * D]).then_inc(
                                [s_st3a, s_st3b][tt % 2], 16)
                s.wait_ge(s_d3, 9 * tt + 9)
                s.dma_start(out=sp_x2n[tt * 128:(tt + 1) * 128, :],
                            in_=x2nt[:, (tt % 2) * D:(tt % 2 + 1) * D]).then_inc(
                                [s_st3a, s_st3b][tt % 2], 16)
            s.wait_ge(s_st3a, 32 * 8)
            s.wait_ge(s_st3b, 32 * 8)

        @blk.gpsimd
        def _(g):
            for rc in range(CC):
                if rc >= 2:
                    g.wait_ge(s_t3, 8 * (rc - 1))
                g.dma_start(out=wrow3[:, (rc % 2) * D:(rc % 2 + 1) * D],
                            in_=wo_in[rc * 128:(rc + 1) * 128, :]).then_inc(
                                [s_wl3a, s_wl3b][rc % 2], 16)
            for tt in range(NT):
                g.wait_ge(s_a3, a3ev[("ev", tt, 7)])
                g.dma_start(out=sp_x2nh[:, tt * D:(tt + 1) * D],
                            in_=x2nhs[:, (tt % 2) * D:(tt % 2 + 1) * D]).then_inc(
                                [s_sp3a, s_sp3b][tt % 2], 16)
            g.wait_ge(s_sp3a, 16 * 8)
            g.wait_ge(s_sp3b, 16 * 8)

    _es_x.close()

    if upto == "p3":
        s_l = nc.alloc_semaphore("fx_l")
        s_st = nc.alloc_semaphore("fx_s")
        with (
            nc.sbuf_tensor("ot", [128, D], F32) as ot,
            nc.Block() as blk,
        ):
            @blk.gpsimd
            def _(g):
                for tt in range(NT):
                    g.dma_start(out=ot[:, :], in_=sp_x2[tt * 128:(tt + 1) * 128, :]).then_inc(s_l, 16)
                    g.wait_ge(s_l, 16 * (tt + 1))
                    g.dma_start(out=out[tt * 128:(tt + 1) * 128, :], in_=ot[:, :]).then_inc(s_st, 16)
                    g.wait_ge(s_st, 16 * (tt + 1))
        return nc


    _es_r = ExitStack()
    dst_i32 = _es_r.enter_context(nc.sbuf_tensor("dst_i32", [128, NT], I32))
    gval = _es_r.enter_context(nc.sbuf_tensor("gval", [128, NT], F32))
    dstc_i32 = _es_r.enter_context(nc.sbuf_tensor("dstc_i32", [128, NT], I32))
    _es_g = ExitStack()
    G_sb = _es_g.enter_context(nc.sbuf_tensor("G_sb", [128, NT * 8], F32))
    _es_o = ExitStack()
    o_tok = _es_o.enter_context(nc.sbuf_tensor("o_tok", [128, NT * D], F32))
    _es_q = ExitStack()
    qf2 = _es_q.enter_context(nc.sbuf_tensor("qf2", [128, CC * T], BF16))
    s_ag4 = nc.alloc_semaphore("p4_ag")

    with ExitStack() as _es:
        wrow4 = _es.enter_context(nc.sbuf_tensor("wrow4", [128, 2 * D], BF16))
        wint = _es.enter_context(nc.sbuf_tensor("wint", [128, CC * 3072], BF16))
        x2h = _es.enter_context(nc.sbuf_tensor("x2h", [128, CC * T], BF16))
        kstg = _es.enter_context(nc.sbuf_tensor("kstg", [128, T], BF16))
        vstg = _es.enter_context(nc.sbuf_tensor("vstg", [128, D], BF16))
        binq_sb = _es.enter_context(nc.sbuf_tensor("binq_sb", [24, 128], F32))
        binqT = _es.enter_context(nc.sbuf_tensor("binqT", [128, 24], F32))
        bvbc = _es.enter_context(nc.sbuf_tensor("bvbc", [128, D], F32))
        trp4 = _es.enter_context(nc.psum_tensor("trp4", [128, 4096], BF16))
        pq1 = _es.enter_context(nc.psum_tensor("pq1", [128, 512], F32))
        pq2 = _es.enter_context(nc.psum_tensor("pq2", [128, 512], F32))
        s_wl4 = [nc.alloc_semaphore("p4_wla"), nc.alloc_semaphore("p4_wlb")]
        s_x4 = nc.alloc_semaphore("p4_x")
        s_t4 = nc.alloc_semaphore("p4_t")
        s_a4 = nc.alloc_semaphore("p4_a")
        s_d4 = nc.alloc_semaphore("p4_d")
        s_ks = nc.alloc_semaphore("p4_ks")
        s_vs = nc.alloc_semaphore("p4_vs")
        blk = _es.enter_context(nc.Block())

        @blk.tensor
        def _(t):
            ntr = 0
            for rc in range(24):
                t.wait_ge(s_wl4[rc % 2], 16 * (rc // 2 + 1))
                for cc in range(CC):
                    if ntr >= 4:
                        t.wait_ge(s_a4, ntr - 3)
                    t.transpose(trp4[:, (ntr % 4) * 1024:(ntr % 4) * 1024 + 128],
                                wrow4[:, (rc % 2) * D + cc * 128:(rc % 2) * D + (cc + 1) * 128],
                                identb[:, :]).then_inc(s_t4, 1)
                    ntr += 1
            t.wait_ge(s_x4, 160)
            t.wait_ge(s_a4, 192)
            t.transpose(trp4.bitcast(F32)[:, 1536:1560], binq_sb[0:24, :],
                        identf[0:24, 0:24]).then_inc(s_t4, 1)
            npm = 0
            for kind in ("k", "q"):
                for dj in range(8):
                    for win in range(4):
                        pb = [pq1, pq2][npm % 2]
                        if npm >= 2:
                            t.wait_ge(s_d4, npm - 1)
                        joff = (0 if kind == "q" else 1024) + dj * 128
                        for cc in range(CC):
                            mm = t.matmul(pb[:, :],
                                          wint[:, cc * 3072 + joff: cc * 3072 + joff + 128],
                                          x2h[:, cc * T + win * 512: cc * T + (win + 1) * 512],
                                          start=(cc == 0), stop=(cc == CC - 1))
                        mm.then_inc(s_t4, 1)
                        npm += 1
            for tt in range(NT):
                for win in range(2):
                    pb = [pq1, pq2][npm % 2]
                    if npm >= 2:
                        t.wait_ge(s_d4, npm - 1)
                    for cc in range(CC):
                        mm = t.matmul(pb[:, :],
                                      x2h[:, cc * T + tt * 128: cc * T + (tt + 1) * 128],
                                      wint[:, cc * 3072 + 2048 + win * 512:
                                           cc * 3072 + 2048 + (win + 1) * 512],
                                      start=(cc == 0), stop=(cc == CC - 1))
                    mm.then_inc(s_t4, 1)
                    npm += 1

        @blk.scalar
        def _(sc):
            ntr = 0
            for rc in range(24):
                for cc in range(CC):
                    sc.wait_ge(s_t4, ntr + 1)
                    sc.copy(wint[:, cc * 3072 + rc * 128: cc * 3072 + rc * 128 + 128],
                            trp4[:, (ntr % 4) * 1024:(ntr % 4) * 1024 + 128]).then_inc(s_a4, 1)
                    ntr += 1
            sc.wait_ge(s_t4, 193)
            sc.copy(binqT[:, :], trp4.bitcast(F32)[:, 1536:1560]).then_inc(s_a4, 1)

        @blk.vector
        def _(v):
            ec = SelfChain(v, s_d4)
            v.wait_ge(s_a4, 193)
            npm = 0
            for kind in ("k", "q"):
                for dj in range(8):
                    for win in range(4):
                        pb = [pq1, pq2][npm % 2]
                        v.wait_ge(s_t4, 193 + npm + 1)
                        if kind == "q":
                            ec.op("tensor_scalar",
                                  qf2[:, dj * T + win * 512: dj * T + (win + 1) * 512],
                                  pb[:, :], binqT[:, dj:dj + 1], 1.0 / 16.0,
                                  OP.add, OP.mult)
                        else:
                            if win == 0 and dj >= 1:
                                v.wait_ge(s_ks, 16 * dj)
                            ec.op("tensor_scalar",
                                  kstg[:, win * 512:(win + 1) * 512],
                                  pb[:, :], binqT[:, 8 + dj:9 + dj], None, OP.add)
                        npm += 1
            for tt in range(NT):
                for win in range(2):
                    pb = [pq1, pq2][npm % 2]
                    v.wait_ge(s_t4, 193 + npm + 1)
                    if win == 0 and tt >= 1:
                        v.wait_ge(s_vs, 16 * tt)
                    ec.op("tensor_tensor",
                          vstg[:, win * 512:(win + 1) * 512],
                          pb[:, :], bvbc[:, win * 512:(win + 1) * 512], OP.add)
                    npm += 1
            assert ec.n == 96

        @blk.sync
        def _(s):
            for cc in range(CC):
                ap_in = bass.AP(sp_x2nh, cc * 128, [[NT * D, 128], [D, NT], [1, 128]])
                s.dma_start(out=x2h.ap().rearrange("p (c t j) -> p c t j", c=CC, t=NT, j=128)[:, cc],
                            in_=ap_in).then_inc(s_x4, 16)
            s.dma_start(out=binq_sb[:, :], in_=binq_in[:, :]).then_inc(s_x4, 16)
            s.dma_start(out=bvbc[:, :],
                        in_=bass.AP(binq_in, 2048, [[0, 128], [1, D]])).then_inc(s_x4, 16)
            for dj in range(8):
                s.wait_ge(s_d4, 4 * dj + 4)
                s.dma_start(out=cc_kin[dj * 128:(dj + 1) * 128, :],
                            in_=kstg[:, :]).then_inc(s_ks, 16)
            for tt in range(NT):
                s.wait_ge(s_d4, 66 + 2 * tt)
                s.dma_start(out=cc_vin[tt * 128:(tt + 1) * 128, :],
                            in_=vstg[:, :]).then_inc(s_vs, 16)

        @blk.gpsimd
        def _(g):
            for rc in range(24):
                if rc >= 2:
                    g.wait_ge(s_t4, 8 * (rc - 1))
                g.dma_start(out=wrow4[:, (rc % 2) * D:(rc % 2 + 1) * D],
                            in_=win_in[rc * 128:(rc + 1) * 128, :]).then_inc(s_wl4[rc % 2], 16)
            g.wait_ge(s_ks, 128)
            g.collective_compute("AllGather", OP.bypass, replica_groups=RG,
                                 ins=[cc_kin.ap().opt()],
                                 outs=[cc_kout.ap().opt()]).then_inc(s_ag4, 1)
            g.wait_ge(s_vs, 256)
            g.collective_compute("AllGather", OP.bypass, replica_groups=RG,
                                 ins=[cc_vin.ap().opt()],
                                 outs=[cc_vout.ap().opt()]).then_inc(s_ag4, 1)

    _es_s = ExitStack()
    selcb = _es_s.enter_context(nc.sbuf_tensor("selcb", [128, 16], BF16))
    stok = _es_s.enter_context(nc.sbuf_tensor("stok", [128, NT * 32], F32))
    etok = _es_s.enter_context(nc.sbuf_tensor("etok", [128, NT * 32], F32))
    vmb = _es_s.enter_context(nc.sbuf_tensor("vmb", [128, 3 * D], F32))
    s_t5 = nc.alloc_semaphore("p5_t")
    s_a5 = nc.alloc_semaphore("p5_a")
    s_d5 = nc.alloc_semaphore("p5_d")
    s_vm = [nc.alloc_semaphore(f"p5_vm{i}") for i in range(3)]

    with ExitStack() as _es:
        kmb = _es.enter_context(nc.sbuf_tensor("kmb", [128, 2 * T], BF16))
        prodb = _es.enter_context(nc.sbuf_tensor("prodb", [128, 2 * T], BF16))
        ssb = _es.enter_context(nc.sbuf_tensor("ssb", [4, 2 * T], F32))
        smax = _es.enter_context(nc.sbuf_tensor("smax", [128, 12], F32))
        Sps = _es.enter_context(nc.psum_tensor("Sps", [4, 2048], F32))
        trp5 = _es.enter_context(nc.psum_tensor("trp5", [128, 1024], F32))
        s_c5 = nc.alloc_semaphore("p5_c")
        s_km = [nc.alloc_semaphore("p5_km0"), nc.alloc_semaphore("p5_km1")]
        blk = _es.enter_context(nc.Block())

        def t5_sc(m, cc):
            return 24 * m + cc + 1

        def t5_tr(m, tt):
            return 24 * m + 8 + tt + 1

        def a5_ssb(m):
            return 17 * m + 1

        def a5_st(m, tt):
            return 17 * m + 2 + tt

        def a5_exp(tt):
            return 137 + tt

        def d5_prod(m, cc):
            return 8 * m + cc + 1

        def d5_sub(tt):
            return 64 + 16 * tt + 16

        def stok_sl(tt, m):
            return bass.AP(stok, tt * 32 + m, [[NT * 32, 128], [8, 4]])

        def etok_sl(tt, m):
            return bass.AP(etok, tt * 32 + m, [[NT * 32, 128], [8, 4]])

        @blk.tensor
        def _(t):
            t.wait_ge(s_c5, 16)
            for m in range(8):
                for cc in range(CC):
                    t.wait_ge(s_d5, d5_prod(m, cc))
                    if cc == 0 and m >= 1:
                        t.wait_ge(s_a5, a5_ssb(m - 1))
                    for win in range(4):
                        mm = t.matmul(Sps[0:4, win * 512:(win + 1) * 512],
                                      selcb[:, (cc // 2) * 4:(cc // 2) * 4 + 4],
                                      prodb[:, (cc % 2) * T + win * 512:
                                            (cc % 2) * T + (win + 1) * 512],
                                      start=(cc == 0), stop=(cc == CC - 1))
                    mm.then_inc(s_t5, 1)
                t.wait_ge(s_a5, a5_ssb(m))
                for tt in range(NT):
                    k = 16 * m + tt
                    if k >= 2:
                        t.wait_ge(s_a5, a5_st((k - 2) // 16, (k - 2) % 16))
                    t.transpose(trp5[:, (k % 2) * 512:(k % 2) * 512 + 4],
                                ssb[0:4, (m % 2) * T + tt * 128:(m % 2) * T + (tt + 1) * 128],
                                identf[0:4, 0:4]).then_inc(s_t5, 1)

        @blk.scalar
        def _(sc):
            for m in range(8):
                sc.wait_ge(s_t5, t5_sc(m, 7))
                if m >= 2:
                    sc.wait_ge(s_t5, t5_tr(m - 2, 15))
                sc.copy(ssb[0:4, (m % 2) * T:(m % 2 + 1) * T], Sps[0:4, :]).then_inc(s_a5, 1)
                for tt in range(NT):
                    sc.wait_ge(s_t5, t5_tr(m, tt))
                    k = 16 * m + tt
                    sc.copy(stok_sl(tt, m),
                            trp5[:, (k % 2) * 512:(k % 2) * 512 + 4]).then_inc(s_a5, 1)
            for tt in range(NT):
                sc.wait_ge(s_d5, d5_sub(tt))
                sc.activation(etok[:, tt * 32:(tt + 1) * 32],
                              stok[:, tt * 32:(tt + 1) * 32], AF.Exp).then_inc(s_a5, 1)

        @blk.vector
        def _(v):
            ec = SelfChain(v, s_d5)
            for m in range(8):
                for cc in range(CC):
                    j = 8 * m + cc
                    v.wait_ge(s_km[j % 2], 16 * (j // 2 + 1))
                    if j >= 2:
                        v.wait_ge(s_t5, t5_sc((j - 2) // 8, (j - 2) % 8))
                    ec.op("tensor_tensor",
                          prodb[:, (cc % 2) * T:(cc % 2 + 1) * T],
                          qf2[:, cc * T:(cc + 1) * T],
                          kmb[:, (j % 2) * T:(j % 2 + 1) * T], OP.mult)
            assert ec.n == 64
            v.wait_ge(s_a5, 136)
            for tt in range(NT):
                ec.op("tensor_copy", smax[:, 0:4], stok_sl(tt, 0))
                for m in range(1, 8):
                    ec.op("tensor_tensor", smax[:, 0:4], smax[:, 0:4],
                          stok_sl(tt, m), OP.max)
                for m in range(8):
                    ec.op("tensor_tensor", stok_sl(tt, m), stok_sl(tt, m),
                          smax[:, 0:4], OP.subtract)
            assert ec.n == 320
            for tt in range(NT):
                v.wait_ge(s_a5, a5_exp(tt))
                ec.op("tensor_copy", smax[:, 4:8], etok_sl(tt, 0))
                for m in range(1, 8):
                    ec.op("tensor_tensor", smax[:, 4:8], smax[:, 4:8],
                          etok_sl(tt, m), OP.add)
                ec.op("reciprocal", smax[:, 8:12], smax[:, 4:8])
                for m in range(8):
                    ec.op("tensor_tensor", etok_sl(tt, m), etok_sl(tt, m),
                          smax[:, 8:12], OP.mult)
            assert ec.n == 592

        @blk.sync
        def _(s):
            s.dma_start(out=selcb[:, :], in_=selc[:, :]).then_inc(s_c5, 16)
            s.wait_ge(s_ag4, 1)
            for m in range(8):
                for cc in range(CC):
                    j = 8 * m + cc
                    if j >= 2:
                        s.wait_ge(s_d5, j - 1)
                    s.dma_start(out=kmb[:, (j % 2) * T:(j % 2 + 1) * T],
                                in_=cc_kout[m * D + cc * 128: m * D + (cc + 1) * 128, :]
                                ).then_inc(s_km[j % 2], 16)

    s_ag6 = nc.alloc_semaphore("p6_ag")

    with ExitStack() as _es:
        wrow6 = _es.enter_context(nc.sbuf_tensor("wrow6", [128, 2 * D], BF16))
        wouT = _es.enter_context(nc.sbuf_tensor("wouT", [128, CC * D], BF16))
        ob16 = _es.enter_context(nc.sbuf_tensor("ob16", [128, 2 * D], BF16))
        ofs = _es.enter_context(nc.sbuf_tensor("ofs", [128, 2 * D], BF16))
        x2nld = _es.enter_context(nc.sbuf_tensor("x2nld", [128, 2 * D], F32))
        att6 = _es.enter_context(nc.sbuf_tensor("att6", [128, D], F32))
        scr6 = _es.enter_context(nc.sbuf_tensor("scr6", [128, D], F32))
        y1t = _es.enter_context(nc.sbuf_tensor("y1t", [128, 2 * D], F32))
        y1b16 = _es.enter_context(nc.sbuf_tensor("y1b16", [128, 2 * D], BF16))
        y1fs = _es.enter_context(nc.sbuf_tensor("y1fs", [128, 2 * D], F32))
        st6 = _es.enter_context(nc.sbuf_tensor("st6", [128, 8 * NT], F32))
        boutbc = _es.enter_context(nc.sbuf_tensor("boutbc", [128, D], F32))
        g1v = _es.enter_context(nc.sbuf_tensor("g1v", [128, D], F32))
        b1v = _es.enter_context(nc.sbuf_tensor("b1v", [128, D], F32))
        wgb = _es.enter_context(nc.sbuf_tensor("wgb", [128, 64], F32))
        trp6 = _es.enter_context(nc.psum_tensor("trp6", [128, 4096], BF16))
        wopA = _es.enter_context(nc.psum_tensor("wopA", [128, 512], F32))
        wopB = _es.enter_context(nc.psum_tensor("wopB", [128, 512], F32))
        Gps = _es.enter_context(nc.psum_tensor("Gps", [128, 1024], F32))
        s_wl6 = [nc.alloc_semaphore("p6_wla"), nc.alloc_semaphore("p6_wlb")]
        s_xl6 = [nc.alloc_semaphore("p6_xla"), nc.alloc_semaphore("p6_xlb")]
        s_y1s = [nc.alloc_semaphore("p6_y1sa"), nc.alloc_semaphore("p6_y1sb")]
        s_y1c = [nc.alloc_semaphore("p6_y1ca"), nc.alloc_semaphore("p6_y1cb")]
        s_cg = nc.alloc_semaphore("p6_cg")
        s_t6 = nc.alloc_semaphore("p6_t")
        s_a6 = nc.alloc_semaphore("p6_a")
        blk = _es.enter_context(nc.Block())
        RSD = 1.0 / 32.0

        dve_order = []
        for tt in range(NT):
            for m in range(8):
                for h in range(4):
                    dve_order.append(("cmb", tt, 8 * m + h * 2))
        dve_order = []
        for tt in range(NT):
            for m in range(8):
                for h in range(4):
                    dve_order.append(("cmb", tt, m, h))
            if tt >= 2:
                for j in range(1, 9):
                    dve_order.append(("ln", tt - 2, j))
        for jt in (NT - 2, NT - 1):
            for j in range(1, 9):
                dve_order.append(("ln", jt, j))
        d5ev = {k: 592 + i + 1 for i, k in enumerate(dve_order)}

        def cmb_end(tt):
            return d5ev[("cmb", tt, 7, 3)]

        def t6_tr(tt, cc):
            return 64 + 19 * tt + cc + 1

        def t6_pr(tt, win):
            return 64 + 19 * tt + 9 + win

        def t6_ftr(tt, cc):
            return 64 + 19 * tt + 11 + cc

        def t6_g(tt):
            return 64 + 19 * tt + 19

        def a6(tt, k):
            return 64 + 21 * tt + k

        @blk.vector
        def _(v):
            ec = SelfChain(v, s_d5, n0=592)
            v.wait_ge(s_cg, 64)
            for key in dve_order:
                if key[0] == "cmb":
                    _, tt, m, h = key
                    q = tt * 8 + m
                    if h == 0:
                        v.wait_ge(s_vm[q % 3], 16 * (q // 3 + 1))
                    col = etok[:, tt * 32 + h * 8 + m: tt * 32 + h * 8 + m + 1]
                    osl = o_tok[:, tt * D + h * 256: tt * D + (h + 1) * 256]
                    vsl = vmb[:, (q % 3) * D + h * 256:(q % 3) * D + (h + 1) * 256]
                    if m == 0:
                        ec.op("tensor_scalar", osl, vsl, col, None, OP.mult)
                    else:
                        ec.op("scalar_tensor_tensor", osl, vsl, col, osl,
                              OP.mult, OP.add)
                    continue
                _, tt, j = key
                sb = st6[:, 8 * tt:8 * tt + 8]
                at = att6[:, :]
                xl = x2nld[:, (tt % 2) * D:(tt % 2 + 1) * D]
                y1 = y1t[:, (tt % 2) * D:(tt % 2 + 1) * D]
                if j == 1:
                    v.wait_ge(s_t6, t6_pr(tt, 0))
                    v.wait_ge(s_xl6[tt % 2], 16 * (tt // 2 + 1))
                    if tt >= 1:
                        v.wait_ge(s_a6, a6(tt - 1, 11))
                    ec.op("tensor_tensor", at[:, 0:512], wopA[:, :], xl[:, 0:512], OP.add)
                elif j == 2:
                    v.wait_ge(s_t6, t6_pr(tt, 1))
                    ec.op("tensor_tensor", at[:, 512:1024], wopB[:, :], xl[:, 512:1024], OP.add)
                elif j == 3:
                    ec.op("tensor_tensor", at, at, boutbc[:, :], OP.add)
                elif j == 4:
                    v.wait_ge(s_a6, a6(tt, 10))
                    ec.op("tensor_scalar", at, at, sb[:, 0:1], None, OP.subtract)
                elif j == 5:
                    v.wait_ge(s_a6, a6(tt, 12))
                    ec.op("reciprocal", sb[:, 4:5], sb[:, 3:4])
                elif j == 6:
                    if tt >= 2:
                        v.wait_ge(s_t6, t6_ftr(tt - 2, 7))
                        v.wait_ge(s_y1s[tt % 2], 16 * (tt // 2))
                    ec.op("scalar_tensor_tensor", y1, at, sb[:, 4:5], g1v[:, :],
                          OP.mult, OP.mult)
                elif j == 7:
                    ec.op("tensor_tensor", y1, y1, b1v[:, :], OP.add)
                else:
                    if tt >= 2:
                        v.wait_ge(s_y1c[tt % 2], 16 * (tt // 2))
                    ec.op("tensor_copy", y1b16[:, (tt % 2) * D:(tt % 2 + 1) * D], y1)
            assert ec.n == 592 + len(dve_order)

        @blk.tensor
        def _(t):
            ntr = 0
            for rc in range(CC):
                t.wait_ge(s_wl6[rc % 2], 16 * (rc // 2 + 1))
                for cc in range(CC):
                    if ntr >= 4:
                        t.wait_ge(s_a6, ntr - 3)
                    t.transpose(trp6[:, (ntr % 4) * 1024:(ntr % 4) * 1024 + 128],
                                wrow6[:, (rc % 2) * D + cc * 128:(rc % 2) * D + (cc + 1) * 128],
                                identb[:, :]).then_inc(s_t6, 1)
                    ntr += 1
            t.wait_ge(s_a6, 64)
            t.wait_ge(s_cg, 64)
            for tt in range(NT):
                t.wait_ge(s_a6, a6(tt, 1))
                for cc in range(CC):
                    k = 8 * tt + cc
                    if k >= 2:
                        t.wait_ge(s_a6, a6((k - 2) // 8, 2 + (k - 2) % 8))
                    t.transpose(trp6[:, (k % 2) * 1024:(k % 2) * 1024 + 128],
                                ob16[:, (tt % 2) * D + cc * 128:(tt % 2) * D + (cc + 1) * 128],
                                identb[:, :]).then_inc(s_t6, 1)
                t.wait_ge(s_a6, a6(tt, 9))
                for win in range(2):
                    if tt >= 1:
                        t.wait_ge(s_d5, d5ev[("ln", tt - 1, win + 1)])
                    pb = [wopA, wopB][win]
                    for cc in range(CC):
                        mm = t.matmul(pb[:, :],
                                      ofs[:, (tt % 2) * D + cc * 128:(tt % 2) * D + (cc + 1) * 128],
                                      wouT[:, cc * D + win * 512: cc * D + (win + 1) * 512],
                                      start=(cc == 0), stop=(cc == CC - 1))
                    mm.then_inc(s_t6, 1)
                t.wait_ge(s_d5, d5ev[("ln", tt, 7)])
                for cc in range(CC):
                    k = 8 * tt + cc
                    if k >= 2:
                        t.wait_ge(s_a6, a6((k - 2) // 8, 13 + (k - 2) % 8))
                    t.transpose(trp6.bitcast(F32)[:, 1024 + (k % 2) * 512:
                                                  1024 + (k % 2) * 512 + 128],
                                y1t[:, (tt % 2) * D + cc * 128:(tt % 2) * D + (cc + 1) * 128],
                                identf[:, :]).then_inc(s_t6, 1)
                t.wait_ge(s_a6, a6(tt, 20))
                if tt >= 2:
                    t.wait_ge(s_a6, a6(tt - 2, 21))
                for cc in range(CC):
                    mm = t.matmul(Gps[:, (tt % 2) * 512:(tt % 2) * 512 + 8],
                                  y1fs[:, (tt % 2) * D + cc * 128:(tt % 2) * D + (cc + 1) * 128],
                                  wgb[:, cc * 8:(cc + 1) * 8],
                                  start=(cc == 0), stop=(cc == CC - 1))
                mm.then_inc(s_t6, 1)

        @blk.scalar
        def _(sc):
            na = 0

            def aop(wait_t=None, wait_d=None, copy=None, act=None):
                nonlocal na
                if wait_t is not None:
                    sc.wait_ge(s_t6, wait_t)
                if wait_d is not None:
                    sc.wait_ge(s_d5, wait_d)
                if na:
                    sc.wait_ge(s_a6, na)
                inst = sc.copy(*copy) if copy else sc.activation(*act[0], **act[1])
                inst.then_inc(s_a6, 1)
                na += 1
            ntr = 0
            for rc in range(CC):
                for cc in range(CC):
                    aop(wait_t=ntr + 1,
                        copy=(wouT[:, cc * D + rc * 128: cc * D + rc * 128 + 128],
                              trp6[:, (ntr % 4) * 1024:(ntr % 4) * 1024 + 128]))
                    ntr += 1
            for tt in range(NT):
                sb = st6[:, 8 * tt:8 * tt + 8]
                aop(wait_t=(t6_tr(tt - 2, 7) if tt >= 2 else None),
                    wait_d=cmb_end(tt),
                    copy=(ob16[:, (tt % 2) * D:(tt % 2 + 1) * D],
                          o_tok[:, tt * D:(tt + 1) * D]))
                for cc in range(CC):
                    aop(wait_t=max(t6_tr(tt, cc),
                                   t6_pr(tt - 2, 1) if (cc == 0 and tt >= 2) else 0),
                        copy=(ofs[:, (tt % 2) * D + cc * 128:(tt % 2) * D + (cc + 1) * 128],
                              trp6[:, ((8 * tt + cc) % 2) * 1024:((8 * tt + cc) % 2) * 1024 + 128]))
                aop(wait_d=d5ev[("ln", tt, 3)],
                    act=((scr6[:, :], att6[:, :], AF.Identity),
                         dict(bias=0.0, scale=1.0 / D, accum_out=sb[:, 0:1])))
                aop(wait_d=d5ev[("ln", tt, 4)],
                    act=((scr6[:, :], att6[:, :], AF.Square),
                         dict(scale=RSD, accum_out=sb[:, 1:2])))
                aop(act=((sb[:, 3:4], sb[:, 1:2], AF.Sqrt),
                         dict(bias=epsc[:, :])))
                for cc in range(CC):
                    aop(wait_t=max(t6_ftr(tt, cc),
                                   t6_g(tt - 2) if (cc == 0 and tt >= 2) else 0),
                        copy=(y1fs[:, (tt % 2) * D + cc * 128:(tt % 2) * D + (cc + 1) * 128],
                              trp6.bitcast(F32)[:, 1024 + ((8 * tt + cc) % 2) * 512:
                                                1024 + ((8 * tt + cc) % 2) * 512 + 128]))
                aop(wait_t=t6_g(tt),
                    copy=(G_sb[:, 8 * tt:8 * tt + 8],
                          Gps[:, (tt % 2) * 512:(tt % 2) * 512 + 8]))
            assert na == 64 + 21 * NT

        @blk.sync
        def _(s):
            for tt in range(NT):
                if tt >= 2:
                    s.wait_ge(s_d5, d5ev[("ln", tt - 2, 2)])
                s.dma_start(out=x2nld[:, (tt % 2) * D:(tt % 2 + 1) * D],
                            in_=sp_x2n[tt * 128:(tt + 1) * 128, :]).then_inc(s_xl6[tt % 2], 16)
                s.wait_ge(s_d5, d5ev[("ln", tt, 7)])
                s.dma_start(out=sp_y1[tt * 128:(tt + 1) * 128, :],
                            in_=y1t[:, (tt % 2) * D:(tt % 2 + 1) * D]).then_inc(s_y1s[tt % 2], 16)
                s.wait_ge(s_d5, d5ev[("ln", tt, 8)])
                s.dma_start(out=cc_y1in[tt * 128:(tt + 1) * 128, :],
                            in_=y1b16[:, (tt % 2) * D:(tt % 2 + 1) * D]).then_inc(s_y1c[tt % 2], 16)
            s.wait_ge(s_y1s[0], 128)
            s.wait_ge(s_y1s[1], 128)

        @blk.gpsimd
        def _(g):
            for rc in range(CC):
                if rc >= 2:
                    g.wait_ge(s_t6, 8 * (rc - 1))
                g.dma_start(out=wrow6[:, (rc % 2) * D:(rc % 2 + 1) * D],
                            in_=wout_in[rc * 128:(rc + 1) * 128, :]).then_inc(s_wl6[rc % 2], 16)
            g.dma_start(out=wgb.ap().rearrange("p (c e) -> p c e", c=8),
                        in_=bass.AP(wg_in, 0, [[8, 128], [1024, 8], [1, 8]])).then_inc(s_cg, 16)
            g.dma_start(out=boutbc[:, :],
                        in_=bout_in.ap().broadcast_to([128, D])).then_inc(s_cg, 16)
            g.dma_start(out=g1v[:, :],
                        in_=ln1g_in.ap().broadcast_to([128, D])).then_inc(s_cg, 16)
            g.dma_start(out=b1v[:, :],
                        in_=ln1b_in.ap().broadcast_to([128, D])).then_inc(s_cg, 16)
            g.wait_ge(s_ag4, 2)
            for q in range(128):
                tt, m = q // 8, q % 8
                if q >= 3:
                    qp = q - 3
                    g.wait_ge(s_d5, d5ev[("cmb", qp // 8, qp % 8, 3)])
                g.dma_start(out=vmb[:, (q % 3) * D:(q % 3 + 1) * D],
                            in_=cc_vout[m * T + tt * 128: m * T + (tt + 1) * 128, :]
                            ).then_inc(s_vm[q % 3], 16)

    _es_s.close()
    _es_q.close()
    _es_o.close()

    if upto == "p4":
        s_l = nc.alloc_semaphore("fx4_l")
        s_st = nc.alloc_semaphore("fx4_s")
        with (
            nc.sbuf_tensor("ot4", [128, D], F32) as ot4,
            nc.Block() as blk,
        ):
            @blk.gpsimd
            def _(g):
                for tt in range(NT):
                    g.dma_start(out=ot4[:, :], in_=sp_y1[tt * 128:(tt + 1) * 128, :]).then_inc(s_l, 16)
                    g.wait_ge(s_l, 16 * (tt + 1))
                    g.dma_start(out=out[tt * 128:(tt + 1) * 128, :], in_=ot4[:, :]).then_inc(s_st, 16)
                    g.wait_ge(s_st, 16 * (tt + 1))
        return nc

    _es_w = ExitStack()
    w1sb = _es_w.enter_context(nc.sbuf_tensor("w1sb", [128, CC * FF], BF16))
    w2sb = _es_w.enter_context(nc.sbuf_tensor("w2sb", [128, 32 * D], BF16))
    gidx = _es_w.enter_context(nc.sbuf_tensor("gidx", [128, NT], I32))
    s_wld = nc.alloc_semaphore("p7_wld")
    s_d7 = nc.alloc_semaphore("p7_d")

    with ExitStack() as _es:
        stairb = _es.enter_context(nc.sbuf_tensor("stairb", [128, 4096], BF16))
        erowb = _es.enter_context(nc.sbuf_tensor("erowb", [128, 8], F32))
        crev = _es.enter_context(nc.sbuf_tensor("crev", [128, 8], F32))
        mb = _es.enter_context(nc.sbuf_tensor("mb", [128, NT * 8], BF16))
        mfull = _es.enter_context(nc.sbuf_tensor("mfull", [128, NT * 8], F32))
        m8 = _es.enter_context(nc.sbuf_tensor("m8", [128, 8], F32))
        l8 = _es.enter_context(nc.sbuf_tensor("l8", [128, 8], F32))
        gsubb = _es.enter_context(nc.sbuf_tensor("gsubb", [128, NT * 8], F32))
        gexpb = _es.enter_context(nc.sbuf_tensor("gexpb", [128, NT * 8], F32))
        smx5 = _es.enter_context(nc.sbuf_tensor("smx5", [128, 8], F32))
        idxf = _es.enter_context(nc.sbuf_tensor("idxf", [128, NT], F32))
        posb = _es.enter_context(nc.sbuf_tensor("posb", [128, NT], F32))
        kpb = _es.enter_context(nc.sbuf_tensor("kpb", [128, NT], F32))
        dstf = _es.enter_context(nc.sbuf_tensor("dstf", [128, NT], F32))
        dstcf = _es.enter_context(nc.sbuf_tensor("dstcf", [128, NT], F32))
        cntsb = _es.enter_context(nc.sbuf_tensor("cntsb", [8, 1], F32))
        cntb = _es.enter_context(nc.sbuf_tensor("cntb", [8, 8], F32))
        pmb = _es.enter_context(nc.sbuf_tensor("pmb", [8, 1], F32))
        basesb = _es.enter_context(nc.sbuf_tensor("basesb", [8, 1], F32))
        baseb = _es.enter_context(nc.sbuf_tensor("baseb", [128, 8], F32))
        locsb = _es.enter_context(nc.sbuf_tensor("locsb", [8, 2048], F32))
        ltok = _es.enter_context(nc.sbuf_tensor("ltok", [128, NT * 8], F32))
        zrow = _es.enter_context(nc.sbuf_tensor("zrow", [128, 129], F32))
        invsb = _es.enter_context(nc.sbuf_tensor("invsb", [128, NT], F32))
        invw_sb = _es.enter_context(nc.sbuf_tensor("invw_sb", [128, 129 * NT], F32))
        invcol = _es.enter_context(nc.sbuf_tensor("invcol", [128, 129], F32))
        myinvb = _es.enter_context(nc.sbuf_tensor("myinvb", [128, 1], I32))
        c128b = _es.enter_context(nc.sbuf_tensor("c128b", [128, 1], F32))
        myinvf = _es.enter_context(nc.sbuf_tensor("myinvf", [128, 1], F32))
        off8f = _es.enter_context(nc.sbuf_tensor("off8f", [128, 8], F32))
        off8i = _es.enter_context(nc.sbuf_tensor("off8i", [128, 8], I32))
        inv8 = _es.enter_context(nc.sbuf_tensor("inv8", [128, 8 * NT], F32))
        izb = _es.enter_context(nc.sbuf_tensor("izb", [128, NT], F32))
        gidxf = _es.enter_context(nc.sbuf_tensor("gidxf", [128, NT], F32))
        sidxsb = _es.enter_context(nc.sbuf_tensor("sidxsb", [128, NT], F32))
        loc_ps = _es.enter_context(nc.psum_tensor("loc_ps", [8, 2048], F32))
        cnt_ps = _es.enter_context(nc.psum_tensor("cnt_ps", [8, 512], F32))
        base_ps = _es.enter_context(nc.psum_tensor("base_ps", [8, 512], F32))
        ltps = _es.enter_context(nc.psum_tensor("ltps", [128, 1024], F32))
        s_c7 = nc.alloc_semaphore("p7_c")
        s_t7 = nc.alloc_semaphore("p7_t")
        s_a7 = nc.alloc_semaphore("p7_a")
        s_cc0 = nc.alloc_semaphore("p7_cc0")
        s_cag = nc.alloc_semaphore("p7_cag")
        s_cnb = nc.alloc_semaphore("p7_cnb")
        s_bb = nc.alloc_semaphore("p7_bb")
        s_cb = nc.alloc_semaphore("p7_cb")
        s_zi = nc.alloc_semaphore("p7_zi")
        s_sc7 = nc.alloc_semaphore("p7_sc")
        s_iw = nc.alloc_semaphore("p7_iw")
        s_is = nc.alloc_semaphore("p7_is")
        s_ar = nc.alloc_semaphore("p7_ar")
        s_ms = nc.alloc_semaphore("p7_ms")
        blk = _es.enter_context(nc.Block())

        @blk.vector
        def _(v):
            ec = SelfChain(v, s_d7)
            v.wait_ge(s_c7, 80)
            ec.op("tensor_scalar", crev[:, :], erowb[:, :], -1.0, 8.0, OP.mult, OP.add)
            ec.op("memset", zrow[:, :], 0.0)
            ec.op("memset", c128b[:, :], 128.0)
            for tt in range(NT):
                Gt = G_sb[:, 8 * tt:8 * tt + 8]
                ec.op("tensor_reduce", smx5[:, 0:1], Gt, AXL.X, OP.max)
                ec.op("tensor_scalar", mfull[:, 8 * tt:8 * tt + 8], Gt,
                      smx5[:, 0:1], None, OP.is_equal)
                ec.op("tensor_copy", mb[:, 8 * tt:8 * tt + 8], mfull[:, 8 * tt:8 * tt + 8])
                ec.op("tensor_tensor", m8[:, :], mfull[:, 8 * tt:8 * tt + 8],
                      crev[:, :], OP.mult)
                ec.op("tensor_reduce", smx5[:, 1:2], m8[:, :], AXL.X, OP.max)
                ec.op("tensor_scalar", idxf[:, tt:tt + 1], smx5[:, 1:2], -1.0, 8.0,
                      OP.mult, OP.add)
                ec.op("tensor_scalar", gsubb[:, 8 * tt:8 * tt + 8], Gt,
                      smx5[:, 0:1], None, OP.subtract)
            assert ec.n == 115
            for tt in range(NT):
                v.wait_ge(s_a7, tt + 1)
                ec.op("tensor_reduce", smx5[:, 2:3], gexpb[:, 8 * tt:8 * tt + 8],
                      AXL.X, OP.add)
                ec.op("reciprocal", gval[:, tt:tt + 1], smx5[:, 2:3])
            assert ec.n == 147
            v.wait_ge(s_cb, 16)
            for tt in range(NT):
                v.wait_ge(s_a7, 20 + tt)
                ec.op("tensor_tensor", l8[:, :], ltok[:, 8 * tt:8 * tt + 8],
                      baseb[:, :], OP.add)
                ec.op("tensor_tensor", l8[:, :], l8[:, :],
                      mfull[:, 8 * tt:8 * tt + 8], OP.mult)
                ec.op("tensor_reduce", posb[:, tt:tt + 1], l8[:, :], AXL.X, OP.add)
                ec.op("tensor_scalar", smx5[:, 3:4], posb[:, tt:tt + 1], 2047.0,
                      None, OP.min)
                ec.op("tensor_tensor", kpb[:, tt:tt + 1], smx5[:, 3:4],
                      posb[:, tt:tt + 1], OP.is_equal)
                ec.op("tensor_scalar", dstcf[:, tt:tt + 1], idxf[:, tt:tt + 1],
                      2048.0, None, OP.mult)
                ec.op("tensor_tensor", dstcf[:, tt:tt + 1], dstcf[:, tt:tt + 1],
                      posb[:, tt:tt + 1], OP.add)
                ec.op("tensor_scalar", smx5[:, 4:5], posb[:, tt:tt + 1],
                      1.0 / 128.0, 8388608.0, OP.mult, OP.add)
                ec.op("tensor_scalar", smx5[:, 4:5], smx5[:, 4:5],
                      8388608.0, None, OP.subtract)
                ec.op("tensor_scalar", smx5[:, 5:6], smx5[:, 4:5],
                      128.0, None, OP.mult)
                ec.op("tensor_tensor", smx5[:, 5:6], posb[:, tt:tt + 1],
                      smx5[:, 5:6], OP.subtract)
                ec.op("tensor_scalar", smx5[:, 6:7], smx5[:, 5:6],
                      -1.0, None, OP.min)
                ec.op("tensor_tensor", smx5[:, 6:7], smx5[:, 6:7],
                      smx5[:, 5:6], OP.is_equal)
                ec.op("tensor_tensor", smx5[:, 4:5], smx5[:, 4:5],
                      smx5[:, 6:7], OP.subtract)
                ec.op("tensor_scalar", smx5[:, 6:7], smx5[:, 6:7],
                      128.0, None, OP.mult)
                ec.op("tensor_tensor", smx5[:, 5:6], smx5[:, 5:6],
                      smx5[:, 6:7], OP.add)
                ec.op("tensor_scalar", smx5[:, 5:6], smx5[:, 5:6],
                      16.0, None, OP.mult)
                ec.op("tensor_tensor", smx5[:, 5:6], smx5[:, 5:6],
                      smx5[:, 4:5], OP.add)
                ec.op("tensor_scalar", dstf[:, tt:tt + 1], idxf[:, tt:tt + 1],
                      2064.0, None, OP.mult)
                ec.op("tensor_tensor", dstf[:, tt:tt + 1], dstf[:, tt:tt + 1],
                      smx5[:, 5:6], OP.add)
                ec.op("tensor_scalar", dstf[:, tt:tt + 1], dstf[:, tt:tt + 1],
                      2048.0, None, OP.subtract)
                ec.op("tensor_tensor", dstf[:, tt:tt + 1], dstf[:, tt:tt + 1],
                      kpb[:, tt:tt + 1], OP.mult)
                ec.op("tensor_scalar", dstf[:, tt:tt + 1], dstf[:, tt:tt + 1],
                      2048.0, None, OP.add)
            assert ec.n == 147 + 23 * NT
            ec.op("tensor_tensor", gval[:, :], gval[:, :], kpb[:, :], OP.mult)
            ec.op("tensor_copy", dst_i32[:, :], dstf[:, :])
            ec.op("tensor_scalar", dstcf[:, :], dstcf[:, :], 16383.0, None, OP.min)
            ec.op("tensor_copy", dstc_i32[:, :], dstcf[:, :])
            ec.op("tensor_copy", myinvf[:, :], myinvb[:, :])
            for r in range(8):
                ec.op("tensor_scalar", off8f[:, r:r + 1], myinvf[:, :],
                      1032.0 * r, None, OP.add)
            ec.op("tensor_copy", off8i[:, :], off8f[:, :])
            v.wait_ge(s_iw, 256)
            ec.op("tensor_copy", invcol[:, :], invw_sb[:, 0:129])
            for cidx in range(1, NT):
                ec.op("tensor_tensor", invcol[:, :], invcol[:, :],
                      invw_sb[:, cidx * 129:(cidx + 1) * 129],
                      OP.add)
            v.wait_ge(s_ms, 128)
            ec.op("tensor_tensor", invsb[:, :], inv8[:, 0:NT],
                  inv8[:, NT:2 * NT], OP.add)
            for r in range(2, 8):
                ec.op("tensor_tensor", invsb[:, :], invsb[:, :],
                      inv8[:, r * NT:(r + 1) * NT], OP.add)
            ec.op("tensor_scalar", izb[:, :], invsb[:, :], 0.0, None, OP.is_equal)
            ec.op("tensor_scalar", gidxf[:, :], izb[:, :], 65536.0, None, OP.mult)
            ec.op("tensor_tensor", gidxf[:, :], gidxf[:, :], invsb[:, :], OP.add)
            ec.op("tensor_scalar", gidxf[:, :], gidxf[:, :], 1.0, None, OP.subtract)
            ec.op("tensor_scalar", gidxf[:, :], gidxf[:, :], 16383.0, None, OP.min)
            ec.op("tensor_copy", gidx[:, :], gidxf[:, :])
            assert ec.n == 558

        @blk.scalar
        def _(sc):
            for tt in range(NT):
                sc.wait_ge(s_d7, 7 * tt + 10)
                sc.activation(gexpb[:, 8 * tt:8 * tt + 8],
                              gsubb[:, 8 * tt:8 * tt + 8], AF.Exp).then_inc(s_a7, 1)
            sc.wait_ge(s_t7, 16)
            sc.copy(cntsb[0:8, 0:1], cnt_ps[0:8, 0:1]).then_inc(s_a7, 1)
            sc.wait_ge(s_t7, 80)
            sc.copy(locsb[0:8, :], loc_ps[0:8, :]).then_inc(s_a7, 1)
            sc.wait_ge(s_t7, 81)
            sc.copy(basesb[0:8, 0:1], base_ps[0:8, 0:1]).then_inc(s_a7, 1)
            for tt in range(NT):
                sc.wait_ge(s_t7, 82 + tt)
                sc.copy(ltok[:, 8 * tt:8 * tt + 8],
                        ltps[:, (tt % 2) * 512:(tt % 2) * 512 + 8]).then_inc(s_a7, 1)

        @blk.tensor
        def _(t):
            for tt in range(NT):
                t.wait_ge(s_d7, 7 * tt + 6)
                t.matmul(cnt_ps[0:8, 0:1], mb[:, 8 * tt:8 * tt + 8], onescol[:, 0:1],
                         start=(tt == 0), stop=(tt == NT - 1)).then_inc(s_t7, 1)
            t.wait_ge(s_c7, 80)
            for win in range(4):
                for tt in range(NT):
                    off = 2048 - 128 * tt - 1 + win * 512
                    t.matmul(loc_ps[0:8, win * 512:(win + 1) * 512],
                             mb[:, 8 * tt:8 * tt + 8], stairb[:, off:off + 512],
                             start=(tt == 0), stop=(tt == NT - 1)).then_inc(s_t7, 1)
            t.wait_ge(s_cnb, 16)
            t.matmul(base_ps[0:8, 0:1], cntb[0:8, :], pmb[0:8, 0:1],
                     start=True, stop=True).then_inc(s_t7, 1)
            t.wait_ge(s_a7, 18)
            for tt in range(NT):
                if tt >= 2:
                    t.wait_ge(s_a7, 20 + tt - 2)
                t.transpose(ltps[:, (tt % 2) * 512:(tt % 2) * 512 + 8],
                            locsb[0:8, tt * 128:(tt + 1) * 128],
                            identf[0:8, 0:8]).then_inc(s_t7, 1)

        @blk.sync
        def _(s):
            s.dma_start(out=erowb[:, :],
                        in_=erow.ap().broadcast_to([128, 8])).then_inc(s_c7, 16)
            s.dma_start(out=stairb[:, :], in_=stair[:, :]).then_inc(s_c7, 16)
            s.dma_start(out=sidxsb[:, :], in_=sidxf_in[:, :]).then_inc(s_c7, 16)
            s.dma_start(out=pmb[0:8, :], in_=prevmask_in[:, :]).then_inc(s_c7, 16)
            s.dma_start(out=myinvb[:, :], in_=myslot_in[:, :]).then_inc(s_c7, 16)
            s.wait_ge(s_wcast, 32)
            for cc in range(CC):
                s.dma_start(out=w1sb[:, cc * FF:(cc + 1) * FF],
                            in_=w1bf[cc * 128:(cc + 1) * 128, :]).then_inc(s_wld, 16)
            for j in range(8):
                s.dma_start(out=w2sb.ap().rearrange("p (f q o) -> p f q o", f=8, q=4)[:, j],
                            in_=bass.AP(w2bf, j * 512 * D,
                                        [[D, 128], [128 * D, 4], [1, D]])).then_inc(s_wld, 16)

        @blk.gpsimd
        def _(g):
            g.wait_ge(s_a7, 17)
            g.dma_start(out=cc_cntin.ap().rearrange("a b -> b a"),
                        in_=cntsb[0:8, 0:1]).then_inc(s_cc0, 16)
            g.wait_ge(s_cc0, 16)
            g.collective_compute("AllGather", OP.bypass, replica_groups=RG,
                                 ins=[cc_cntin.ap().opt()],
                                 outs=[cc_cntout.ap().opt()]).then_inc(s_cag, 1)
            g.wait_ge(s_y1c[0], 128)
            g.wait_ge(s_y1c[1], 128)
            g.collective_compute("AllGather", OP.bypass, replica_groups=RG,
                                 ins=[cc_y1in.ap().opt()],
                                 outs=[cc_y1out.ap().opt()]).then_inc(s_ag6, 1)
            g.wait_ge(s_cag, 1)
            g.dma_start(out=cntb[0:8, :], in_=cc_cntout[:, :]).then_inc(s_cnb, 16)
            g.wait_ge(s_a7, 19)
            g.dma_start(out=basesc.ap().rearrange("a b -> b a"),
                        in_=basesb[0:8, 0:1]).then_inc(s_bb, 16)
            g.wait_ge(s_bb, 16)
            g.dma_start(out=baseb[:, :],
                        in_=basesc.ap().broadcast_to([128, 8])).then_inc(s_cb, 16)
            g.wait_ge(s_d7, 2)
            for tt in range(NT):
                g.dma_start(out=cc_invw_t[tt].ap().rearrange("(p a) b -> p (a b)", p=128),
                            in_=zrow[:, :]).then_inc(s_zi, 16)
            g.wait_ge(s_zi, 256)
            g.wait_ge(s_d7, 517)
            for tt in range(NT):
                g.indirect_dma_start(
                    out=cc_invw_t[tt][:, :],
                    out_offset=IndirectOffsetOnAxis(ap=dst_i32[:, tt:tt + 1], axis=0),
                    in_=sidxsb[:, tt:tt + 1], in_offset=None).then_inc(s_sc7, 16)
            g.wait_ge(s_sc7, 256)
            for tt in range(NT):
                g.dma_start(out=invw_sb[:, tt * 129:(tt + 1) * 129],
                            in_=cc_invw_t[tt].ap().rearrange("(p a) b -> p (a b)", p=128)
                            ).then_inc(s_iw, 16)
            g.wait_ge(s_d7, 545)
            g.dma_start(out=cc_invin.ap().rearrange("(p a) b -> p (a b)", p=128),
                        in_=invcol[:, :]).then_inc(s_is, 16)
            g.wait_ge(s_is, 16)
            g.collective_compute("AllGather", OP.bypass, replica_groups=RG,
                                 ins=[cc_invin.ap().rearrange(
                                     "(a b) c -> a (b c)", b=128).opt()],
                                 outs=[cc_iall.ap().rearrange(
                                     "(a b) c -> a (b c)", b=128).opt()]).then_inc(s_ar, 1)
            g.wait_ge(s_ar, 1)
            g.wait_ge(s_d7, 529)
            iview = cc_iall.ap().rearrange("(g w) b -> g (w b)", w=16)
            for r in range(8):
                g.indirect_dma_start(
                    out=inv8[:, r * NT:(r + 1) * NT], out_offset=None,
                    in_=iview,
                    in_offset=IndirectOffsetOnAxis(ap=off8i[:, r:r + 1], axis=0)
                    ).then_inc(s_ms, 16)
            if upto == "p5":
                s_dg = nc.alloc_semaphore("p5_dump")
                g.wait_ge(s_d7, 558)
                dumps = [dstf, idxf, gval, kpb, posb, invsb, gidxf]
                for i, tsr in enumerate(dumps):
                    g.dma_start(out=out[i * 128:(i + 1) * 128, 0:NT],
                                in_=tsr[:, :]).then_inc(s_dg, 16)
                g.dma_start(out=out[7 * 128:8 * 128, 0:128],
                            in_=G_sb[:, :]).then_inc(s_dg, 16)
                g.dma_start(out=out[8 * 128:9 * 128, 0:128],
                            in_=ltok[:, :]).then_inc(s_dg, 16)
                g.dma_start(out=out[9 * 128:10 * 128, 0:8],
                            in_=baseb[:, :]).then_inc(s_dg, 16)
                g.wait_ge(s_dg, 160)


    if upto == "p5":
        return nc

    with ExitStack() as _es:
        dsb = _es.enter_context(nc.sbuf_tensor("dsb", [128, 2 * D], BF16))
        dispf = _es.enter_context(nc.sbuf_tensor("dispf", [128, 2 * CC * 512], BF16))
        h_sb = _es.enter_context(nc.sbuf_tensor("h_sb", [128, 32 * 512], BF16))
        yesb = _es.enter_context(nc.sbuf_tensor("yesb", [128, 2 * D], BF16))
        b1sb = _es.enter_context(nc.sbuf_tensor("b1sb", [32, 128], F32))
        b1T = _es.enter_context(nc.sbuf_tensor("b1T", [128, 32], F32))
        b2bc = _es.enter_context(nc.sbuf_tensor("b2bc", [128, D], F32))
        h_ps = _es.enter_context(nc.psum_tensor("h_ps", [128, 1024], F32))
        ye_ps = _es.enter_context(nc.psum_tensor("ye_ps", [128, 1024], F32))
        trdp = _es.enter_context(nc.psum_tensor("trdp", [128, 2048], BF16))
        s_c8 = nc.alloc_semaphore("p8_c")
        s_ds = [nc.alloc_semaphore("p8_ds0"), nc.alloc_semaphore("p8_ds1")]
        s_ye = [nc.alloc_semaphore("p8_ye0"), nc.alloc_semaphore("p8_ye1")]
        s_t8 = nc.alloc_semaphore("p8_t")
        s_a8 = nc.alloc_semaphore("p8_a")
        s_d8 = nc.alloc_semaphore("p8_d")
        s_ag8 = nc.alloc_semaphore("p8_ag")
        blk = _es.enter_context(nc.Block())

        pe_order = [("b1tr",)]
        for jg in range(4):
            for cc in range(CC):
                pe_order.append(("tr", jg, cc))
        for jw in range(4):
            for fs in range(32):
                pe_order.append(("w1", jw, fs))
            if jw + 1 < 4:
                for jg in range(4 * (jw + 1), 4 * (jw + 2)):
                    for cc in range(CC):
                        pe_order.append(("tr", jg, cc))
            for g8 in range(8):
                pe_order.append(("w2", jw, g8))
        t8ev = {k: i + 1 for i, k in enumerate(pe_order)}

        act_order = [("b1ev",)]
        for jw in range(4):
            if jw == 0:
                for jg in range(4):
                    for cc in range(CC):
                        act_order.append(("dp", jg, cc))
            for fs in range(32):
                act_order.append(("h", jw, fs))
            if jw + 1 < 4:
                for jg in range(4 * (jw + 1), 4 * (jw + 2)):
                    for cc in range(CC):
                        act_order.append(("dp", jg, cc))
        a8ev = {k: i + 1 for i, k in enumerate(act_order)}

        dve_order = []
        for jg in range(4):
            dve_order.append(("ms", jg))
        for jw in range(4):
            for jg in range(4 * (jw + 1), min(4 * (jw + 2), 16)):
                dve_order.append(("ms", jg))
            for g8 in range(8):
                dve_order.append(("ye", jw, g8))
        d8ev = {k: i + 1 for i, k in enumerate(dve_order)}

        @blk.tensor
        def _(t):
            for key in pe_order:
                if key[0] == "b1tr":
                    t.wait_ge(s_c8, 32)
                    t.transpose(trdp.bitcast(F32)[:, 0:32], b1sb[0:32, :],
                                identf[0:32, 0:32]).then_inc(s_t8, 1)
                elif key[0] == "tr":
                    _, jg, cc = key
                    k = jg * 8 + cc
                    if cc == 0:
                        t.wait_ge(s_ds[jg % 2], 16 * (jg // 2 + 1))
                    if k == 0:
                        t.wait_ge(s_a8, 1)
                    if k >= 2:
                        t.wait_ge(s_a8, a8ev[("dp", (k - 2) // 8, (k - 2) % 8)])
                    t.transpose(trdp[:, (k % 2) * 1024:(k % 2) * 1024 + 128],
                                dsb[:, (jg % 2) * D + cc * 128:(jg % 2) * D + (cc + 1) * 128],
                                identb[:, :]).then_inc(s_t8, 1)
                elif key[0] == "w1":
                    _, jw, fs = key
                    if fs == 0:
                        t.wait_ge(s_a8, a8ev[("dp", 4 * jw + 3, 7)])
                        if jw == 0:
                            t.wait_ge(s_wld, 256)
                    if fs >= 2:
                        t.wait_ge(s_a8, a8ev[("h", jw, fs - 2)])
                    elif jw >= 1:
                        t.wait_ge(s_a8, a8ev[("h", jw - 1, 30 + fs)])
                    for cc in range(CC):
                        mm = t.matmul(h_ps[:, (fs % 2) * 512:(fs % 2 + 1) * 512],
                                      w1sb[:, cc * FF + fs * 128: cc * FF + (fs + 1) * 128],
                                      dispf[:, (jw % 2) * 4096 + cc * 512:
                                            (jw % 2) * 4096 + (cc + 1) * 512],
                                      start=(cc == 0), stop=(cc == CC - 1))
                    mm.then_inc(s_t8, 1)
                else:
                    _, jw, g8 = key
                    if g8 == 0:
                        t.wait_ge(s_a8, a8ev[("h", jw, 31)])
                    if g8 >= 2:
                        t.wait_ge(s_d8, d8ev[("ye", jw, g8 - 2)])
                    elif jw >= 1:
                        t.wait_ge(s_d8, d8ev[("ye", jw - 1, 6 + g8)])
                    js, ow = g8 // 2, g8 % 2
                    for fs in range(32):
                        mm = t.matmul(ye_ps[:, (g8 % 2) * 512:(g8 % 2 + 1) * 512],
                                      h_sb[:, fs * 512 + js * 128: fs * 512 + (js + 1) * 128],
                                      w2sb[:, fs * D + ow * 512: fs * D + (ow + 1) * 512],
                                      start=(fs == 0), stop=(fs == 31))
                    mm.then_inc(s_t8, 1)

        @blk.scalar
        def _(sc):
            for key in act_order:
                if key[0] == "b1ev":
                    sc.wait_ge(s_t8, 1)
                    sc.copy(b1T[:, :], trdp.bitcast(F32)[:, 0:32]).then_inc(s_a8, 1)
                elif key[0] == "dp":
                    _, jg, cc = key
                    jw = jg // 4
                    g4 = jg % 4
                    k = jg * 8 + cc
                    sc.wait_ge(s_t8, t8ev[("tr", jg, cc)])
                    sc.copy(dispf[:, (jw % 2) * 4096 + cc * 512 + g4 * 128:
                                  (jw % 2) * 4096 + cc * 512 + (g4 + 1) * 128],
                            trdp[:, (k % 2) * 1024:(k % 2) * 1024 + 128]).then_inc(s_a8, 1)
                else:
                    _, jw, fs = key
                    sc.wait_ge(s_t8, t8ev[("w1", jw, fs)])
                    if fs == 0 and jw >= 1:
                        sc.wait_ge(s_t8, t8ev[("w2", jw - 1, 7)])
                    sc.activation(h_sb[:, fs * 512:(fs + 1) * 512],
                                  h_ps[:, (fs % 2) * 512:(fs % 2 + 1) * 512],
                                  AF.Identity, bias=b1T[:, fs:fs + 1]).then_inc(s_a8, 1)

        @blk.vector
        def _(v):
            ec = SelfChain(v, s_d8)
            v.wait_ge(s_c8, 32)
            for key in dve_order:
                if key[0] == "ms":
                    jg = key[1]
                    if jg >= 2:
                        v.wait_ge(s_t8, t8ev[("tr", jg - 2, 7)])
                    ec.op("memset", dsb[:, (jg % 2) * D:(jg % 2 + 1) * D], 0.0)
                else:
                    _, jw, g8 = key
                    js, ow = g8 // 2, g8 % 2
                    js16 = jw * 4 + js
                    v.wait_ge(s_t8, t8ev[("w2", jw, g8)])
                    if js16 >= 2 and ow == 0:
                        v.wait_ge(s_ye[js16 % 2], 16 * (js16 // 2))
                    ec.op("tensor_tensor",
                          yesb[:, (js % 2) * D + ow * 512:(js % 2) * D + (ow + 1) * 512],
                          ye_ps[:, (g8 % 2) * 512:(g8 % 2 + 1) * 512],
                          b2bc[:, ow * 512:(ow + 1) * 512], OP.add)

        @blk.sync
        def _(s):
            for jw in range(4):
                for js in range(4):
                    s.wait_ge(s_d8, d8ev[("ye", jw, 2 * js + 1)])
                    js16 = jw * 4 + js
                    s.dma_start(out=cc_yein[js16 * 128:(js16 + 1) * 128, :],
                                in_=yesb[:, (js % 2) * D:(js % 2 + 1) * D]
                                ).then_inc(s_ye[js16 % 2], 16)

        @blk.gpsimd
        def _(g):
            g.dma_start(out=b1sb[0:32, :], in_=b1_in[:, :]).then_inc(s_c8, 16)
            g.dma_start(out=b2bc[:, :],
                        in_=b2_in.ap().broadcast_to([128, D])).then_inc(s_c8, 16)
            g.wait_ge(s_d7, 558)
            g.wait_ge(s_ag6, 1)
            for jg in range(16):
                g.wait_ge(s_d8, d8ev[("ms", jg)])
                g.indirect_dma_start(
                    out=dsb[:, (jg % 2) * D:(jg % 2 + 1) * D], out_offset=None,
                    in_=cc_y1out[:, :],
                    in_offset=IndirectOffsetOnAxis(ap=gidx[:, jg:jg + 1], axis=0)
                    ).then_inc(s_ds[jg % 2], 16)
            g.wait_ge(s_ye[0], 128)
            g.wait_ge(s_ye[1], 128)
            g.collective_compute("AllGather", OP.bypass, replica_groups=RG,
                                 ins=[cc_yein.ap().opt()],
                                 outs=[cc_yeout.ap().opt()]).then_inc(s_ag8, 1)

    _es_w.close()
    _es_g.close()

    with ExitStack() as _es:
        ymg = _es.enter_context(nc.sbuf_tensor("ymg", [128, NT * D], BF16))
        ymf = _es.enter_context(nc.sbuf_tensor("ymf", [128, 2 * D], F32))
        scr9 = _es.enter_context(nc.sbuf_tensor("scr9", [128, D], F32))
        y1ld = _es.enter_context(nc.sbuf_tensor("y1ld", [128, 2 * D], F32))
        x2ld = _es.enter_context(nc.sbuf_tensor("x2ld", [128, 2 * D], F32))
        t2b = _es.enter_context(nc.sbuf_tensor("t2b", [128, 2 * D], F32))
        y2b = _es.enter_context(nc.sbuf_tensor("y2b", [128, 2 * D], F32))
        outb = _es.enter_context(nc.sbuf_tensor("outb", [128, 2 * D], F32))
        st9 = _es.enter_context(nc.sbuf_tensor("st9", [128, 12 * NT], F32))
        g2v = _es.enter_context(nc.sbuf_tensor("g2v", [128, D], F32))
        l2bv = _es.enter_context(nc.sbuf_tensor("l2bv", [128, D], F32))
        s_c9 = nc.alloc_semaphore("p9_c")
        s_ymg = nc.alloc_semaphore("p9_ym")
        s_yl = [nc.alloc_semaphore("p9_yl0"), nc.alloc_semaphore("p9_yl1")]
        s_xl9 = [nc.alloc_semaphore("p9_xl0"), nc.alloc_semaphore("p9_xl1")]
        s_os = [nc.alloc_semaphore("p9_os0"), nc.alloc_semaphore("p9_os1")]
        s_a9 = nc.alloc_semaphore("p9_a")
        s_d9 = nc.alloc_semaphore("p9_d")
        blk = _es.enter_context(nc.Block())
        RSD = 1.0 / 32.0

        def d9(tt, j):
            return 1 + 8 * tt + j

        act9_order = [("cast", 0), ("cast", 1)]
        for _tt in range(NT):
            for _nm in ("acc1", "sqacc1", "sqrt1", "acc2", "sqacc2", "sqrt2"):
                act9_order.append((_nm, _tt))
            if _tt + 2 < NT:
                act9_order.append(("cast", _tt + 2))
        a9ev = {k: i + 1 for i, k in enumerate(act9_order)}

        @blk.vector
        def _(v):
            ec = SelfChain(v, s_d9)
            ec.op("memset", ymg[:, :], 0.0)
            v.wait_ge(s_c9, 32)
            for tt in range(NT):
                sb = st9[:, 12 * tt:12 * tt + 12]
                sl = (tt % 2) * D
                t2 = t2b[:, sl:sl + D]
                y2 = y2b[:, sl:sl + D]
                ot = outb[:, sl:sl + D]
                v.wait_ge(s_a9, a9ev[("cast", tt)])
                v.wait_ge(s_yl[tt % 2], 16 * (tt // 2 + 1))
                ec.op("scalar_tensor_tensor", t2, ymf[:, sl:sl + D],
                      gval[:, tt:tt + 1], y1ld[:, sl:sl + D], OP.mult, OP.add)
                v.wait_ge(s_a9, a9ev[("acc1", tt)])
                ec.op("tensor_scalar", t2, t2, sb[:, 0:1], None, OP.subtract)
                v.wait_ge(s_a9, a9ev[("sqrt1", tt)])
                ec.op("reciprocal", sb[:, 4:5], sb[:, 3:4])
                ec.op("scalar_tensor_tensor", y2, t2, sb[:, 4:5], g2v[:, :],
                      OP.mult, OP.mult)
                ec.op("tensor_tensor", y2, y2, l2bv[:, :], OP.add)
                v.wait_ge(s_a9, a9ev[("acc2", tt)])
                ec.op("tensor_scalar", y2, y2, sb[:, 5:6], None, OP.subtract)
                v.wait_ge(s_a9, a9ev[("sqrt2", tt)])
                ec.op("reciprocal", sb[:, 9:10], sb[:, 8:9])
                v.wait_ge(s_xl9[tt % 2], 16 * (tt // 2 + 1))
                if tt >= 2:
                    v.wait_ge(s_os[tt % 2], 16 * (tt // 2))
                ec.op("scalar_tensor_tensor", ot, y2, sb[:, 9:10],
                      x2ld[:, sl:sl + D], OP.mult, OP.add)
            assert ec.n == 1 + 8 * NT

        @blk.scalar
        def _(sc):
            sc.wait_ge(s_ymg, 256)
            sc.wait_ge(s_c9, 32)
            na = 0

            def aop(wd, *a, **k):
                nonlocal na
                if wd is not None:
                    sc.wait_ge(s_d9, wd)
                if na:
                    sc.wait_ge(s_a9, na)
                sc.activation(*a, **k).then_inc(s_a9, 1)
                na += 1
            for key in act9_order:
                nm, tt = key
                sb = st9[:, 12 * tt:12 * tt + 12]
                sl = (tt % 2) * D
                if nm == "cast":
                    aop(d9(tt - 2, 1) if tt >= 2 else None,
                        ymf[:, sl:sl + D], ymg[:, tt * D:(tt + 1) * D], AF.Identity,
                        bias=0.0)
                elif nm == "acc1":
                    aop(d9(tt, 1), scr9[:, :], t2b[:, sl:sl + D], AF.Identity,
                        bias=0.0, scale=1.0 / D, accum_out=sb[:, 0:1])
                elif nm == "sqacc1":
                    aop(d9(tt, 2), scr9[:, :], t2b[:, sl:sl + D], AF.Square,
                        scale=RSD, accum_out=sb[:, 1:2])
                elif nm == "sqrt1":
                    aop(None, sb[:, 3:4], sb[:, 1:2], AF.Sqrt, bias=epsc[:, :])
                elif nm == "acc2":
                    aop(d9(tt, 5), scr9[:, :], y2b[:, sl:sl + D], AF.Identity,
                        bias=0.0, scale=1.0 / D, accum_out=sb[:, 5:6])
                elif nm == "sqacc2":
                    aop(d9(tt, 6), scr9[:, :], y2b[:, sl:sl + D], AF.Square,
                        scale=RSD, accum_out=sb[:, 6:7])
                else:
                    aop(None, sb[:, 8:9], sb[:, 6:7], AF.Sqrt, bias=epsc[:, :])
            assert na == len(act9_order)

        @blk.gpsimd
        def _(g):
            g.dma_start(out=g2v[:, :],
                        in_=ln2g_in.ap().broadcast_to([128, D])).then_inc(s_c9, 16)
            g.dma_start(out=l2bv[:, :],
                        in_=ln2b_in.ap().broadcast_to([128, D])).then_inc(s_c9, 16)
            g.wait_ge(s_ag8, 1)
            g.wait_ge(s_d9, 1)
            for tt in range(NT):
                g.indirect_dma_start(
                    out=ymg[:, tt * D:(tt + 1) * D], out_offset=None,
                    in_=cc_yeout[:, :],
                    in_offset=IndirectOffsetOnAxis(ap=dstc_i32[:, tt:tt + 1], axis=0)
                    ).then_inc(s_ymg, 16)

        @blk.sync
        def _(s):
            for tt in range(NT):
                if tt >= 2:
                    s.wait_ge(s_d9, d9(tt - 2, 1))
                s.dma_start(out=y1ld[:, (tt % 2) * D:(tt % 2 + 1) * D],
                            in_=sp_y1[tt * 128:(tt + 1) * 128, :]).then_inc(s_yl[tt % 2], 16)
                if tt >= 2:
                    s.wait_ge(s_d9, d9(tt - 2, 8))
                s.dma_start(out=x2ld[:, (tt % 2) * D:(tt % 2 + 1) * D],
                            in_=sp_x2[tt * 128:(tt + 1) * 128, :]).then_inc(s_xl9[tt % 2], 16)
                s.wait_ge(s_d9, d9(tt, 8))
                s.dma_start(out=out[tt * 128:(tt + 1) * 128, :],
                            in_=outb[:, (tt % 2) * D:(tt % 2 + 1) * D]).then_inc(s_os[tt % 2], 16)
            s.wait_ge(s_os[0], 128)
            s.wait_ge(s_os[1], 128)

    _es_r.close()
    return nc


_NC_CACHE = {}
UPTO = "full"


def kernel(**inputs):
    if UPTO not in _NC_CACHE:
        _NC_CACHE[UPTO] = build_nc(UPTO)
    nc = _NC_CACHE[UPTO]

    f32 = np.float32
    x = np.asarray(inputs["L0_attn_x"], f32)
    in_maps = []
    for c in range(NCORES):
        m = {
            "x_in": np.ascontiguousarray(x[c]),
            "wq": np.asarray(inputs["L0_attn_W_Q"], f32),
            "wk": np.asarray(inputs["L0_attn_W_K"], f32),
            "wv": np.asarray(inputs["L0_attn_W_V"], f32),
            "wkp": np.asarray(inputs["L0_attn_W_K_prime"], f32),
            "wo": np.asarray(inputs["L0_attn_W_O"], f32),
            "alpha": np.asarray(inputs["L0_attn_alpha"], f32).reshape(1, 1),
            "win": np.asarray(inputs["L0_ffn_in_proj_w"], f32),
            "binq": np.asarray(inputs["L0_ffn_in_proj_b"], f32).reshape(24, 128),
            "wout": np.asarray(inputs["L0_ffn_out_proj_w"], f32),
            "bout": np.asarray(inputs["L0_ffn_out_proj_b"], f32).reshape(1, D),
            "ln1g": np.asarray(inputs["L0_ffn_ln1_g"], f32).reshape(1, D),
            "ln1b": np.asarray(inputs["L0_ffn_ln1_b"], f32).reshape(1, D),
            "ln2g": np.asarray(inputs["L0_ffn_ln2_g"], f32).reshape(1, D),
            "ln2b": np.asarray(inputs["L0_ffn_ln2_b"], f32).reshape(1, D),
            "wg": np.asarray(inputs["L0_ffn_wg"], f32),
            "w1": np.ascontiguousarray(np.asarray(inputs["L0_ffn_w1"], f32)[c]),
            "b1": np.ascontiguousarray(np.asarray(inputs["L0_ffn_b1"], f32)[c]).reshape(32, 128),
            "w2": np.ascontiguousarray(np.asarray(inputs["L0_ffn_w2"], f32)[c]),
            "b2": np.ascontiguousarray(np.asarray(inputs["L0_ffn_b2"], f32)[c]).reshape(1, D),
            "prevmask": (np.arange(8) < c).astype(f32).reshape(8, 1),
            "sidxf": np.ascontiguousarray(
                np.arange(T, dtype=np.float32).reshape(NT, 128).T + (c * T + 1)),
            "myslot": np.ascontiguousarray(
                (np.arange(128, dtype=np.int32) + 129 * c).reshape(128, 1)),
        }
        in_maps.append(m)

    res = run_bass_kernel_spmd(nc, in_maps, core_ids=list(range(NCORES)))
    return np.stack([r["out"] for r in res.results], axis=0)


# version 25
# speedup vs baseline: 1.1383x; 1.0215x over previous
import numpy as np
import ml_dtypes
from contextlib import ExitStack
import concourse.bass as bass
import concourse.mybir as mybir
from concourse.bass_utils import run_bass_kernel_spmd
from concourse.bass_core_rust import IndirectOffsetOnAxis

F32 = mybir.dt.float32
BF16 = mybir.dt.bfloat16
I32 = mybir.dt.int32
AF = mybir.ActivationFunctionType
OP = mybir.AluOpType
AXL = mybir.AxisListType

T = 2048
D = 1024
NT = 16
CC = 8
PAIRS = 8
NITER = 1024
EXP_SHIFT = -20.0
EPS = 1e-5
NCORES = 8
FF = 4096
NE = 8
CAP = 2048
VP = 129 * 8
RG = [list(range(NCORES))]


class SelfChain:

    def __init__(self, eng, sem, n0=0):
        self.e = eng
        self.sem = sem
        self.n = n0

    def op(self, name, *a, **k):
        if self.n:
            self.e.wait_ge(self.sem, self.n)
        inst = getattr(self.e, name)(*a, **k)
        inst.then_inc(self.sem, 1)
        self.n += 1
        return inst


def build_nc(upto="full", detect_races=True, serial=False):
    nc = bass.Bass("TRN2", target_bir_lowering=False,
                   detect_race_conditions=detect_races)

    x_in = nc.dram_tensor("x_in", [T, D], F32, kind="ExternalInput")
    wq_in = nc.dram_tensor("wq", [D, D], F32, kind="ExternalInput")
    wk_in = nc.dram_tensor("wk", [D, D], F32, kind="ExternalInput")
    wv_in = nc.dram_tensor("wv", [D, D], F32, kind="ExternalInput")
    wkp_in = nc.dram_tensor("wkp", [D, D], F32, kind="ExternalInput")
    wo_in = nc.dram_tensor("wo", [D, D], F32, kind="ExternalInput")
    alpha_in = nc.dram_tensor("alpha", [1, 1], F32, kind="ExternalInput")
    win_in = nc.dram_tensor("win", [3 * D, D], F32, kind="ExternalInput")
    binq_in = nc.dram_tensor("binq", [24, 128], F32, kind="ExternalInput")
    wout_in = nc.dram_tensor("wout", [D, D], F32, kind="ExternalInput")
    bout_in = nc.dram_tensor("bout", [1, D], F32, kind="ExternalInput")
    ln1g_in = nc.dram_tensor("ln1g", [1, D], F32, kind="ExternalInput")
    ln1b_in = nc.dram_tensor("ln1b", [1, D], F32, kind="ExternalInput")
    ln2g_in = nc.dram_tensor("ln2g", [1, D], F32, kind="ExternalInput")
    ln2b_in = nc.dram_tensor("ln2b", [1, D], F32, kind="ExternalInput")
    wg_in = nc.dram_tensor("wg", [D, NE], F32, kind="ExternalInput")
    w1_in = nc.dram_tensor("w1", [D, FF], F32, kind="ExternalInput")
    b1_in = nc.dram_tensor("b1", [32, 128], F32, kind="ExternalInput")
    w2_in = nc.dram_tensor("w2", [FF, D], F32, kind="ExternalInput")
    b2_in = nc.dram_tensor("b2", [1, D], F32, kind="ExternalInput")
    prevmask_in = nc.dram_tensor("prevmask", [8, 1], F32, kind="ExternalInput")
    sidxf_in = nc.dram_tensor("sidxf", [128, NT], F32, kind="ExternalInput")
    myslot_in = nc.dram_tensor("myslot", [128, 1], I32, kind="ExternalInput")

    out = nc.dram_tensor("out", [T, D], F32, kind="ExternalOutput")

    sp_x2 = nc.dram_tensor("sp_x2", [T, D], F32, kind="Internal")
    sp_x2n = nc.dram_tensor("sp_x2n", [T, D], F32, kind="Internal")
    sp_x2nh = nc.dram_tensor("sp_x2nh", [128, NT * D], BF16, kind="Internal")
    sp_y1 = nc.dram_tensor("sp_y1", [T, D], F32, kind="Internal")
    sp_y1h = nc.dram_tensor("sp_y1h", [128, NT * D], BF16, kind="Internal")
    cc_kin = nc.dram_tensor("cc_kin", [D, T], BF16, kind="Internal")
    cc_kout = nc.dram_tensor("cc_kout", [8 * D, T], BF16, kind="Internal")
    cc_vin = nc.dram_tensor("cc_vin", [T, D], BF16, kind="Internal")
    cc_vout = nc.dram_tensor("cc_vout", [8 * T, D], BF16, kind="Internal")
    cc_y1in = nc.dram_tensor("cc_y1in", [T, D], BF16, kind="Internal")
    cc_y1out = nc.dram_tensor("cc_y1out", [8 * T, D], BF16, kind="Internal")
    cc_cntin = nc.dram_tensor("cc_cntin", [1, NE], F32, kind="Internal")
    cc_cntout = nc.dram_tensor("cc_cntout", [8, NE], F32, kind="Internal")
    cc_invin = nc.dram_tensor("cc_invin", [16512, 1], F32, kind="Internal")
    cc_invout = nc.dram_tensor("cc_invout", [16512, 1], F32, kind="Internal")
    cc_iall = nc.dram_tensor("cc_iall", [8 * 16512, 1], F32, kind="Internal")
    cc_invw_t = [nc.dram_tensor(f"cc_invwt{tt}", [16512, 1], F32, kind="Internal")
                 for tt in range(NT)]
    w1bf = nc.dram_tensor("w1bf", [D, FF], BF16, kind="Internal")
    w2bf = nc.dram_tensor("w2bf", [FF, D], BF16, kind="Internal")
    cc_yein = nc.dram_tensor("cc_yein", [CAP, D], BF16, kind="Internal")
    cc_yeout = nc.dram_tensor("cc_yeout", [8 * CAP, D], BF16, kind="Internal")

    ident_bf = nc.inline_tensor(np.eye(128, dtype=ml_dtypes.bfloat16), "ident_bf")
    ones_bf = nc.inline_tensor(np.ones((128, 64), dtype=ml_dtypes.bfloat16), "ones_bf")
    stair_np = (np.arange(4096)[None, :] >= (2048 + np.arange(128))[:, None])
    stair = nc.inline_tensor(stair_np.astype(ml_dtypes.bfloat16), "stair")
    erow = nc.inline_tensor(np.arange(8, dtype=np.float32).reshape(1, 8), "erow")
    ident_f32 = nc.inline_tensor(np.eye(128, dtype=np.float32), "ident_f32")
    eps_col = nc.inline_tensor(np.full((128, 1), EPS, dtype=np.float32), "eps_col")
    selc_np = np.zeros((128, 16), dtype=ml_dtypes.bfloat16)
    for v_ in range(4):
        selc_np[:, v_ * 4 + v_] = 1.0
    selc = nc.inline_tensor(selc_np, "selc")
    basesc = nc.dram_tensor("basesc", [1, 8], F32, kind="Internal")

    identb = nc.alloc_sbuf_tensor("identb", [128, 128], BF16)
    identf = nc.alloc_sbuf_tensor("identf", [128, 128], F32)
    onescol = nc.alloc_sbuf_tensor("onescol", [128, 64], BF16)
    alpha_bc = nc.alloc_sbuf_tensor("alpha_bc", [128, 1], F32)
    oma_bc = nc.alloc_sbuf_tensor("oma_bc", [128, 1], F32)
    shiftc = nc.alloc_sbuf_tensor("shiftc", [128, 1], F32)
    epsc = nc.alloc_sbuf_tensor("epsc", [128, 1], F32)

    _es_x = ExitStack()
    xhat_ocomb = _es_x.enter_context(nc.sbuf_tensor("xhat_ocomb", [128, PAIRS * T], BF16))
    _es_a = ExitStack()
    q_feat = _es_a.enter_context(nc.sbuf_tensor("q_feat", [128, PAIRS * T], BF16))
    k_feat = _es_a.enter_context(nc.sbuf_tensor("k_feat", [128, PAIRS * T], BF16))
    kp_feat = _es_a.enter_context(nc.sbuf_tensor("kp_feat", [128, PAIRS * T], BF16))
    vpack = _es_a.enter_context(nc.sbuf_tensor("vpack", [128, NT * VP], BF16))

    xhat = xhat_ocomb

    with ExitStack() as _es:
        xload = _es.enter_context(nc.sbuf_tensor("xload", [128, 2 * D], F32))
        sqt = _es.enter_context(nc.sbuf_tensor("sqt", [128, D], F32))
        xhat_t = _es.enter_context(nc.sbuf_tensor("xhat_t", [128, D], BF16))
        st0 = _es.enter_context(nc.sbuf_tensor("st0", [128, 8], F32))
        trps = _es.enter_context(nc.psum_tensor("trps", [128, 2048], BF16))
        s_c0 = nc.alloc_semaphore("ph0_c")
        sxt0 = nc.alloc_semaphore("ph0_xt0")
        sxt1 = nc.alloc_semaphore("ph0_xt1")
        s_d0 = nc.alloc_semaphore("ph0_d")
        s_a0 = nc.alloc_semaphore("ph0_a")
        s_t0 = nc.alloc_semaphore("ph0_t")
        blk = _es.enter_context(nc.Block())

        def a0_sqrt(i):
            return 9 * i + 1

        def a0_copy(k):
            return 9 * (k // 8) + 2 + (k % 8)

        @blk.tensor
        def _(t):
            for i in range(NT):
                t.wait_ge(s_d0, 8 * i + 10)
                for cc in range(CC):
                    k = i * CC + cc
                    if k >= 2:
                        t.wait_ge(s_a0, a0_copy(k - 2))
                    t.transpose(trps[:, (k % 2) * 1024:(k % 2) * 1024 + 128],
                                xhat_t[:, cc * 128:(cc + 1) * 128],
                                identb[:, :]).then_inc(s_t0, 1)

        @blk.scalar
        def _(sc):
            for i in range(NT):
                sc.wait_ge(s_d0, 8 * i + 8)
                sc.activation(st0[:, 4:5], st0[:, 3:4], AF.Sqrt).then_inc(s_a0, 1)
                for cc in range(CC):
                    k = i * CC + cc
                    sc.wait_ge(s_t0, k + 1)
                    sc.copy(xhat[:, cc * T + i * 128: cc * T + (i + 1) * 128],
                            trps[:, (k % 2) * 1024:(k % 2) * 1024 + 128]).then_inc(s_a0, 1)

        @blk.vector
        def _(v):
            sxt = [sxt0, sxt1]
            ec = SelfChain(v, s_d0)
            v.wait_ge(s_c0, 80)
            ec.op("memset", shiftc[:, :], EXP_SHIFT)
            ec.op("tensor_scalar", oma_bc[:, :], alpha_bc[:, :], -1.0, 1.0, OP.mult, OP.add)
            for i in range(NT):
                xt = xload[:, (i % 2) * D:(i % 2 + 1) * D]
                v.wait_ge(sxt[i % 2], 16 * (i // 2 + 1))
                ec.op("tensor_reduce", st0[:, 0:1], xt, AXL.X, OP.add)
                ec.op("tensor_scalar", st0[:, 1:2], st0[:, 0:1], 1.0 / D, None, OP.mult)
                ec.op("tensor_scalar", xt, xt, st0[:, 1:2], None, OP.subtract)
                ec.op("tensor_tensor", sqt[:, :], xt, xt, OP.mult)
                ec.op("tensor_reduce", st0[:, 2:3], sqt[:, :], AXL.X, OP.add)
                ec.op("tensor_scalar", st0[:, 3:4], st0[:, 2:3], 1.0 / D, EPS,
                      OP.mult, OP.add)
                v.wait_ge(s_a0, a0_sqrt(i))
                ec.op("reciprocal", st0[:, 5:6], st0[:, 4:5])
                if i >= 1:
                    v.wait_ge(s_t0, 8 * i)
                ec.op("tensor_scalar", xhat_t[:, :], xt, st0[:, 5:6], None,
                      OP.mult)
            assert ec.n == 2 + 8 * NT

        @blk.sync
        def _(s):
            s.dma_start(out=identb[:, :], in_=ident_bf[:, :]).then_inc(s_c0, 16)
            s.dma_start(out=onescol[:, :], in_=ones_bf[:, :]).then_inc(s_c0, 16)
            s.dma_start(out=identf[:, :], in_=ident_f32[:, :]).then_inc(s_c0, 16)
            s.dma_start(out=alpha_bc[:, :],
                        in_=alpha_in.ap().broadcast_to([128, 1])).then_inc(s_c0, 16)
            s.dma_start(out=epsc[:, :], in_=eps_col[:, :]).then_inc(s_c0, 16)
            sxt = [sxt0, sxt1]
            for i in range(NT):
                if i >= 2:
                    s.wait_ge(s_d0, 8 * (i - 2) + 10)
                s.dma_start(out=xload[:, (i % 2) * D:(i % 2 + 1) * D],
                            in_=x_in[i * 128:(i + 1) * 128, :]).then_inc(sxt[i % 2], 16)

    def dump_sbuf_bf16(src_t, ncols):
        ov = out.ap().rearrange("(a b) d -> a (b d)", a=128)
        with ExitStack() as es2:
            sl = nc.alloc_semaphore("dump_l")
            blk2 = es2.enter_context(nc.Block())

            @blk2.gpsimd
            def _(g):
                for j in range(4):
                    w = ncols // 4
                    g.dma_start(out=ov[:, j * w:(j + 1) * w],
                                in_=src_t[:, j * w:(j + 1) * w]).then_inc(sl, 16)
                g.wait_ge(sl, 64)

    if upto == "ph0":
        dump_sbuf_bf16(xhat, CC * T)
        return nc

    with ExitStack() as _es:
        wrow = _es.enter_context(nc.sbuf_tensor("wrow", [128, 2 * D], BF16))
        wts = _es.enter_context(nc.sbuf_tensor("wts", [128, 2 * D], BF16))
        wvt = _es.enter_context(nc.sbuf_tensor("wvt", [128, CC * D], BF16))
        trp1 = _es.enter_context(nc.psum_tensor("trp1", [128, 4096], BF16))
        pp1 = _es.enter_context(nc.psum_tensor("pp1", [128, 512], F32))
        pp2 = _es.enter_context(nc.psum_tensor("pp2", [128, 512], F32))
        s_wcast = nc.alloc_semaphore("ph1_wcast")
        s_wl0 = nc.alloc_semaphore("ph1_wl0")
        s_wl1 = nc.alloc_semaphore("ph1_wl1")
        s_t1 = nc.alloc_semaphore("ph1_t")
        s_a1 = nc.alloc_semaphore("ph1_a")
        s_d1 = nc.alloc_semaphore("ph1_d")
        blk = _es.enter_context(nc.Block())

        loads = [("v", rc) for rc in range(CC)] + \
                [(w, pair) for pair in range(PAIRS) for w in ("q", "k", "kp")]
        wd_map = {"v": wv_in, "q": wq_in, "k": wk_in, "kp": wkp_in}
        dest_map = {"q": q_feat, "k": k_feat, "kp": kp_feat}

        def t1_tr(li, cc):
            if li < 8:
                return li * 8 + cc + 1
            return 64 + (li - 8) * 12 + cc + 1

        def t1_proj(npm):
            if npm < 96:
                li = 8 + npm // 4
                win = npm % 4
                return 64 + (li - 8) * 12 + 8 + win + 1
            return 64 + 288 + (npm - 96) + 1

        @blk.tensor
        def _(t):
            ntr = 0
            npm = 0
            for li, (w, rc) in enumerate(loads):
                t.wait_ge([s_wl0, s_wl1][li % 2], 16 * (li // 2 + 1))
                for cc in range(CC):
                    if ntr >= 4:
                        t.wait_ge(s_a1, ntr - 3)
                    t.transpose(trp1[:, (ntr % 4) * 1024:(ntr % 4) * 1024 + 128],
                                wrow[:, (li % 2) * D + cc * 128:(li % 2) * D + (cc + 1) * 128],
                                identb[:, :]).then_inc(s_t1, 1)
                    ntr += 1
                if w == "v":
                    continue
                t.wait_ge(s_a1, ntr)
                for win in range(4):
                    pb = [pp1, pp2][npm % 2]
                    if npm >= 2:
                        t.wait_ge(s_d1, npm - 1)
                    for cc in range(CC):
                        mm = t.matmul(pb[:, :],
                                      wts[:, (li % 2) * D + cc * 128:(li % 2) * D + (cc + 1) * 128],
                                      xhat[:, cc * T + win * 512:cc * T + (win + 1) * 512],
                                      start=(cc == 0), stop=(cc == CC - 1))
                    mm.then_inc(s_t1, 1)
                    npm += 1
            t.wait_ge(s_a1, 64)
            for tt in range(NT):
                for win in range(2):
                    pb = [pp1, pp2][npm % 2]
                    if npm >= 2:
                        t.wait_ge(s_d1, npm - 1)
                    for cc in range(CC):
                        mm = t.matmul(pb[:, :],
                                      xhat[:, cc * T + tt * 128:cc * T + (tt + 1) * 128],
                                      wvt[:, cc * D + win * 512:cc * D + (win + 1) * 512],
                                      start=(cc == 0), stop=(cc == CC - 1))
                    mm.then_inc(s_t1, 1)
                    npm += 1

        @blk.scalar
        def _(sc):
            ntr = 0
            for li, (w, rc) in enumerate(loads):
                for cc in range(CC):
                    sc.wait_ge(s_t1, t1_tr(li, cc))
                    if w == "v":
                        dst = wvt[:, cc * D + rc * 128: cc * D + rc * 128 + 128]
                    else:
                        dst = wts[:, (li % 2) * D + cc * 128:(li % 2) * D + (cc + 1) * 128]
                    sc.copy(dst, trp1[:, (ntr % 4) * 1024:(ntr % 4) * 1024 + 128]).then_inc(s_a1, 1)
                    ntr += 1

        @blk.vector
        def _(v):
            vp4 = vpack.ap().rearrange("p (t pr c) -> p t pr c", t=NT, pr=8, c=129)
            v.memset(vp4[:, :, :, 64:65], 1.0)
            npm = 0
            for li, (w, rc) in enumerate(loads):
                if w == "v":
                    continue
                pair = rc
                dest = dest_map[w]
                for win in range(4):
                    pb = [pp1, pp2][npm % 2]
                    v.wait_ge(s_t1, t1_proj(npm))
                    if w == "q":
                        v.tensor_scalar(dest[:, pair * T + win * 512: pair * T + (win + 1) * 512],
                                        pb[:, :], 0.125, None, OP.mult).then_inc(s_d1, 1)
                    else:
                        v.tensor_copy(dest[:, pair * T + win * 512: pair * T + (win + 1) * 512],
                                      pb[:, :]).then_inc(s_d1, 1)
                    npm += 1
            for tt in range(NT):
                for win in range(2):
                    pb = [pp1, pp2][npm % 2]
                    v.wait_ge(s_t1, t1_proj(npm))
                    for pw in range(4):
                        pair = win * 4 + pw
                        base = tt * VP + pair * 129
                        v.tensor_copy(vpack[:, base: base + 64], pb[:, pw * 128: pw * 128 + 64])
                        inst = v.tensor_copy(vpack[:, base + 65: base + 129],
                                             pb[:, pw * 128 + 64: pw * 128 + 128])
                    inst.then_inc(s_d1, 1)
                    npm += 1

        @blk.gpsimd
        def _(g):
            for li, (w, rc) in enumerate(loads):
                if li >= 2:
                    g.wait_ge(s_t1, t1_tr(li - 2, 7))
                g.dma_start(out=wrow[:, (li % 2) * D:(li % 2 + 1) * D],
                            in_=wd_map[w][rc * 128:(rc + 1) * 128, :]).then_inc(
                                [s_wl0, s_wl1][li % 2], 16)
            g.dma_start(out=w1bf[:, :], in_=w1_in[:, :]).then_inc(s_wcast, 16)
            g.dma_start(out=w2bf[:, :], in_=w2_in[:, :]).then_inc(s_wcast, 16)

    if upto == "ph1":
        dump_sbuf_bf16(q_feat, PAIRS * T)
        return nc

    o_comb = xhat_ocomb

    with ExitStack() as _es:
        pbuf = _es.enter_context(nc.sbuf_tensor("pbuf", [128, 3 * 1024], BF16))
        osb = _es.enter_context(nc.sbuf_tensor("osb", [128, 1024], F32))
        recrow = _es.enter_context(nc.sbuf_tensor("recrow", [128, 1024], F32))
        recrow_bf = _es.enter_context(nc.sbuf_tensor("recrow_bf", [128, 1024], BF16))
        sstd = _es.enter_context(nc.sbuf_tensor("sstd", [128, 4 * 512], F32))
        S0 = _es.enter_context(nc.psum_tensor("S0", [128, 1024], F32))
        S1 = _es.enter_context(nc.psum_tensor("S1", [128, 1024], F32))
        oeA = _es.enter_context(nc.psum_tensor("oeA", [65, 512], F32))
        ooA = _es.enter_context(nc.psum_tensor("ooA", [128, 512], F32))
        oeB = _es.enter_context(nc.psum_tensor("oeB", [65, 512], F32))
        ooB = _es.enter_context(nc.psum_tensor("ooB", [128, 512], F32))
        s_t2 = nc.alloc_semaphore("p2_t")
        s_a2 = nc.alloc_semaphore("p2_a")
        s_d2 = nc.alloc_semaphore("p2_d")
        blk = _es.enter_context(nc.Block())

        Sbufs = [S0, S1]
        oe = [oeA, oeB]
        oo = [ooA, ooB]

        def iter_idx(g):
            return g >> 7, (g >> 6) & 1, (g >> 4) & 3, g & 15

        pe_ev = {"s": {}, "o": {}, "rep": {}}

        def d2_start(gr):
            n = 0
            for g2 in range(gr):
                n += 8 if ((g2 >> 2) & 1) == 0 else 12
            return n

        def d2_od(gr, head):
            return d2_start(gr) + 3 * (head + 1)

        def d2_on(gr, head):
            br = (gr >> 2) & 1
            if br == 0:
                return d2_start(gr) + 7 + head
            return d2_start(gr) + 9 + 3 * head

        def d2_end(gr):
            return d2_start(gr + 1)

        @blk.tensor
        def _(t):
            nev = 0

            def emit_oside(go):
                nonlocal nev
                pair2, br2, qt2, kt2 = iter_idx(go)
                gr2 = go >> 4
                st = gr2 % 2
                t.wait_ge(s_a2, go + 1)
                if kt2 == 0 and gr2 >= 2:
                    t.wait_ge(s_d2, d2_end(gr2 - 2))
                vb = vpack[:, kt2 * VP + pair2 * 129: kt2 * VP + pair2 * 129 + 65]
                t.matmul(oe[st][:, :], vb, pbuf[:, (go % 3) * 1024:(go % 3) * 1024 + 512],
                         start=(kt2 == 0), stop=(kt2 == 15))
                vo = vpack[:, kt2 * VP + pair2 * 129 + 65: kt2 * VP + pair2 * 129 + 129]
                t.matmul(oo[st][64:128, :], vo,
                         pbuf[:, (go % 3) * 1024 + 512:(go % 3) * 1024 + 1024],
                         start=(kt2 == 0), stop=(kt2 == 15), tile_position=(0, 64))
                nev += 1
                t.matmul(oo[st][0:1, :], onescol[:, 0:1],
                         pbuf[:, (go % 3) * 1024 + 512:(go % 3) * 1024 + 1024],
                         start=(kt2 == 0), stop=(kt2 == 15)).then_inc(s_t2, 1)
                pe_ev["o"][go] = nev
                if kt2 == 2 and gr2 >= 1:
                    gp = gr2 - 1
                    t.wait_ge(s_d2, d2_od(gp, 0))
                    nev += 1
                    t.matmul(oe[gp % 2][0:64, :], onescol[64:65, :],
                             recrow_bf[64:65, 0:512], start=True, stop=True,
                             tile_position=(64, 0)).then_inc(s_t2, 1)
                    pe_ev["rep"][2 * gp] = nev
                if kt2 == 4 and gr2 >= 1:
                    gp = gr2 - 1
                    t.wait_ge(s_d2, d2_od(gp, 1))
                    nev += 1
                    t.matmul(oo[gp % 2][64:128, :], onescol[0:1, :],
                             recrow_bf[0:1, 512:1024], start=True, stop=True,
                             tile_position=(0, 64)).then_inc(s_t2, 1)
                    pe_ev["rep"][2 * gp + 1] = nev

            for g in range(NITER):
                pair, br, qt, kt = iter_idx(g)
                kc = k_feat if br == 0 else kp_feat
                if g >= 2:
                    t.wait_ge(s_a2, g - 1)
                sb = Sbufs[g % 2]
                t.matmul(sb[:, 0:512],
                         kc[0:64, pair * T + kt * 128: pair * T + (kt + 1) * 128],
                         q_feat[0:64, pair * T + qt * 512: pair * T + (qt + 1) * 512],
                         start=True, stop=True, tile_position=(0, 0))
                nev += 1
                t.matmul(sb[:, 512:1024],
                         kc[64:128, pair * T + kt * 128: pair * T + (kt + 1) * 128],
                         q_feat[64:128, pair * T + qt * 512: pair * T + (qt + 1) * 512],
                         start=True, stop=True, tile_position=(64, 0)).then_inc(s_t2, 1)
                pe_ev["s"][g] = nev
                if g >= 1:
                    emit_oside(g - 1)
            emit_oside(NITER - 1)
            for gp in (63,):
                t.wait_ge(s_d2, d2_od(gp, 0))
                nev += 1
                t.matmul(oe[gp % 2][0:64, :], onescol[64:65, :], recrow_bf[64:65, 0:512],
                         start=True, stop=True, tile_position=(64, 0)).then_inc(s_t2, 1)
                pe_ev["rep"][2 * gp] = nev
                t.wait_ge(s_d2, d2_od(gp, 1))
                nev += 1
                t.matmul(oo[gp % 2][64:128, :], onescol[0:1, :], recrow_bf[0:1, 512:1024],
                         start=True, stop=True, tile_position=(0, 64)).then_inc(s_t2, 1)
                pe_ev["rep"][2 * gp + 1] = nev

        @blk.scalar
        def _(sc):
            for g in range(NITER):
                sc.wait_ge(s_t2, pe_ev["s"][g])
                if g >= 3:
                    sc.wait_ge(s_t2, pe_ev["o"][g - 3])
                sc.activation(pbuf[:, (g % 3) * 1024:(g % 3) * 1024 + 1024],
                              Sbufs[g % 2][:, :], AF.Exp,
                              bias=shiftc[:, :]).then_inc(s_a2, 1)

        @blk.vector
        def _(v):
            ec = SelfChain(v, s_d2)
            for gr in range(64):
                pair, br, qt = gr >> 3, (gr >> 2) & 1, gr & 3
                st = gr % 2
                v.wait_ge(s_t2, pe_ev["o"][16 * gr + 15])
                ec.op("reciprocal", recrow[64:65, 0:512], oe[st][64:65, :])
                ec.op("tensor_copy", recrow_bf[64:65, 0:512], recrow[64:65, 0:512])
                ec.op("tensor_copy", osb[0:64, 0:512], oe[st][0:64, :])
                assert ec.n == d2_od(gr, 0)
                ec.op("reciprocal", recrow[0:1, 512:1024], oo[st][0:1, :])
                ec.op("tensor_copy", recrow_bf[0:1, 512:1024], recrow[0:1, 512:1024])
                ec.op("tensor_copy", osb[64:128, 512:1024], oo[st][64:128, :])
                assert ec.n == d2_od(gr, 1)
                v.wait_ge(s_t2, pe_ev["rep"][2 * gr])
                if br == 0:
                    ec.op("tensor_tensor", sstd[0:64, qt * 512:(qt + 1) * 512],
                          osb[0:64, 0:512], oe[st][0:64, :], OP.mult)
                else:
                    ec.op("tensor_tensor", osb[0:64, 0:512], osb[0:64, 0:512], oe[st][0:64, :], OP.mult)
                    ec.op("tensor_scalar", osb[0:64, 0:512], osb[0:64, 0:512], oma_bc[0:64, :], None, OP.mult)
                    ec.op("scalar_tensor_tensor",
                          o_comb[0:64, pair * T + qt * 512: pair * T + (qt + 1) * 512],
                          sstd[0:64, qt * 512:(qt + 1) * 512], alpha_bc[0:64, :],
                          osb[0:64, 0:512], OP.mult, OP.add)
                assert ec.n == d2_on(gr, 0)
                v.wait_ge(s_t2, pe_ev["rep"][2 * gr + 1])
                if br == 0:
                    ec.op("tensor_tensor", sstd[64:128, qt * 512:(qt + 1) * 512],
                          osb[64:128, 512:1024], oo[st][64:128, :], OP.mult)
                else:
                    ec.op("tensor_tensor", osb[64:128, 512:1024], osb[64:128, 512:1024],
                          oo[st][64:128, :], OP.mult)
                    ec.op("tensor_scalar", osb[64:128, 512:1024], osb[64:128, 512:1024],
                          oma_bc[64:128, :], None, OP.mult)
                    ec.op("scalar_tensor_tensor",
                          o_comb[64:128, pair * T + qt * 512: pair * T + (qt + 1) * 512],
                          sstd[64:128, qt * 512:(qt + 1) * 512], alpha_bc[64:128, :],
                          osb[64:128, 512:1024], OP.mult, OP.add)
                assert ec.n == d2_on(gr, 1) == d2_end(gr)

    if upto == "ph2":
        dump_sbuf_bf16(o_comb, PAIRS * T)
        _es_a.close()
        return nc

    _es_a.close()

    with ExitStack() as _es:
        wrow3 = _es.enter_context(nc.sbuf_tensor("wrow3", [128, 2 * D], BF16))
        wot = _es.enter_context(nc.sbuf_tensor("wot", [128, CC * D], BF16))
        xld3 = _es.enter_context(nc.sbuf_tensor("xld3", [128, 2 * D], F32))
        att3 = _es.enter_context(nc.sbuf_tensor("att3", [128, 2 * D], F32))
        scr3 = _es.enter_context(nc.sbuf_tensor("scr3", [128, D], F32))
        x1t = _es.enter_context(nc.sbuf_tensor("x1t", [128, 2 * D], F32))
        x2nt = _es.enter_context(nc.sbuf_tensor("x2nt", [128, 2 * D], F32))
        x2nb = _es.enter_context(nc.sbuf_tensor("x2nb", [128, D], BF16))
        x2nhs = _es.enter_context(nc.sbuf_tensor("x2nhs", [128, 2 * D], BF16))
        st3 = _es.enter_context(nc.sbuf_tensor("st3", [128, 8 * NT], F32))
        trp3 = _es.enter_context(nc.psum_tensor("trp3", [128, 4096], BF16))
        wop0 = _es.enter_context(nc.psum_tensor("wop0", [128, 512], F32))
        wop1 = _es.enter_context(nc.psum_tensor("wop1", [128, 512], F32))
        wop2 = _es.enter_context(nc.psum_tensor("wop2", [128, 512], F32))
        wop3 = _es.enter_context(nc.psum_tensor("wop3", [128, 512], F32))
        s_wl3a = nc.alloc_semaphore("p3_wla")
        s_wl3b = nc.alloc_semaphore("p3_wlb")
        s_sp3a = nc.alloc_semaphore("p3_spa")
        s_sp3b = nc.alloc_semaphore("p3_spb")
        s_xl3a = nc.alloc_semaphore("p3_xla")
        s_xl3b = nc.alloc_semaphore("p3_xlb")
        s_st3a = nc.alloc_semaphore("p3_sta")
        s_st3b = nc.alloc_semaphore("p3_stb")
        s_t3 = nc.alloc_semaphore("p3_t")
        s_a3 = nc.alloc_semaphore("p3_a")
        s_d3 = nc.alloc_semaphore("p3_d")
        blk = _es.enter_context(nc.Block())
        RSD = 1.0 / 32.0

        wops = [wop0, wop1, wop2, wop3]

        act_order = []
        for tt in range(NT):
            if tt >= 2:
                for cc in range(CC):
                    act_order.append(("ev", tt - 2, cc))
            for nm in ("acc1", "sqacc1", "sqrt1", "acc2", "sqacc2", "sqrt2"):
                act_order.append((nm, tt))
        for j in (NT - 2, NT - 1):
            for cc in range(CC):
                act_order.append(("ev", j, cc))
        a3ev = {k: 64 + i + 1 for i, k in enumerate(act_order)}

        def t3_proj(tt, win):
            n = 64
            for t2 in range(NT):
                n += 1
                if t2 == tt and win == 0:
                    return n
                n += 1
                if t2 == tt and win == 1:
                    return n
                if t2 >= 2:
                    n += 8
            raise AssertionError

        def t3_trx(j, cc):
            n = 64
            for t2 in range(NT):
                n += 2
                if t2 >= 2:
                    if t2 - 2 == j:
                        return n + cc + 1
                    n += 8
            for jj in (NT - 2, NT - 1):
                if jj == j:
                    return n + cc + 1
                n += 8
            raise AssertionError

        @blk.tensor
        def _(t):
            ntr = 0
            for rc in range(CC):
                t.wait_ge([s_wl3a, s_wl3b][rc % 2], 16 * (rc // 2 + 1))
                for cc in range(CC):
                    if ntr >= 4:
                        t.wait_ge(s_a3, ntr - 3)
                    t.transpose(trp3[:, (ntr % 4) * 1024:(ntr % 4) * 1024 + 128],
                                wrow3[:, (rc % 2) * D + cc * 128:(rc % 2) * D + (cc + 1) * 128],
                                identb[:, :]).then_inc(s_t3, 1)
                    ntr += 1

            def emit_tb(j):
                t.wait_ge(s_d3, 9 * j + 9)
                for cc in range(CC):
                    k = j * CC + cc
                    if k < 4:
                        t.wait_ge(s_a3, 64)
                    else:
                        kp = k - 4
                        t.wait_ge(s_a3, a3ev[("ev", kp // 8, kp % 8)])
                    t.transpose(trp3[:, (k % 4) * 1024:(k % 4) * 1024 + 128],
                                x2nb[:, cc * 128:(cc + 1) * 128],
                                identb[:, :]).then_inc(s_t3, 1)

            t.wait_ge(s_a3, 64)
            for tt in range(NT):
                for win in range(2):
                    pb = wops[(tt * 2 + win) % 4]
                    if tt >= 2:
                        t.wait_ge(s_d3, 9 * (tt - 2) + 2)
                    for cc in range(CC):
                        mm = t.matmul(pb[:, :],
                                      o_comb[:, cc * T + tt * 128:cc * T + (tt + 1) * 128],
                                      wot[:, cc * D + win * 512:cc * D + (win + 1) * 512],
                                      start=(cc == 0), stop=(cc == CC - 1))
                    mm.then_inc(s_t3, 1)
                if tt >= 2:
                    emit_tb(tt - 2)
            emit_tb(NT - 2)
            emit_tb(NT - 1)

        @blk.scalar
        def _(sc):
            na = 0

            def aop(wait_t=None, wait_d=None, wait_sem=None, copy=None, act=None):
                nonlocal na
                if wait_t is not None:
                    sc.wait_ge(s_t3, wait_t)
                if wait_d is not None:
                    sc.wait_ge(s_d3, wait_d)
                if wait_sem is not None:
                    sc.wait_ge(*wait_sem)
                if na:
                    sc.wait_ge(s_a3, na)
                inst = sc.copy(*copy) if copy else sc.activation(*act[0], **act[1])
                inst.then_inc(s_a3, 1)
                na += 1
            ntr = 0
            for rc in range(CC):
                for cc in range(CC):
                    aop(wait_t=ntr + 1,
                        copy=(wot[:, cc * D + rc * 128: cc * D + rc * 128 + 128],
                              trp3[:, (ntr % 4) * 1024:(ntr % 4) * 1024 + 128]))
                    ntr += 1
            for key in act_order:
                if key[0] == "ev":
                    _, j, cc = key
                    k = j * CC + cc
                    aop(wait_t=t3_trx(j, cc),
                        wait_sem=(([s_sp3a, s_sp3b][j % 2], 16 * (j // 2))
                                  if j >= 2 else None),
                        copy=(x2nhs[:, (j % 2) * D + cc * 128:(j % 2) * D + (cc + 1) * 128],
                              trp3[:, (k % 4) * 1024:(k % 4) * 1024 + 128]))
                    continue
                nm, tt = key
                sb = st3[:, 8 * tt:8 * tt + 8]
                at = att3[:, (tt % 2) * D:(tt % 2 + 1) * D]
                x1 = x1t[:, (tt % 2) * D:(tt % 2 + 1) * D]
                x2n = x2nt[:, (tt % 2) * D:(tt % 2 + 1) * D]
                if nm == "acc1":
                    aop(wait_d=9 * tt + 2,
                        act=((scr3[:, :], at, AF.Identity),
                             dict(bias=0.0, scale=1.0 / D, accum_out=sb[:, 0:1])))
                elif nm == "sqacc1":
                    aop(wait_d=9 * tt + 3,
                        act=((scr3[:, :], at, AF.Square),
                             dict(scale=RSD, accum_out=sb[:, 1:2])))
                elif nm == "sqrt1":
                    aop(act=((sb[:, 3:4], sb[:, 1:2], AF.Sqrt),
                             dict(bias=epsc[:, :])))
                elif nm == "acc2":
                    aop(wait_d=9 * tt + 5,
                        act=((scr3[:, :], x1, AF.Identity),
                             dict(bias=0.0, scale=1.0 / D, accum_out=sb[:, 5:6])))
                elif nm == "sqacc2":
                    aop(wait_d=9 * tt + 6,
                        act=((scr3[:, :], x2n, AF.Square),
                             dict(scale=RSD, accum_out=sb[:, 6:7])))
                else:
                    aop(act=((sb[:, 2:3], sb[:, 6:7], AF.Sqrt),
                             dict(bias=epsc[:, :])))
            assert na == 64 + len(act_order)

        @blk.vector
        def _(v):
            ec = SelfChain(v, s_d3)
            for tt in range(NT):
                sb = st3[:, 8 * tt:8 * tt + 8]
                at = att3[:, (tt % 2) * D:(tt % 2 + 1) * D]
                xt = xld3[:, (tt % 2) * D:(tt % 2 + 1) * D]
                x1 = x1t[:, (tt % 2) * D:(tt % 2 + 1) * D]
                x2n = x2nt[:, (tt % 2) * D:(tt % 2 + 1) * D]
                v.wait_ge(s_t3, t3_proj(tt, 0))
                if tt >= 2:
                    v.wait_ge([s_st3a, s_st3b][tt % 2], 32 * (tt // 2))
                    v.wait_ge(s_a3, a3ev[("sqacc1", tt - 2)])
                ec.op("tensor_copy", at[:, 0:512], wops[(2 * tt) % 4][:, :])
                v.wait_ge(s_t3, t3_proj(tt, 1))
                ec.op("tensor_copy", at[:, 512:1024], wops[(2 * tt + 1) % 4][:, :])
                v.wait_ge(s_a3, a3ev[("acc1", tt)])
                ec.op("tensor_scalar", at, at, sb[:, 0:1], None, OP.subtract)
                v.wait_ge(s_a3, a3ev[("sqrt1", tt)])
                ec.op("reciprocal", sb[:, 4:5], sb[:, 3:4])
                v.wait_ge([s_xl3a, s_xl3b][tt % 2], 16 * (tt // 2 + 1))
                ec.op("scalar_tensor_tensor", x1, at, sb[:, 4:5], xt, OP.mult, OP.add)
                v.wait_ge(s_a3, a3ev[("acc2", tt)])
                ec.op("tensor_scalar", x2n, x1, sb[:, 5:6], None, OP.subtract)
                v.wait_ge(s_a3, a3ev[("sqrt2", tt)])
                ec.op("reciprocal", sb[:, 7:8], sb[:, 2:3])
                ec.op("tensor_scalar", x2n, x2n, sb[:, 7:8], None, OP.mult)
                if tt >= 1:
                    v.wait_ge(s_t3, t3_trx(tt - 1, 7))
                ec.op("tensor_copy", x2nb[:, :], x2n)
                assert ec.n == 9 * tt + 9

        @blk.sync
        def _(s):
            for tt in range(NT):
                if tt >= 2:
                    s.wait_ge(s_d3, 9 * (tt - 2) + 5)
                s.dma_start(out=xld3[:, (tt % 2) * D:(tt % 2 + 1) * D],
                            in_=x_in[tt * 128:(tt + 1) * 128, :]).then_inc(
                                [s_xl3a, s_xl3b][tt % 2], 16)
                s.wait_ge(s_d3, 9 * tt + 5)
                s.dma_start(out=sp_x2[tt * 128:(tt + 1) * 128, :],
                            in_=x1t[:, (tt % 2) * D:(tt % 2 + 1) * D]).then_inc(
                                [s_st3a, s_st3b][tt % 2], 16)
                s.wait_ge(s_d3, 9 * tt + 9)
                s.dma_start(out=sp_x2n[tt * 128:(tt + 1) * 128, :],
                            in_=x2nt[:, (tt % 2) * D:(tt % 2 + 1) * D]).then_inc(
                                [s_st3a, s_st3b][tt % 2], 16)
            s.wait_ge(s_st3a, 32 * 8)
            s.wait_ge(s_st3b, 32 * 8)

        @blk.gpsimd
        def _(g):
            for rc in range(CC):
                if rc >= 2:
                    g.wait_ge(s_t3, 8 * (rc - 1))
                g.dma_start(out=wrow3[:, (rc % 2) * D:(rc % 2 + 1) * D],
                            in_=wo_in[rc * 128:(rc + 1) * 128, :]).then_inc(
                                [s_wl3a, s_wl3b][rc % 2], 16)
            for tt in range(NT):
                g.wait_ge(s_a3, a3ev[("ev", tt, 7)])
                g.dma_start(out=sp_x2nh[:, tt * D:(tt + 1) * D],
                            in_=x2nhs[:, (tt % 2) * D:(tt % 2 + 1) * D]).then_inc(
                                [s_sp3a, s_sp3b][tt % 2], 16)
            g.wait_ge(s_sp3a, 16 * 8)
            g.wait_ge(s_sp3b, 16 * 8)

    _es_x.close()

    if upto == "p3":
        s_l = nc.alloc_semaphore("fx_l")
        s_st = nc.alloc_semaphore("fx_s")
        with (
            nc.sbuf_tensor("ot", [128, D], F32) as ot,
            nc.Block() as blk,
        ):
            @blk.gpsimd
            def _(g):
                for tt in range(NT):
                    g.dma_start(out=ot[:, :], in_=sp_x2[tt * 128:(tt + 1) * 128, :]).then_inc(s_l, 16)
                    g.wait_ge(s_l, 16 * (tt + 1))
                    g.dma_start(out=out[tt * 128:(tt + 1) * 128, :], in_=ot[:, :]).then_inc(s_st, 16)
                    g.wait_ge(s_st, 16 * (tt + 1))
        return nc


    _es_r = ExitStack()
    dst_i32 = _es_r.enter_context(nc.sbuf_tensor("dst_i32", [128, NT], I32))
    gval = _es_r.enter_context(nc.sbuf_tensor("gval", [128, NT], F32))
    dstc_i32 = _es_r.enter_context(nc.sbuf_tensor("dstc_i32", [128, NT], I32))
    _es_g = ExitStack()
    G_sb = _es_g.enter_context(nc.sbuf_tensor("G_sb", [128, NT * 8], F32))
    _es_o = ExitStack()
    o_tok = _es_o.enter_context(nc.sbuf_tensor("o_tok", [128, NT * D], F32))
    _es_q = ExitStack()
    qf2 = _es_q.enter_context(nc.sbuf_tensor("qf2", [128, CC * T], BF16))
    s_ag4 = nc.alloc_semaphore("p4_ag")

    with ExitStack() as _es:
        wrow4 = _es.enter_context(nc.sbuf_tensor("wrow4", [128, 2 * D], BF16))
        wint = _es.enter_context(nc.sbuf_tensor("wint", [128, CC * 3072], BF16))
        x2h = _es.enter_context(nc.sbuf_tensor("x2h", [128, CC * T], BF16))
        kstg = _es.enter_context(nc.sbuf_tensor("kstg", [128, T], BF16))
        vstg = _es.enter_context(nc.sbuf_tensor("vstg", [128, D], BF16))
        binq_sb = _es.enter_context(nc.sbuf_tensor("binq_sb", [24, 128], F32))
        binqT = _es.enter_context(nc.sbuf_tensor("binqT", [128, 24], F32))
        bvbc = _es.enter_context(nc.sbuf_tensor("bvbc", [128, D], F32))
        trp4 = _es.enter_context(nc.psum_tensor("trp4", [128, 4096], BF16))
        pq1 = _es.enter_context(nc.psum_tensor("pq1", [128, 512], F32))
        pq2 = _es.enter_context(nc.psum_tensor("pq2", [128, 512], F32))
        s_wl4 = [nc.alloc_semaphore("p4_wla"), nc.alloc_semaphore("p4_wlb")]
        s_x4 = nc.alloc_semaphore("p4_x")
        s_t4 = nc.alloc_semaphore("p4_t")
        s_a4 = nc.alloc_semaphore("p4_a")
        s_d4 = nc.alloc_semaphore("p4_d")
        s_ks = nc.alloc_semaphore("p4_ks")
        s_vs = nc.alloc_semaphore("p4_vs")
        blk = _es.enter_context(nc.Block())

        @blk.tensor
        def _(t):
            ntr = 0
            for rc in range(24):
                t.wait_ge(s_wl4[rc % 2], 16 * (rc // 2 + 1))
                for cc in range(CC):
                    if ntr >= 4:
                        t.wait_ge(s_a4, ntr - 3)
                    t.transpose(trp4[:, (ntr % 4) * 1024:(ntr % 4) * 1024 + 128],
                                wrow4[:, (rc % 2) * D + cc * 128:(rc % 2) * D + (cc + 1) * 128],
                                identb[:, :]).then_inc(s_t4, 1)
                    ntr += 1
            t.wait_ge(s_x4, 160)
            t.wait_ge(s_a4, 192)
            t.transpose(trp4.bitcast(F32)[:, 1536:1560], binq_sb[0:24, :],
                        identf[0:24, 0:24]).then_inc(s_t4, 1)
            npm = 0
            for kind in ("k", "q"):
                for dj in range(8):
                    for win in range(4):
                        pb = [pq1, pq2][npm % 2]
                        if npm >= 2:
                            t.wait_ge(s_d4, npm - 1)
                        joff = (0 if kind == "q" else 1024) + dj * 128
                        for cc in range(CC):
                            mm = t.matmul(pb[:, :],
                                          wint[:, cc * 3072 + joff: cc * 3072 + joff + 128],
                                          x2h[:, cc * T + win * 512: cc * T + (win + 1) * 512],
                                          start=(cc == 0), stop=(cc == CC - 1))
                        mm.then_inc(s_t4, 1)
                        npm += 1
            for tt in range(NT):
                for win in range(2):
                    pb = [pq1, pq2][npm % 2]
                    if npm >= 2:
                        t.wait_ge(s_d4, npm - 1)
                    for cc in range(CC):
                        mm = t.matmul(pb[:, :],
                                      x2h[:, cc * T + tt * 128: cc * T + (tt + 1) * 128],
                                      wint[:, cc * 3072 + 2048 + win * 512:
                                           cc * 3072 + 2048 + (win + 1) * 512],
                                      start=(cc == 0), stop=(cc == CC - 1))
                    mm.then_inc(s_t4, 1)
                    npm += 1

        @blk.scalar
        def _(sc):
            ntr = 0
            for rc in range(24):
                for cc in range(CC):
                    sc.wait_ge(s_t4, ntr + 1)
                    sc.copy(wint[:, cc * 3072 + rc * 128: cc * 3072 + rc * 128 + 128],
                            trp4[:, (ntr % 4) * 1024:(ntr % 4) * 1024 + 128]).then_inc(s_a4, 1)
                    ntr += 1
            sc.wait_ge(s_t4, 193)
            sc.copy(binqT[:, :], trp4.bitcast(F32)[:, 1536:1560]).then_inc(s_a4, 1)

        @blk.vector
        def _(v):
            ec = SelfChain(v, s_d4)
            v.wait_ge(s_a4, 193)
            npm = 0
            for kind in ("k", "q"):
                for dj in range(8):
                    for win in range(4):
                        pb = [pq1, pq2][npm % 2]
                        v.wait_ge(s_t4, 193 + npm + 1)
                        if kind == "q":
                            ec.op("tensor_scalar",
                                  qf2[:, dj * T + win * 512: dj * T + (win + 1) * 512],
                                  pb[:, :], binqT[:, dj:dj + 1], 1.0 / 16.0,
                                  OP.add, OP.mult)
                        else:
                            if win == 0 and dj >= 1:
                                v.wait_ge(s_ks, 16 * dj)
                            ec.op("tensor_scalar",
                                  kstg[:, win * 512:(win + 1) * 512],
                                  pb[:, :], binqT[:, 8 + dj:9 + dj], None, OP.add)
                        npm += 1
            for tt in range(NT):
                for win in range(2):
                    pb = [pq1, pq2][npm % 2]
                    v.wait_ge(s_t4, 193 + npm + 1)
                    if win == 0 and tt >= 1:
                        v.wait_ge(s_vs, 16 * tt)
                    ec.op("tensor_tensor",
                          vstg[:, win * 512:(win + 1) * 512],
                          pb[:, :], bvbc[:, win * 512:(win + 1) * 512], OP.add)
                    npm += 1
            assert ec.n == 96

        @blk.sync
        def _(s):
            for cc in range(CC):
                ap_in = bass.AP(sp_x2nh, cc * 128, [[NT * D, 128], [D, NT], [1, 128]])
                s.dma_start(out=x2h.ap().rearrange("p (c t j) -> p c t j", c=CC, t=NT, j=128)[:, cc],
                            in_=ap_in).then_inc(s_x4, 16)
            s.dma_start(out=binq_sb[:, :], in_=binq_in[:, :]).then_inc(s_x4, 16)
            s.dma_start(out=bvbc[:, :],
                        in_=bass.AP(binq_in, 2048, [[0, 128], [1, D]])).then_inc(s_x4, 16)
            for dj in range(8):
                s.wait_ge(s_d4, 4 * dj + 4)
                s.dma_start(out=cc_kin[dj * 128:(dj + 1) * 128, :],
                            in_=kstg[:, :]).then_inc(s_ks, 16)
            for tt in range(NT):
                s.wait_ge(s_d4, 66 + 2 * tt)
                s.dma_start(out=cc_vin[tt * 128:(tt + 1) * 128, :],
                            in_=vstg[:, :]).then_inc(s_vs, 16)

        @blk.gpsimd
        def _(g):
            for rc in range(24):
                if rc >= 2:
                    g.wait_ge(s_t4, 8 * (rc - 1))
                g.dma_start(out=wrow4[:, (rc % 2) * D:(rc % 2 + 1) * D],
                            in_=win_in[rc * 128:(rc + 1) * 128, :]).then_inc(s_wl4[rc % 2], 16)
            g.wait_ge(s_ks, 128)
            g.collective_compute("AllGather", OP.bypass, replica_groups=RG,
                                 ins=[cc_kin.ap().opt()],
                                 outs=[cc_kout.ap().opt()]).then_inc(s_ag4, 1)
            g.wait_ge(s_vs, 256)
            g.collective_compute("AllGather", OP.bypass, replica_groups=RG,
                                 ins=[cc_vin.ap().opt()],
                                 outs=[cc_vout.ap().opt()]).then_inc(s_ag4, 1)

    _es_s = ExitStack()
    selcb = _es_s.enter_context(nc.sbuf_tensor("selcb", [128, 16], BF16))
    stok = _es_s.enter_context(nc.sbuf_tensor("stok", [128, NT * 32], F32))
    etok = _es_s.enter_context(nc.sbuf_tensor("etok", [128, NT * 32], F32))
    vmb = _es_s.enter_context(nc.sbuf_tensor("vmb", [128, 3 * D], F32))
    s_t5 = nc.alloc_semaphore("p5_t")
    s_a5 = nc.alloc_semaphore("p5_a")
    s_d5 = nc.alloc_semaphore("p5_d")
    s_vm = [nc.alloc_semaphore(f"p5_vm{i}") for i in range(3)]

    with ExitStack() as _es:
        kmb = _es.enter_context(nc.sbuf_tensor("kmb", [128, 2 * T], BF16))
        prodb = _es.enter_context(nc.sbuf_tensor("prodb", [128, 2 * T], BF16))
        ssb = _es.enter_context(nc.sbuf_tensor("ssb", [4, 2 * T], F32))
        smax = _es.enter_context(nc.sbuf_tensor("smax", [128, 128], F32))
        Sps = _es.enter_context(nc.psum_tensor("Sps", [4, 2048], F32))
        trp5 = _es.enter_context(nc.psum_tensor("trp5", [128, 1024], F32))
        s_c5 = nc.alloc_semaphore("p5_c")
        s_km = [nc.alloc_semaphore("p5_km0"), nc.alloc_semaphore("p5_km1")]
        blk = _es.enter_context(nc.Block())

        def t5_sc(m, cc):
            return 24 * m + cc + 1

        def t5_tr(m, tt):
            return 24 * m + 8 + tt + 1

        def a5_ssb(m):
            return 17 * m + 1

        def a5_st(m, tt):
            return 17 * m + 2 + tt

        def d5_prod(m, cc):
            return 8 * m + cc + 1

        def stok_sl(tt, m):
            return bass.AP(stok, tt * 32 + m, [[NT * 32, 128], [8, 4]])

        def stok_m(m):
            return bass.AP(stok, m, [[NT * 32, 128], [8, 64]])

        def etok_m(m):
            return bass.AP(etok, m, [[NT * 32, 128], [8, 64]])

        @blk.tensor
        def _(t):
            t.wait_ge(s_c5, 16)
            for m in range(8):
                for cc in range(CC):
                    t.wait_ge(s_d5, d5_prod(m, cc))
                    if cc == 0 and m >= 1:
                        t.wait_ge(s_a5, a5_ssb(m - 1))
                    for win in range(4):
                        mm = t.matmul(Sps[0:4, win * 512:(win + 1) * 512],
                                      selcb[:, (cc // 2) * 4:(cc // 2) * 4 + 4],
                                      prodb[:, (cc % 2) * T + win * 512:
                                            (cc % 2) * T + (win + 1) * 512],
                                      start=(cc == 0), stop=(cc == CC - 1))
                    mm.then_inc(s_t5, 1)
                t.wait_ge(s_a5, a5_ssb(m))
                for tt in range(NT):
                    k = 16 * m + tt
                    if k >= 2:
                        t.wait_ge(s_a5, a5_st((k - 2) // 16, (k - 2) % 16))
                    t.transpose(trp5[:, (k % 2) * 512:(k % 2) * 512 + 4],
                                ssb[0:4, (m % 2) * T + tt * 128:(m % 2) * T + (tt + 1) * 128],
                                identf[0:4, 0:4]).then_inc(s_t5, 1)

        @blk.scalar
        def _(sc):
            for m in range(8):
                sc.wait_ge(s_t5, t5_sc(m, 7))
                if m >= 2:
                    sc.wait_ge(s_t5, t5_tr(m - 2, 15))
                sc.copy(ssb[0:4, (m % 2) * T:(m % 2 + 1) * T], Sps[0:4, :]).then_inc(s_a5, 1)
                for tt in range(NT):
                    sc.wait_ge(s_t5, t5_tr(m, tt))
                    k = 16 * m + tt
                    sc.copy(stok_sl(tt, m),
                            trp5[:, (k % 2) * 512:(k % 2) * 512 + 4]).then_inc(s_a5, 1)
            sc.wait_ge(s_d5, 80)
            sc.activation(etok[:, :], stok[:, :], AF.Exp).then_inc(s_a5, 1)

        @blk.vector
        def _(v):
            ec = SelfChain(v, s_d5)
            for m in range(8):
                for cc in range(CC):
                    j = 8 * m + cc
                    v.wait_ge(s_km[j % 2], 16 * (j // 2 + 1))
                    if j >= 2:
                        v.wait_ge(s_t5, t5_sc((j - 2) // 8, (j - 2) % 8))
                    ec.op("tensor_tensor",
                          prodb[:, (cc % 2) * T:(cc % 2 + 1) * T],
                          qf2[:, cc * T:(cc + 1) * T],
                          kmb[:, (j % 2) * T:(j % 2 + 1) * T], OP.mult)
            assert ec.n == 64
            v.wait_ge(s_a5, 136)
            ec.op("tensor_copy", smax[:, 0:64], stok_m(0))
            for m in range(1, 8):
                ec.op("tensor_tensor", smax[:, 0:64], smax[:, 0:64],
                      stok_m(m), OP.max)
            for m in range(8):
                ec.op("tensor_tensor", stok_m(m), stok_m(m),
                      smax[:, 0:64], OP.subtract)
            assert ec.n == 80
            v.wait_ge(s_a5, 137)
            ec.op("tensor_copy", smax[:, 64:128], etok_m(0))
            for m in range(1, 8):
                ec.op("tensor_tensor", smax[:, 64:128], smax[:, 64:128],
                      etok_m(m), OP.add)
            ec.op("reciprocal", smax[:, 0:64], smax[:, 64:128])
            for m in range(8):
                ec.op("tensor_tensor", etok_m(m), etok_m(m),
                      smax[:, 0:64], OP.mult)
            assert ec.n == 97

        @blk.sync
        def _(s):
            s.dma_start(out=selcb[:, :], in_=selc[:, :]).then_inc(s_c5, 16)
            s.wait_ge(s_ag4, 1)
            for m in range(8):
                for cc in range(CC):
                    j = 8 * m + cc
                    if j >= 2:
                        s.wait_ge(s_d5, j - 1)
                    s.dma_start(out=kmb[:, (j % 2) * T:(j % 2 + 1) * T],
                                in_=cc_kout[m * D + cc * 128: m * D + (cc + 1) * 128, :]
                                ).then_inc(s_km[j % 2], 16)

    s_ag6 = nc.alloc_semaphore("p6_ag")

    with ExitStack() as _es:
        wrow6 = _es.enter_context(nc.sbuf_tensor("wrow6", [128, 2 * D], BF16))
        wouT = _es.enter_context(nc.sbuf_tensor("wouT", [128, CC * D], BF16))
        ob16 = _es.enter_context(nc.sbuf_tensor("ob16", [128, 2 * D], BF16))
        ofs = _es.enter_context(nc.sbuf_tensor("ofs", [128, 2 * D], BF16))
        x2nld = _es.enter_context(nc.sbuf_tensor("x2nld", [128, 2 * D], F32))
        att6 = _es.enter_context(nc.sbuf_tensor("att6", [128, D], F32))
        scr6 = _es.enter_context(nc.sbuf_tensor("scr6", [128, D], F32))
        y1t = _es.enter_context(nc.sbuf_tensor("y1t", [128, 2 * D], F32))
        y1b16 = _es.enter_context(nc.sbuf_tensor("y1b16", [128, 2 * D], BF16))
        y1fs = _es.enter_context(nc.sbuf_tensor("y1fs", [128, 2 * D], F32))
        st6 = _es.enter_context(nc.sbuf_tensor("st6", [128, 8 * NT], F32))
        boutbc = _es.enter_context(nc.sbuf_tensor("boutbc", [128, D], F32))
        g1v = _es.enter_context(nc.sbuf_tensor("g1v", [128, D], F32))
        b1v = _es.enter_context(nc.sbuf_tensor("b1v", [128, D], F32))
        wgb = _es.enter_context(nc.sbuf_tensor("wgb", [128, 64], F32))
        trp6 = _es.enter_context(nc.psum_tensor("trp6", [128, 4096], BF16))
        wopA = _es.enter_context(nc.psum_tensor("wopA", [128, 512], F32))
        wopB = _es.enter_context(nc.psum_tensor("wopB", [128, 512], F32))
        Gps = _es.enter_context(nc.psum_tensor("Gps", [128, 1024], F32))
        s_wl6 = [nc.alloc_semaphore("p6_wla"), nc.alloc_semaphore("p6_wlb")]
        s_xl6 = [nc.alloc_semaphore("p6_xla"), nc.alloc_semaphore("p6_xlb")]
        s_y1s = [nc.alloc_semaphore("p6_y1sa"), nc.alloc_semaphore("p6_y1sb")]
        s_y1c = [nc.alloc_semaphore("p6_y1ca"), nc.alloc_semaphore("p6_y1cb")]
        s_cg = nc.alloc_semaphore("p6_cg")
        s_t6 = nc.alloc_semaphore("p6_t")
        s_a6 = nc.alloc_semaphore("p6_a")
        blk = _es.enter_context(nc.Block())
        RSD = 1.0 / 32.0

        dve_order = []
        for tt in range(NT):
            for m in range(8):
                for h in range(4):
                    dve_order.append(("cmb", tt, 8 * m + h * 2))
        dve_order = []
        for tt in range(NT):
            for m in range(8):
                for h in range(4):
                    dve_order.append(("cmb", tt, m, h))
            if tt >= 2:
                for j in range(1, 9):
                    dve_order.append(("ln", tt - 2, j))
        for jt in (NT - 2, NT - 1):
            for j in range(1, 9):
                dve_order.append(("ln", jt, j))
        d5ev = {k: 97 + i + 1 for i, k in enumerate(dve_order)}

        def cmb_end(tt):
            return d5ev[("cmb", tt, 7, 3)]

        def t6_tr(tt, cc):
            return 64 + 19 * tt + cc + 1

        def t6_pr(tt, win):
            return 64 + 19 * tt + 9 + win

        def t6_ftr(tt, cc):
            return 64 + 19 * tt + 11 + cc

        def t6_g(tt):
            return 64 + 19 * tt + 19

        def a6(tt, k):
            return 64 + 21 * tt + k

        @blk.vector
        def _(v):
            ec = SelfChain(v, s_d5, n0=97)
            v.wait_ge(s_cg, 64)
            for key in dve_order:
                if key[0] == "cmb":
                    _, tt, m, h = key
                    q = tt * 8 + m
                    if h == 0:
                        v.wait_ge(s_vm[q % 3], 16 * (q // 3 + 1))
                    col = etok[:, tt * 32 + h * 8 + m: tt * 32 + h * 8 + m + 1]
                    osl = o_tok[:, tt * D + h * 256: tt * D + (h + 1) * 256]
                    vsl = vmb[:, (q % 3) * D + h * 256:(q % 3) * D + (h + 1) * 256]
                    if m == 0:
                        ec.op("tensor_scalar", osl, vsl, col, None, OP.mult)
                    else:
                        ec.op("scalar_tensor_tensor", osl, vsl, col, osl,
                              OP.mult, OP.add)
                    continue
                _, tt, j = key
                sb = st6[:, 8 * tt:8 * tt + 8]
                at = att6[:, :]
                xl = x2nld[:, (tt % 2) * D:(tt % 2 + 1) * D]
                y1 = y1t[:, (tt % 2) * D:(tt % 2 + 1) * D]
                if j == 1:
                    v.wait_ge(s_t6, t6_pr(tt, 0))
                    v.wait_ge(s_xl6[tt % 2], 16 * (tt // 2 + 1))
                    if tt >= 1:
                        v.wait_ge(s_a6, a6(tt - 1, 11))
                    ec.op("tensor_tensor", at[:, 0:512], wopA[:, :], xl[:, 0:512], OP.add)
                elif j == 2:
                    v.wait_ge(s_t6, t6_pr(tt, 1))
                    ec.op("tensor_tensor", at[:, 512:1024], wopB[:, :], xl[:, 512:1024], OP.add)
                elif j == 3:
                    ec.op("tensor_tensor", at, at, boutbc[:, :], OP.add)
                elif j == 4:
                    v.wait_ge(s_a6, a6(tt, 10))
                    ec.op("tensor_scalar", at, at, sb[:, 0:1], None, OP.subtract)
                elif j == 5:
                    v.wait_ge(s_a6, a6(tt, 12))
                    ec.op("reciprocal", sb[:, 4:5], sb[:, 3:4])
                elif j == 6:
                    if tt >= 2:
                        v.wait_ge(s_t6, t6_ftr(tt - 2, 7))
                        v.wait_ge(s_y1s[tt % 2], 16 * (tt // 2))
                    ec.op("scalar_tensor_tensor", y1, at, sb[:, 4:5], g1v[:, :],
                          OP.mult, OP.mult)
                elif j == 7:
                    ec.op("tensor_tensor", y1, y1, b1v[:, :], OP.add)
                else:
                    if tt >= 2:
                        v.wait_ge(s_y1c[tt % 2], 16 * (tt // 2))
                    ec.op("tensor_copy", y1b16[:, (tt % 2) * D:(tt % 2 + 1) * D], y1)
            assert ec.n == 97 + len(dve_order)

        @blk.tensor
        def _(t):
            ntr = 0
            for rc in range(CC):
                t.wait_ge(s_wl6[rc % 2], 16 * (rc // 2 + 1))
                for cc in range(CC):
                    if ntr >= 4:
                        t.wait_ge(s_a6, ntr - 3)
                    t.transpose(trp6[:, (ntr % 4) * 1024:(ntr % 4) * 1024 + 128],
                                wrow6[:, (rc % 2) * D + cc * 128:(rc % 2) * D + (cc + 1) * 128],
                                identb[:, :]).then_inc(s_t6, 1)
                    ntr += 1
            t.wait_ge(s_a6, 64)
            t.wait_ge(s_cg, 64)
            for tt in range(NT):
                t.wait_ge(s_a6, a6(tt, 1))
                for cc in range(CC):
                    k = 8 * tt + cc
                    if k >= 2:
                        t.wait_ge(s_a6, a6((k - 2) // 8, 2 + (k - 2) % 8))
                    t.transpose(trp6[:, (k % 2) * 1024:(k % 2) * 1024 + 128],
                                ob16[:, (tt % 2) * D + cc * 128:(tt % 2) * D + (cc + 1) * 128],
                                identb[:, :]).then_inc(s_t6, 1)
                t.wait_ge(s_a6, a6(tt, 9))
                for win in range(2):
                    if tt >= 1:
                        t.wait_ge(s_d5, d5ev[("ln", tt - 1, win + 1)])
                    pb = [wopA, wopB][win]
                    for cc in range(CC):
                        mm = t.matmul(pb[:, :],
                                      ofs[:, (tt % 2) * D + cc * 128:(tt % 2) * D + (cc + 1) * 128],
                                      wouT[:, cc * D + win * 512: cc * D + (win + 1) * 512],
                                      start=(cc == 0), stop=(cc == CC - 1))
                    mm.then_inc(s_t6, 1)
                t.wait_ge(s_d5, d5ev[("ln", tt, 7)])
                for cc in range(CC):
                    k = 8 * tt + cc
                    if k >= 2:
                        t.wait_ge(s_a6, a6((k - 2) // 8, 13 + (k - 2) % 8))
                    t.transpose(trp6.bitcast(F32)[:, 1024 + (k % 2) * 512:
                                                  1024 + (k % 2) * 512 + 128],
                                y1t[:, (tt % 2) * D + cc * 128:(tt % 2) * D + (cc + 1) * 128],
                                identf[:, :]).then_inc(s_t6, 1)
                t.wait_ge(s_a6, a6(tt, 20))
                if tt >= 2:
                    t.wait_ge(s_a6, a6(tt - 2, 21))
                for cc in range(CC):
                    mm = t.matmul(Gps[:, (tt % 2) * 512:(tt % 2) * 512 + 8],
                                  y1fs[:, (tt % 2) * D + cc * 128:(tt % 2) * D + (cc + 1) * 128],
                                  wgb[:, cc * 8:(cc + 1) * 8],
                                  start=(cc == 0), stop=(cc == CC - 1))
                mm.then_inc(s_t6, 1)

        @blk.scalar
        def _(sc):
            na = 0

            def aop(wait_t=None, wait_d=None, copy=None, act=None):
                nonlocal na
                if wait_t is not None:
                    sc.wait_ge(s_t6, wait_t)
                if wait_d is not None:
                    sc.wait_ge(s_d5, wait_d)
                if na:
                    sc.wait_ge(s_a6, na)
                inst = sc.copy(*copy) if copy else sc.activation(*act[0], **act[1])
                inst.then_inc(s_a6, 1)
                na += 1
            ntr = 0
            for rc in range(CC):
                for cc in range(CC):
                    aop(wait_t=ntr + 1,
                        copy=(wouT[:, cc * D + rc * 128: cc * D + rc * 128 + 128],
                              trp6[:, (ntr % 4) * 1024:(ntr % 4) * 1024 + 128]))
                    ntr += 1
            for tt in range(NT):
                sb = st6[:, 8 * tt:8 * tt + 8]
                aop(wait_t=(t6_tr(tt - 2, 7) if tt >= 2 else None),
                    wait_d=cmb_end(tt),
                    copy=(ob16[:, (tt % 2) * D:(tt % 2 + 1) * D],
                          o_tok[:, tt * D:(tt + 1) * D]))
                for cc in range(CC):
                    aop(wait_t=max(t6_tr(tt, cc),
                                   t6_pr(tt - 2, 1) if (cc == 0 and tt >= 2) else 0),
                        copy=(ofs[:, (tt % 2) * D + cc * 128:(tt % 2) * D + (cc + 1) * 128],
                              trp6[:, ((8 * tt + cc) % 2) * 1024:((8 * tt + cc) % 2) * 1024 + 128]))
                aop(wait_d=d5ev[("ln", tt, 3)],
                    act=((scr6[:, :], att6[:, :], AF.Identity),
                         dict(bias=0.0, scale=1.0 / D, accum_out=sb[:, 0:1])))
                aop(wait_d=d5ev[("ln", tt, 4)],
                    act=((scr6[:, :], att6[:, :], AF.Square),
                         dict(scale=RSD, accum_out=sb[:, 1:2])))
                aop(act=((sb[:, 3:4], sb[:, 1:2], AF.Sqrt),
                         dict(bias=epsc[:, :])))
                for cc in range(CC):
                    aop(wait_t=max(t6_ftr(tt, cc),
                                   t6_g(tt - 2) if (cc == 0 and tt >= 2) else 0),
                        copy=(y1fs[:, (tt % 2) * D + cc * 128:(tt % 2) * D + (cc + 1) * 128],
                              trp6.bitcast(F32)[:, 1024 + ((8 * tt + cc) % 2) * 512:
                                                1024 + ((8 * tt + cc) % 2) * 512 + 128]))
                aop(wait_t=t6_g(tt),
                    copy=(G_sb[:, 8 * tt:8 * tt + 8],
                          Gps[:, (tt % 2) * 512:(tt % 2) * 512 + 8]))
            assert na == 64 + 21 * NT

        @blk.sync
        def _(s):
            for tt in range(NT):
                if tt >= 2:
                    s.wait_ge(s_d5, d5ev[("ln", tt - 2, 2)])
                s.dma_start(out=x2nld[:, (tt % 2) * D:(tt % 2 + 1) * D],
                            in_=sp_x2n[tt * 128:(tt + 1) * 128, :]).then_inc(s_xl6[tt % 2], 16)
                s.wait_ge(s_d5, d5ev[("ln", tt, 7)])
                s.dma_start(out=sp_y1[tt * 128:(tt + 1) * 128, :],
                            in_=y1t[:, (tt % 2) * D:(tt % 2 + 1) * D]).then_inc(s_y1s[tt % 2], 16)
                s.wait_ge(s_d5, d5ev[("ln", tt, 8)])
                s.dma_start(out=cc_y1in[tt * 128:(tt + 1) * 128, :],
                            in_=y1b16[:, (tt % 2) * D:(tt % 2 + 1) * D]).then_inc(s_y1c[tt % 2], 16)
            s.wait_ge(s_y1s[0], 128)
            s.wait_ge(s_y1s[1], 128)

        @blk.gpsimd
        def _(g):
            for rc in range(CC):
                if rc >= 2:
                    g.wait_ge(s_t6, 8 * (rc - 1))
                g.dma_start(out=wrow6[:, (rc % 2) * D:(rc % 2 + 1) * D],
                            in_=wout_in[rc * 128:(rc + 1) * 128, :]).then_inc(s_wl6[rc % 2], 16)
            g.dma_start(out=wgb.ap().rearrange("p (c e) -> p c e", c=8),
                        in_=bass.AP(wg_in, 0, [[8, 128], [1024, 8], [1, 8]])).then_inc(s_cg, 16)
            g.dma_start(out=boutbc[:, :],
                        in_=bout_in.ap().broadcast_to([128, D])).then_inc(s_cg, 16)
            g.dma_start(out=g1v[:, :],
                        in_=ln1g_in.ap().broadcast_to([128, D])).then_inc(s_cg, 16)
            g.dma_start(out=b1v[:, :],
                        in_=ln1b_in.ap().broadcast_to([128, D])).then_inc(s_cg, 16)
            g.wait_ge(s_ag4, 2)
            for q in range(128):
                tt, m = q // 8, q % 8
                if q >= 3:
                    qp = q - 3
                    g.wait_ge(s_d5, d5ev[("cmb", qp // 8, qp % 8, 3)])
                g.dma_start(out=vmb[:, (q % 3) * D:(q % 3 + 1) * D],
                            in_=cc_vout[m * T + tt * 128: m * T + (tt + 1) * 128, :]
                            ).then_inc(s_vm[q % 3], 16)

    _es_s.close()
    _es_q.close()
    _es_o.close()

    if upto == "p4":
        s_l = nc.alloc_semaphore("fx4_l")
        s_st = nc.alloc_semaphore("fx4_s")
        with (
            nc.sbuf_tensor("ot4", [128, D], F32) as ot4,
            nc.Block() as blk,
        ):
            @blk.gpsimd
            def _(g):
                for tt in range(NT):
                    g.dma_start(out=ot4[:, :], in_=sp_y1[tt * 128:(tt + 1) * 128, :]).then_inc(s_l, 16)
                    g.wait_ge(s_l, 16 * (tt + 1))
                    g.dma_start(out=out[tt * 128:(tt + 1) * 128, :], in_=ot4[:, :]).then_inc(s_st, 16)
                    g.wait_ge(s_st, 16 * (tt + 1))
        return nc

    _es_w = ExitStack()
    w1sb = _es_w.enter_context(nc.sbuf_tensor("w1sb", [128, CC * FF], BF16))
    w2sb = _es_w.enter_context(nc.sbuf_tensor("w2sb", [128, 32 * D], BF16))
    gidx = _es_w.enter_context(nc.sbuf_tensor("gidx", [128, NT], I32))
    s_wld = nc.alloc_semaphore("p7_wld")
    s_d7 = nc.alloc_semaphore("p7_d")

    with ExitStack() as _es:
        stairb = _es.enter_context(nc.sbuf_tensor("stairb", [128, 4096], BF16))
        erowb = _es.enter_context(nc.sbuf_tensor("erowb", [128, 8], F32))
        crev = _es.enter_context(nc.sbuf_tensor("crev", [128, 8], F32))
        mb = _es.enter_context(nc.sbuf_tensor("mb", [128, NT * 8], BF16))
        mfull = _es.enter_context(nc.sbuf_tensor("mfull", [128, NT * 8], F32))
        m8 = _es.enter_context(nc.sbuf_tensor("m8", [128, 8], F32))
        l8 = _es.enter_context(nc.sbuf_tensor("l8", [128, 8], F32))
        gsubb = _es.enter_context(nc.sbuf_tensor("gsubb", [128, NT * 8], F32))
        gexpb = _es.enter_context(nc.sbuf_tensor("gexpb", [128, NT * 8], F32))
        smx5 = _es.enter_context(nc.sbuf_tensor("smx5", [128, 8], F32))
        idxf = _es.enter_context(nc.sbuf_tensor("idxf", [128, NT], F32))
        posb = _es.enter_context(nc.sbuf_tensor("posb", [128, NT], F32))
        kpb = _es.enter_context(nc.sbuf_tensor("kpb", [128, NT], F32))
        dstf = _es.enter_context(nc.sbuf_tensor("dstf", [128, NT], F32))
        dstcf = _es.enter_context(nc.sbuf_tensor("dstcf", [128, NT], F32))
        cntsb = _es.enter_context(nc.sbuf_tensor("cntsb", [8, 1], F32))
        cntb = _es.enter_context(nc.sbuf_tensor("cntb", [8, 8], F32))
        pmb = _es.enter_context(nc.sbuf_tensor("pmb", [8, 1], F32))
        basesb = _es.enter_context(nc.sbuf_tensor("basesb", [8, 1], F32))
        baseb = _es.enter_context(nc.sbuf_tensor("baseb", [128, 8], F32))
        locsb = _es.enter_context(nc.sbuf_tensor("locsb", [8, 2048], F32))
        ltok = _es.enter_context(nc.sbuf_tensor("ltok", [128, NT * 8], F32))
        zrow = _es.enter_context(nc.sbuf_tensor("zrow", [128, 129], F32))
        invsb = _es.enter_context(nc.sbuf_tensor("invsb", [128, NT], F32))
        invw_sb = _es.enter_context(nc.sbuf_tensor("invw_sb", [128, 129 * NT], F32))
        invcol = _es.enter_context(nc.sbuf_tensor("invcol", [128, 129], F32))
        myinvb = _es.enter_context(nc.sbuf_tensor("myinvb", [128, 1], I32))
        c128b = _es.enter_context(nc.sbuf_tensor("c128b", [128, 1], F32))
        myinvf = _es.enter_context(nc.sbuf_tensor("myinvf", [128, 1], F32))
        off8f = _es.enter_context(nc.sbuf_tensor("off8f", [128, 8], F32))
        off8i = _es.enter_context(nc.sbuf_tensor("off8i", [128, 8], I32))
        inv8 = _es.enter_context(nc.sbuf_tensor("inv8", [128, 8 * NT], F32))
        izb = _es.enter_context(nc.sbuf_tensor("izb", [128, NT], F32))
        gidxf = _es.enter_context(nc.sbuf_tensor("gidxf", [128, NT], F32))
        sidxsb = _es.enter_context(nc.sbuf_tensor("sidxsb", [128, NT], F32))
        loc_ps = _es.enter_context(nc.psum_tensor("loc_ps", [8, 2048], F32))
        cnt_ps = _es.enter_context(nc.psum_tensor("cnt_ps", [8, 512], F32))
        base_ps = _es.enter_context(nc.psum_tensor("base_ps", [8, 512], F32))
        ltps = _es.enter_context(nc.psum_tensor("ltps", [128, 1024], F32))
        s_c7 = nc.alloc_semaphore("p7_c")
        s_t7 = nc.alloc_semaphore("p7_t")
        s_a7 = nc.alloc_semaphore("p7_a")
        s_cc0 = nc.alloc_semaphore("p7_cc0")
        s_cag = nc.alloc_semaphore("p7_cag")
        s_cnb = nc.alloc_semaphore("p7_cnb")
        s_bb = nc.alloc_semaphore("p7_bb")
        s_cb = nc.alloc_semaphore("p7_cb")
        s_zi = nc.alloc_semaphore("p7_zi")
        s_sc7 = nc.alloc_semaphore("p7_sc")
        s_iw = nc.alloc_semaphore("p7_iw")
        s_is = nc.alloc_semaphore("p7_is")
        s_ar = nc.alloc_semaphore("p7_ar")
        s_ms = nc.alloc_semaphore("p7_ms")
        blk = _es.enter_context(nc.Block())

        @blk.vector
        def _(v):
            ec = SelfChain(v, s_d7)
            v.wait_ge(s_c7, 80)
            ec.op("tensor_scalar", crev[:, :], erowb[:, :], -1.0, 8.0, OP.mult, OP.add)
            ec.op("memset", zrow[:, :], 0.0)
            ec.op("memset", c128b[:, :], 128.0)
            for tt in range(NT):
                Gt = G_sb[:, 8 * tt:8 * tt + 8]
                ec.op("tensor_reduce", smx5[:, 0:1], Gt, AXL.X, OP.max)
                ec.op("tensor_scalar", mfull[:, 8 * tt:8 * tt + 8], Gt,
                      smx5[:, 0:1], None, OP.is_equal)
                ec.op("tensor_copy", mb[:, 8 * tt:8 * tt + 8], mfull[:, 8 * tt:8 * tt + 8])
                ec.op("tensor_tensor", m8[:, :], mfull[:, 8 * tt:8 * tt + 8],
                      crev[:, :], OP.mult)
                ec.op("tensor_reduce", smx5[:, 1:2], m8[:, :], AXL.X, OP.max)
                ec.op("tensor_scalar", idxf[:, tt:tt + 1], smx5[:, 1:2], -1.0, 8.0,
                      OP.mult, OP.add)
                ec.op("tensor_scalar", gsubb[:, 8 * tt:8 * tt + 8], Gt,
                      smx5[:, 0:1], None, OP.subtract)
            assert ec.n == 115
            for tt in range(NT):
                v.wait_ge(s_a7, tt + 1)
                ec.op("tensor_reduce", smx5[:, 2:3], gexpb[:, 8 * tt:8 * tt + 8],
                      AXL.X, OP.add)
                ec.op("reciprocal", gval[:, tt:tt + 1], smx5[:, 2:3])
            assert ec.n == 147
            v.wait_ge(s_cb, 16)
            for tt in range(NT):
                v.wait_ge(s_a7, 20 + tt)
                ec.op("tensor_tensor", l8[:, :], ltok[:, 8 * tt:8 * tt + 8],
                      baseb[:, :], OP.add)
                ec.op("tensor_tensor", l8[:, :], l8[:, :],
                      mfull[:, 8 * tt:8 * tt + 8], OP.mult)
                ec.op("tensor_reduce", posb[:, tt:tt + 1], l8[:, :], AXL.X, OP.add)
                ec.op("tensor_scalar", smx5[:, 3:4], posb[:, tt:tt + 1], 2047.0,
                      None, OP.min)
                ec.op("tensor_tensor", kpb[:, tt:tt + 1], smx5[:, 3:4],
                      posb[:, tt:tt + 1], OP.is_equal)
                ec.op("tensor_scalar", dstcf[:, tt:tt + 1], idxf[:, tt:tt + 1],
                      2048.0, None, OP.mult)
                ec.op("tensor_tensor", dstcf[:, tt:tt + 1], dstcf[:, tt:tt + 1],
                      posb[:, tt:tt + 1], OP.add)
                ec.op("tensor_scalar", smx5[:, 4:5], posb[:, tt:tt + 1],
                      1.0 / 128.0, 8388608.0, OP.mult, OP.add)
                ec.op("tensor_scalar", smx5[:, 4:5], smx5[:, 4:5],
                      8388608.0, None, OP.subtract)
                ec.op("tensor_scalar", smx5[:, 5:6], smx5[:, 4:5],
                      128.0, None, OP.mult)
                ec.op("tensor_tensor", smx5[:, 5:6], posb[:, tt:tt + 1],
                      smx5[:, 5:6], OP.subtract)
                ec.op("tensor_scalar", smx5[:, 6:7], smx5[:, 5:6],
                      -1.0, None, OP.min)
                ec.op("tensor_tensor", smx5[:, 6:7], smx5[:, 6:7],
                      smx5[:, 5:6], OP.is_equal)
                ec.op("tensor_tensor", smx5[:, 4:5], smx5[:, 4:5],
                      smx5[:, 6:7], OP.subtract)
                ec.op("tensor_scalar", smx5[:, 6:7], smx5[:, 6:7],
                      128.0, None, OP.mult)
                ec.op("tensor_tensor", smx5[:, 5:6], smx5[:, 5:6],
                      smx5[:, 6:7], OP.add)
                ec.op("tensor_scalar", smx5[:, 5:6], smx5[:, 5:6],
                      16.0, None, OP.mult)
                ec.op("tensor_tensor", smx5[:, 5:6], smx5[:, 5:6],
                      smx5[:, 4:5], OP.add)
                ec.op("tensor_scalar", dstf[:, tt:tt + 1], idxf[:, tt:tt + 1],
                      2064.0, None, OP.mult)
                ec.op("tensor_tensor", dstf[:, tt:tt + 1], dstf[:, tt:tt + 1],
                      smx5[:, 5:6], OP.add)
                ec.op("tensor_scalar", dstf[:, tt:tt + 1], dstf[:, tt:tt + 1],
                      2048.0, None, OP.subtract)
                ec.op("tensor_tensor", dstf[:, tt:tt + 1], dstf[:, tt:tt + 1],
                      kpb[:, tt:tt + 1], OP.mult)
                ec.op("tensor_scalar", dstf[:, tt:tt + 1], dstf[:, tt:tt + 1],
                      2048.0, None, OP.add)
            assert ec.n == 147 + 23 * NT
            ec.op("tensor_tensor", gval[:, :], gval[:, :], kpb[:, :], OP.mult)
            ec.op("tensor_copy", dst_i32[:, :], dstf[:, :])
            ec.op("tensor_scalar", dstcf[:, :], dstcf[:, :], 16383.0, None, OP.min)
            ec.op("tensor_copy", dstc_i32[:, :], dstcf[:, :])
            ec.op("tensor_copy", myinvf[:, :], myinvb[:, :])
            for r in range(8):
                ec.op("tensor_scalar", off8f[:, r:r + 1], myinvf[:, :],
                      1032.0 * r, None, OP.add)
            ec.op("tensor_copy", off8i[:, :], off8f[:, :])
            v.wait_ge(s_iw, 256)
            ec.op("tensor_copy", invcol[:, :], invw_sb[:, 0:129])
            for cidx in range(1, NT):
                ec.op("tensor_tensor", invcol[:, :], invcol[:, :],
                      invw_sb[:, cidx * 129:(cidx + 1) * 129],
                      OP.add)
            v.wait_ge(s_ms, 128)
            ec.op("tensor_tensor", invsb[:, :], inv8[:, 0:NT],
                  inv8[:, NT:2 * NT], OP.add)
            for r in range(2, 8):
                ec.op("tensor_tensor", invsb[:, :], invsb[:, :],
                      inv8[:, r * NT:(r + 1) * NT], OP.add)
            ec.op("tensor_scalar", izb[:, :], invsb[:, :], 0.0, None, OP.is_equal)
            ec.op("tensor_scalar", gidxf[:, :], izb[:, :], 65536.0, None, OP.mult)
            ec.op("tensor_tensor", gidxf[:, :], gidxf[:, :], invsb[:, :], OP.add)
            ec.op("tensor_scalar", gidxf[:, :], gidxf[:, :], 1.0, None, OP.subtract)
            ec.op("tensor_scalar", gidxf[:, :], gidxf[:, :], 16383.0, None, OP.min)
            ec.op("tensor_copy", gidx[:, :], gidxf[:, :])
            assert ec.n == 558

        @blk.scalar
        def _(sc):
            for tt in range(NT):
                sc.wait_ge(s_d7, 7 * tt + 10)
                sc.activation(gexpb[:, 8 * tt:8 * tt + 8],
                              gsubb[:, 8 * tt:8 * tt + 8], AF.Exp).then_inc(s_a7, 1)
            sc.wait_ge(s_t7, 16)
            sc.copy(cntsb[0:8, 0:1], cnt_ps[0:8, 0:1]).then_inc(s_a7, 1)
            sc.wait_ge(s_t7, 80)
            sc.copy(locsb[0:8, :], loc_ps[0:8, :]).then_inc(s_a7, 1)
            sc.wait_ge(s_t7, 81)
            sc.copy(basesb[0:8, 0:1], base_ps[0:8, 0:1]).then_inc(s_a7, 1)
            for tt in range(NT):
                sc.wait_ge(s_t7, 82 + tt)
                sc.copy(ltok[:, 8 * tt:8 * tt + 8],
                        ltps[:, (tt % 2) * 512:(tt % 2) * 512 + 8]).then_inc(s_a7, 1)

        @blk.tensor
        def _(t):
            for tt in range(NT):
                t.wait_ge(s_d7, 7 * tt + 6)
                t.matmul(cnt_ps[0:8, 0:1], mb[:, 8 * tt:8 * tt + 8], onescol[:, 0:1],
                         start=(tt == 0), stop=(tt == NT - 1)).then_inc(s_t7, 1)
            t.wait_ge(s_c7, 80)
            for win in range(4):
                for tt in range(NT):
                    off = 2048 - 128 * tt - 1 + win * 512
                    t.matmul(loc_ps[0:8, win * 512:(win + 1) * 512],
                             mb[:, 8 * tt:8 * tt + 8], stairb[:, off:off + 512],
                             start=(tt == 0), stop=(tt == NT - 1)).then_inc(s_t7, 1)
            t.wait_ge(s_cnb, 16)
            t.matmul(base_ps[0:8, 0:1], cntb[0:8, :], pmb[0:8, 0:1],
                     start=True, stop=True).then_inc(s_t7, 1)
            t.wait_ge(s_a7, 18)
            for tt in range(NT):
                if tt >= 2:
                    t.wait_ge(s_a7, 20 + tt - 2)
                t.transpose(ltps[:, (tt % 2) * 512:(tt % 2) * 512 + 8],
                            locsb[0:8, tt * 128:(tt + 1) * 128],
                            identf[0:8, 0:8]).then_inc(s_t7, 1)

        @blk.sync
        def _(s):
            s.dma_start(out=erowb[:, :],
                        in_=erow.ap().broadcast_to([128, 8])).then_inc(s_c7, 16)
            s.dma_start(out=stairb[:, :], in_=stair[:, :]).then_inc(s_c7, 16)
            s.dma_start(out=sidxsb[:, :], in_=sidxf_in[:, :]).then_inc(s_c7, 16)
            s.dma_start(out=pmb[0:8, :], in_=prevmask_in[:, :]).then_inc(s_c7, 16)
            s.dma_start(out=myinvb[:, :], in_=myslot_in[:, :]).then_inc(s_c7, 16)
            s.wait_ge(s_wcast, 32)
            for cc in range(CC):
                s.dma_start(out=w1sb[:, cc * FF:(cc + 1) * FF],
                            in_=w1bf[cc * 128:(cc + 1) * 128, :]).then_inc(s_wld, 16)
            for j in range(8):
                s.dma_start(out=w2sb.ap().rearrange("p (f q o) -> p f q o", f=8, q=4)[:, j],
                            in_=bass.AP(w2bf, j * 512 * D,
                                        [[D, 128], [128 * D, 4], [1, D]])).then_inc(s_wld, 16)

        @blk.gpsimd
        def _(g):
            g.wait_ge(s_a7, 17)
            g.dma_start(out=cc_cntin.ap().rearrange("a b -> b a"),
                        in_=cntsb[0:8, 0:1]).then_inc(s_cc0, 16)
            g.wait_ge(s_cc0, 16)
            g.collective_compute("AllGather", OP.bypass, replica_groups=RG,
                                 ins=[cc_cntin.ap().opt()],
                                 outs=[cc_cntout.ap().opt()]).then_inc(s_cag, 1)
            g.wait_ge(s_y1c[0], 128)
            g.wait_ge(s_y1c[1], 128)
            g.collective_compute("AllGather", OP.bypass, replica_groups=RG,
                                 ins=[cc_y1in.ap().opt()],
                                 outs=[cc_y1out.ap().opt()]).then_inc(s_ag6, 1)
            g.wait_ge(s_cag, 1)
            g.dma_start(out=cntb[0:8, :], in_=cc_cntout[:, :]).then_inc(s_cnb, 16)
            g.wait_ge(s_a7, 19)
            g.dma_start(out=basesc.ap().rearrange("a b -> b a"),
                        in_=basesb[0:8, 0:1]).then_inc(s_bb, 16)
            g.wait_ge(s_bb, 16)
            g.dma_start(out=baseb[:, :],
                        in_=basesc.ap().broadcast_to([128, 8])).then_inc(s_cb, 16)
            g.wait_ge(s_d7, 2)
            for tt in range(NT):
                g.dma_start(out=cc_invw_t[tt].ap().rearrange("(p a) b -> p (a b)", p=128),
                            in_=zrow[:, :]).then_inc(s_zi, 16)
            g.wait_ge(s_zi, 256)
            g.wait_ge(s_d7, 517)
            for tt in range(NT):
                g.indirect_dma_start(
                    out=cc_invw_t[tt][:, :],
                    out_offset=IndirectOffsetOnAxis(ap=dst_i32[:, tt:tt + 1], axis=0),
                    in_=sidxsb[:, tt:tt + 1], in_offset=None).then_inc(s_sc7, 16)
            g.wait_ge(s_sc7, 256)
            for tt in range(NT):
                g.dma_start(out=invw_sb[:, tt * 129:(tt + 1) * 129],
                            in_=cc_invw_t[tt].ap().rearrange("(p a) b -> p (a b)", p=128)
                            ).then_inc(s_iw, 16)
            g.wait_ge(s_d7, 545)
            g.dma_start(out=cc_invin.ap().rearrange("(p a) b -> p (a b)", p=128),
                        in_=invcol[:, :]).then_inc(s_is, 16)
            g.wait_ge(s_is, 16)
            g.collective_compute("AllGather", OP.bypass, replica_groups=RG,
                                 ins=[cc_invin.ap().rearrange(
                                     "(a b) c -> a (b c)", b=128).opt()],
                                 outs=[cc_iall.ap().rearrange(
                                     "(a b) c -> a (b c)", b=128).opt()]).then_inc(s_ar, 1)
            g.wait_ge(s_ar, 1)
            g.wait_ge(s_d7, 529)
            iview = cc_iall.ap().rearrange("(g w) b -> g (w b)", w=16)
            for r in range(8):
                g.indirect_dma_start(
                    out=inv8[:, r * NT:(r + 1) * NT], out_offset=None,
                    in_=iview,
                    in_offset=IndirectOffsetOnAxis(ap=off8i[:, r:r + 1], axis=0)
                    ).then_inc(s_ms, 16)
            if upto == "p5":
                s_dg = nc.alloc_semaphore("p5_dump")
                g.wait_ge(s_d7, 558)
                dumps = [dstf, idxf, gval, kpb, posb, invsb, gidxf]
                for i, tsr in enumerate(dumps):
                    g.dma_start(out=out[i * 128:(i + 1) * 128, 0:NT],
                                in_=tsr[:, :]).then_inc(s_dg, 16)
                g.dma_start(out=out[7 * 128:8 * 128, 0:128],
                            in_=G_sb[:, :]).then_inc(s_dg, 16)
                g.dma_start(out=out[8 * 128:9 * 128, 0:128],
                            in_=ltok[:, :]).then_inc(s_dg, 16)
                g.dma_start(out=out[9 * 128:10 * 128, 0:8],
                            in_=baseb[:, :]).then_inc(s_dg, 16)
                g.wait_ge(s_dg, 160)


    if upto == "p5":
        return nc

    with ExitStack() as _es:
        dsb = _es.enter_context(nc.sbuf_tensor("dsb", [128, 2 * D], BF16))
        dispf = _es.enter_context(nc.sbuf_tensor("dispf", [128, 2 * CC * 512], BF16))
        h_sb = _es.enter_context(nc.sbuf_tensor("h_sb", [128, 32 * 512], BF16))
        yesb = _es.enter_context(nc.sbuf_tensor("yesb", [128, 2 * D], BF16))
        b1sb = _es.enter_context(nc.sbuf_tensor("b1sb", [32, 128], F32))
        b1T = _es.enter_context(nc.sbuf_tensor("b1T", [128, 32], F32))
        b2bc = _es.enter_context(nc.sbuf_tensor("b2bc", [128, D], F32))
        h_ps = _es.enter_context(nc.psum_tensor("h_ps", [128, 1024], F32))
        ye_ps = _es.enter_context(nc.psum_tensor("ye_ps", [128, 1024], F32))
        trdp = _es.enter_context(nc.psum_tensor("trdp", [128, 2048], BF16))
        s_c8 = nc.alloc_semaphore("p8_c")
        s_ds = [nc.alloc_semaphore("p8_ds0"), nc.alloc_semaphore("p8_ds1")]
        s_ye = [nc.alloc_semaphore("p8_ye0"), nc.alloc_semaphore("p8_ye1")]
        s_t8 = nc.alloc_semaphore("p8_t")
        s_a8 = nc.alloc_semaphore("p8_a")
        s_d8 = nc.alloc_semaphore("p8_d")
        s_ag8 = nc.alloc_semaphore("p8_ag")
        blk = _es.enter_context(nc.Block())

        pe_order = [("b1tr",)]
        for jg in range(4):
            for cc in range(CC):
                pe_order.append(("tr", jg, cc))
        for jw in range(4):
            for fs in range(32):
                pe_order.append(("w1", jw, fs))
            if jw + 1 < 4:
                for jg in range(4 * (jw + 1), 4 * (jw + 2)):
                    for cc in range(CC):
                        pe_order.append(("tr", jg, cc))
            for g8 in range(8):
                pe_order.append(("w2", jw, g8))
        t8ev = {k: i + 1 for i, k in enumerate(pe_order)}

        act_order = [("b1ev",)]
        for jw in range(4):
            if jw == 0:
                for jg in range(4):
                    for cc in range(CC):
                        act_order.append(("dp", jg, cc))
            for fs in range(32):
                act_order.append(("h", jw, fs))
            if jw + 1 < 4:
                for jg in range(4 * (jw + 1), 4 * (jw + 2)):
                    for cc in range(CC):
                        act_order.append(("dp", jg, cc))
        a8ev = {k: i + 1 for i, k in enumerate(act_order)}

        dve_order = []
        for jg in range(4):
            dve_order.append(("ms", jg))
        for jw in range(4):
            for jg in range(4 * (jw + 1), min(4 * (jw + 2), 16)):
                dve_order.append(("ms", jg))
            for g8 in range(8):
                dve_order.append(("ye", jw, g8))
        d8ev = {k: i + 1 for i, k in enumerate(dve_order)}

        @blk.tensor
        def _(t):
            for key in pe_order:
                if key[0] == "b1tr":
                    t.wait_ge(s_c8, 32)
                    t.transpose(trdp.bitcast(F32)[:, 0:32], b1sb[0:32, :],
                                identf[0:32, 0:32]).then_inc(s_t8, 1)
                elif key[0] == "tr":
                    _, jg, cc = key
                    k = jg * 8 + cc
                    if cc == 0:
                        t.wait_ge(s_ds[jg % 2], 16 * (jg // 2 + 1))
                    if k == 0:
                        t.wait_ge(s_a8, 1)
                    if k >= 2:
                        t.wait_ge(s_a8, a8ev[("dp", (k - 2) // 8, (k - 2) % 8)])
                    t.transpose(trdp[:, (k % 2) * 1024:(k % 2) * 1024 + 128],
                                dsb[:, (jg % 2) * D + cc * 128:(jg % 2) * D + (cc + 1) * 128],
                                identb[:, :]).then_inc(s_t8, 1)
                elif key[0] == "w1":
                    _, jw, fs = key
                    if fs == 0:
                        t.wait_ge(s_a8, a8ev[("dp", 4 * jw + 3, 7)])
                        if jw == 0:
                            t.wait_ge(s_wld, 256)
                    if fs >= 2:
                        t.wait_ge(s_a8, a8ev[("h", jw, fs - 2)])
                    elif jw >= 1:
                        t.wait_ge(s_a8, a8ev[("h", jw - 1, 30 + fs)])
                    for cc in range(CC):
                        mm = t.matmul(h_ps[:, (fs % 2) * 512:(fs % 2 + 1) * 512],
                                      w1sb[:, cc * FF + fs * 128: cc * FF + (fs + 1) * 128],
                                      dispf[:, (jw % 2) * 4096 + cc * 512:
                                            (jw % 2) * 4096 + (cc + 1) * 512],
                                      start=(cc == 0), stop=(cc == CC - 1))
                    mm.then_inc(s_t8, 1)
                else:
                    _, jw, g8 = key
                    if g8 == 0:
                        t.wait_ge(s_a8, a8ev[("h", jw, 31)])
                    if g8 >= 2:
                        t.wait_ge(s_d8, d8ev[("ye", jw, g8 - 2)])
                    elif jw >= 1:
                        t.wait_ge(s_d8, d8ev[("ye", jw - 1, 6 + g8)])
                    js, ow = g8 // 2, g8 % 2
                    for fs in range(32):
                        mm = t.matmul(ye_ps[:, (g8 % 2) * 512:(g8 % 2 + 1) * 512],
                                      h_sb[:, fs * 512 + js * 128: fs * 512 + (js + 1) * 128],
                                      w2sb[:, fs * D + ow * 512: fs * D + (ow + 1) * 512],
                                      start=(fs == 0), stop=(fs == 31))
                    mm.then_inc(s_t8, 1)

        @blk.scalar
        def _(sc):
            for key in act_order:
                if key[0] == "b1ev":
                    sc.wait_ge(s_t8, 1)
                    sc.copy(b1T[:, :], trdp.bitcast(F32)[:, 0:32]).then_inc(s_a8, 1)
                elif key[0] == "dp":
                    _, jg, cc = key
                    jw = jg // 4
                    g4 = jg % 4
                    k = jg * 8 + cc
                    sc.wait_ge(s_t8, t8ev[("tr", jg, cc)])
                    sc.copy(dispf[:, (jw % 2) * 4096 + cc * 512 + g4 * 128:
                                  (jw % 2) * 4096 + cc * 512 + (g4 + 1) * 128],
                            trdp[:, (k % 2) * 1024:(k % 2) * 1024 + 128]).then_inc(s_a8, 1)
                else:
                    _, jw, fs = key
                    sc.wait_ge(s_t8, t8ev[("w1", jw, fs)])
                    if fs == 0 and jw >= 1:
                        sc.wait_ge(s_t8, t8ev[("w2", jw - 1, 7)])
                    sc.activation(h_sb[:, fs * 512:(fs + 1) * 512],
                                  h_ps[:, (fs % 2) * 512:(fs % 2 + 1) * 512],
                                  AF.Identity, bias=b1T[:, fs:fs + 1]).then_inc(s_a8, 1)

        @blk.vector
        def _(v):
            ec = SelfChain(v, s_d8)
            v.wait_ge(s_c8, 32)
            for key in dve_order:
                if key[0] == "ms":
                    jg = key[1]
                    if jg >= 2:
                        v.wait_ge(s_t8, t8ev[("tr", jg - 2, 7)])
                    ec.op("memset", dsb[:, (jg % 2) * D:(jg % 2 + 1) * D], 0.0)
                else:
                    _, jw, g8 = key
                    js, ow = g8 // 2, g8 % 2
                    js16 = jw * 4 + js
                    v.wait_ge(s_t8, t8ev[("w2", jw, g8)])
                    if js16 >= 2 and ow == 0:
                        v.wait_ge(s_ye[js16 % 2], 16 * (js16 // 2))
                    ec.op("tensor_tensor",
                          yesb[:, (js % 2) * D + ow * 512:(js % 2) * D + (ow + 1) * 512],
                          ye_ps[:, (g8 % 2) * 512:(g8 % 2 + 1) * 512],
                          b2bc[:, ow * 512:(ow + 1) * 512], OP.add)

        @blk.sync
        def _(s):
            for jw in range(4):
                for js in range(4):
                    s.wait_ge(s_d8, d8ev[("ye", jw, 2 * js + 1)])
                    js16 = jw * 4 + js
                    s.dma_start(out=cc_yein[js16 * 128:(js16 + 1) * 128, :],
                                in_=yesb[:, (js % 2) * D:(js % 2 + 1) * D]
                                ).then_inc(s_ye[js16 % 2], 16)

        @blk.gpsimd
        def _(g):
            g.dma_start(out=b1sb[0:32, :], in_=b1_in[:, :]).then_inc(s_c8, 16)
            g.dma_start(out=b2bc[:, :],
                        in_=b2_in.ap().broadcast_to([128, D])).then_inc(s_c8, 16)
            g.wait_ge(s_d7, 558)
            g.wait_ge(s_ag6, 1)
            for jg in range(16):
                g.wait_ge(s_d8, d8ev[("ms", jg)])
                g.indirect_dma_start(
                    out=dsb[:, (jg % 2) * D:(jg % 2 + 1) * D], out_offset=None,
                    in_=cc_y1out[:, :],
                    in_offset=IndirectOffsetOnAxis(ap=gidx[:, jg:jg + 1], axis=0)
                    ).then_inc(s_ds[jg % 2], 16)
            g.wait_ge(s_ye[0], 128)
            g.wait_ge(s_ye[1], 128)
            g.collective_compute("AllGather", OP.bypass, replica_groups=RG,
                                 ins=[cc_yein.ap().opt()],
                                 outs=[cc_yeout.ap().opt()]).then_inc(s_ag8, 1)

    _es_w.close()
    _es_g.close()

    with ExitStack() as _es:
        ymg = _es.enter_context(nc.sbuf_tensor("ymg", [128, NT * D], BF16))
        ymf = _es.enter_context(nc.sbuf_tensor("ymf", [128, 2 * D], F32))
        scr9 = _es.enter_context(nc.sbuf_tensor("scr9", [128, D], F32))
        y1ld = _es.enter_context(nc.sbuf_tensor("y1ld", [128, 2 * D], F32))
        x2ld = _es.enter_context(nc.sbuf_tensor("x2ld", [128, 2 * D], F32))
        t2b = _es.enter_context(nc.sbuf_tensor("t2b", [128, 2 * D], F32))
        y2b = _es.enter_context(nc.sbuf_tensor("y2b", [128, 2 * D], F32))
        outb = _es.enter_context(nc.sbuf_tensor("outb", [128, 2 * D], F32))
        st9 = _es.enter_context(nc.sbuf_tensor("st9", [128, 12 * NT], F32))
        g2v = _es.enter_context(nc.sbuf_tensor("g2v", [128, D], F32))
        l2bv = _es.enter_context(nc.sbuf_tensor("l2bv", [128, D], F32))
        s_c9 = nc.alloc_semaphore("p9_c")
        s_ymg = nc.alloc_semaphore("p9_ym")
        s_yl = [nc.alloc_semaphore("p9_yl0"), nc.alloc_semaphore("p9_yl1")]
        s_xl9 = [nc.alloc_semaphore("p9_xl0"), nc.alloc_semaphore("p9_xl1")]
        s_os = [nc.alloc_semaphore("p9_os0"), nc.alloc_semaphore("p9_os1")]
        s_a9 = nc.alloc_semaphore("p9_a")
        s_d9 = nc.alloc_semaphore("p9_d")
        blk = _es.enter_context(nc.Block())
        RSD = 1.0 / 32.0

        def d9(tt, j):
            return 1 + 8 * tt + j

        act9_order = [("cast", 0), ("cast", 1)]
        for _tt in range(NT):
            for _nm in ("acc1", "sqacc1", "sqrt1", "acc2", "sqacc2", "sqrt2"):
                act9_order.append((_nm, _tt))
            if _tt + 2 < NT:
                act9_order.append(("cast", _tt + 2))
        a9ev = {k: i + 1 for i, k in enumerate(act9_order)}

        @blk.vector
        def _(v):
            ec = SelfChain(v, s_d9)
            ec.op("memset", ymg[:, :], 0.0)
            v.wait_ge(s_c9, 32)
            for tt in range(NT):
                sb = st9[:, 12 * tt:12 * tt + 12]
                sl = (tt % 2) * D
                t2 = t2b[:, sl:sl + D]
                y2 = y2b[:, sl:sl + D]
                ot = outb[:, sl:sl + D]
                v.wait_ge(s_a9, a9ev[("cast", tt)])
                v.wait_ge(s_yl[tt % 2], 16 * (tt // 2 + 1))
                ec.op("scalar_tensor_tensor", t2, ymf[:, sl:sl + D],
                      gval[:, tt:tt + 1], y1ld[:, sl:sl + D], OP.mult, OP.add)
                v.wait_ge(s_a9, a9ev[("acc1", tt)])
                ec.op("tensor_scalar", t2, t2, sb[:, 0:1], None, OP.subtract)
                v.wait_ge(s_a9, a9ev[("sqrt1", tt)])
                ec.op("reciprocal", sb[:, 4:5], sb[:, 3:4])
                ec.op("scalar_tensor_tensor", y2, t2, sb[:, 4:5], g2v[:, :],
                      OP.mult, OP.mult)
                ec.op("tensor_tensor", y2, y2, l2bv[:, :], OP.add)
                v.wait_ge(s_a9, a9ev[("acc2", tt)])
                ec.op("tensor_scalar", y2, y2, sb[:, 5:6], None, OP.subtract)
                v.wait_ge(s_a9, a9ev[("sqrt2", tt)])
                ec.op("reciprocal", sb[:, 9:10], sb[:, 8:9])
                v.wait_ge(s_xl9[tt % 2], 16 * (tt // 2 + 1))
                if tt >= 2:
                    v.wait_ge(s_os[tt % 2], 16 * (tt // 2))
                ec.op("scalar_tensor_tensor", ot, y2, sb[:, 9:10],
                      x2ld[:, sl:sl + D], OP.mult, OP.add)
            assert ec.n == 1 + 8 * NT

        @blk.scalar
        def _(sc):
            sc.wait_ge(s_ymg, 256)
            sc.wait_ge(s_c9, 32)
            na = 0

            def aop(wd, *a, **k):
                nonlocal na
                if wd is not None:
                    sc.wait_ge(s_d9, wd)
                if na:
                    sc.wait_ge(s_a9, na)
                sc.activation(*a, **k).then_inc(s_a9, 1)
                na += 1
            for key in act9_order:
                nm, tt = key
                sb = st9[:, 12 * tt:12 * tt + 12]
                sl = (tt % 2) * D
                if nm == "cast":
                    aop(d9(tt - 2, 1) if tt >= 2 else None,
                        ymf[:, sl:sl + D], ymg[:, tt * D:(tt + 1) * D], AF.Identity,
                        bias=0.0)
                elif nm == "acc1":
                    aop(d9(tt, 1), scr9[:, :], t2b[:, sl:sl + D], AF.Identity,
                        bias=0.0, scale=1.0 / D, accum_out=sb[:, 0:1])
                elif nm == "sqacc1":
                    aop(d9(tt, 2), scr9[:, :], t2b[:, sl:sl + D], AF.Square,
                        scale=RSD, accum_out=sb[:, 1:2])
                elif nm == "sqrt1":
                    aop(None, sb[:, 3:4], sb[:, 1:2], AF.Sqrt, bias=epsc[:, :])
                elif nm == "acc2":
                    aop(d9(tt, 5), scr9[:, :], y2b[:, sl:sl + D], AF.Identity,
                        bias=0.0, scale=1.0 / D, accum_out=sb[:, 5:6])
                elif nm == "sqacc2":
                    aop(d9(tt, 6), scr9[:, :], y2b[:, sl:sl + D], AF.Square,
                        scale=RSD, accum_out=sb[:, 6:7])
                else:
                    aop(None, sb[:, 8:9], sb[:, 6:7], AF.Sqrt, bias=epsc[:, :])
            assert na == len(act9_order)

        @blk.gpsimd
        def _(g):
            g.dma_start(out=g2v[:, :],
                        in_=ln2g_in.ap().broadcast_to([128, D])).then_inc(s_c9, 16)
            g.dma_start(out=l2bv[:, :],
                        in_=ln2b_in.ap().broadcast_to([128, D])).then_inc(s_c9, 16)
            g.wait_ge(s_ag8, 1)
            g.wait_ge(s_d9, 1)
            for tt in range(NT):
                g.indirect_dma_start(
                    out=ymg[:, tt * D:(tt + 1) * D], out_offset=None,
                    in_=cc_yeout[:, :],
                    in_offset=IndirectOffsetOnAxis(ap=dstc_i32[:, tt:tt + 1], axis=0)
                    ).then_inc(s_ymg, 16)

        @blk.sync
        def _(s):
            for tt in range(NT):
                if tt >= 2:
                    s.wait_ge(s_d9, d9(tt - 2, 1))
                s.dma_start(out=y1ld[:, (tt % 2) * D:(tt % 2 + 1) * D],
                            in_=sp_y1[tt * 128:(tt + 1) * 128, :]).then_inc(s_yl[tt % 2], 16)
                if tt >= 2:
                    s.wait_ge(s_d9, d9(tt - 2, 8))
                s.dma_start(out=x2ld[:, (tt % 2) * D:(tt % 2 + 1) * D],
                            in_=sp_x2[tt * 128:(tt + 1) * 128, :]).then_inc(s_xl9[tt % 2], 16)
                s.wait_ge(s_d9, d9(tt, 8))
                s.dma_start(out=out[tt * 128:(tt + 1) * 128, :],
                            in_=outb[:, (tt % 2) * D:(tt % 2 + 1) * D]).then_inc(s_os[tt % 2], 16)
            s.wait_ge(s_os[0], 128)
            s.wait_ge(s_os[1], 128)

    _es_r.close()
    return nc


_NC_CACHE = {}
UPTO = "full"


def kernel(**inputs):
    if UPTO not in _NC_CACHE:
        _NC_CACHE[UPTO] = build_nc(UPTO)
    nc = _NC_CACHE[UPTO]

    f32 = np.float32
    x = np.asarray(inputs["L0_attn_x"], f32)
    in_maps = []
    for c in range(NCORES):
        m = {
            "x_in": np.ascontiguousarray(x[c]),
            "wq": np.asarray(inputs["L0_attn_W_Q"], f32),
            "wk": np.asarray(inputs["L0_attn_W_K"], f32),
            "wv": np.asarray(inputs["L0_attn_W_V"], f32),
            "wkp": np.asarray(inputs["L0_attn_W_K_prime"], f32),
            "wo": np.asarray(inputs["L0_attn_W_O"], f32),
            "alpha": np.asarray(inputs["L0_attn_alpha"], f32).reshape(1, 1),
            "win": np.asarray(inputs["L0_ffn_in_proj_w"], f32),
            "binq": np.asarray(inputs["L0_ffn_in_proj_b"], f32).reshape(24, 128),
            "wout": np.asarray(inputs["L0_ffn_out_proj_w"], f32),
            "bout": np.asarray(inputs["L0_ffn_out_proj_b"], f32).reshape(1, D),
            "ln1g": np.asarray(inputs["L0_ffn_ln1_g"], f32).reshape(1, D),
            "ln1b": np.asarray(inputs["L0_ffn_ln1_b"], f32).reshape(1, D),
            "ln2g": np.asarray(inputs["L0_ffn_ln2_g"], f32).reshape(1, D),
            "ln2b": np.asarray(inputs["L0_ffn_ln2_b"], f32).reshape(1, D),
            "wg": np.asarray(inputs["L0_ffn_wg"], f32),
            "w1": np.ascontiguousarray(np.asarray(inputs["L0_ffn_w1"], f32)[c]),
            "b1": np.ascontiguousarray(np.asarray(inputs["L0_ffn_b1"], f32)[c]).reshape(32, 128),
            "w2": np.ascontiguousarray(np.asarray(inputs["L0_ffn_w2"], f32)[c]),
            "b2": np.ascontiguousarray(np.asarray(inputs["L0_ffn_b2"], f32)[c]).reshape(1, D),
            "prevmask": (np.arange(8) < c).astype(f32).reshape(8, 1),
            "sidxf": np.ascontiguousarray(
                np.arange(T, dtype=np.float32).reshape(NT, 128).T + (c * T + 1)),
            "myslot": np.ascontiguousarray(
                (np.arange(128, dtype=np.int32) + 129 * c).reshape(128, 1)),
        }
        in_maps.append(m)

    res = run_bass_kernel_spmd(nc, in_maps, core_ids=list(range(NCORES)))
    return np.stack([r["out"] for r in res.results], axis=0)
